# version 11
# baseline (speedup 1.0000x reference)
import math
import numpy as np
from contextlib import ExitStack

import concourse.bass as bass
import concourse.mybir as mybir
from concourse.bass_utils import run_bass_kernel_spmd

F32 = mybir.dt.float32
BF16 = mybir.dt.bfloat16
AF = mybir.ActivationFunctionType
ALU = mybir.AluOpType

NCORES = 8
WARM = True
H = 2048
HC = 16
INTER = 8192
IC = 64
NT = 2048
NB = 512
NBLK = NT // NB
CH = 128
NCH = NT // CH
CPB = NB // CH
ISL = INTER // NCORES
HSL = H // NCORES
RS2 = 1.0 / math.sqrt(2048.0)
RS8 = 1.0 / math.sqrt(8192.0)
EPS_LN = 1e-5
EPS_AT = 1e-6
ST = 132


def _hadamard(n):
    Hm = np.array([[1.0]], dtype=np.float32)
    while Hm.shape[0] < n:
        Hm = np.block([[Hm, Hm], [Hm, -Hm]])
    return np.ascontiguousarray(Hm.astype(np.float32))


_H16 = _hadamard(16)


class Ctr:

    def __init__(self, stack, nc, name):
        self.sem = stack.enter_context(nc.semaphore(name))
        self.val = 0

    def reserve(self, n=1):
        self.val += n
        return self.val

    def at(self):
        return self.val


class DQ:

    def __init__(self, stack, nc, name, lanes=8):
        self.sems = [stack.enter_context(nc.semaphore("%s%d" % (name, i)))
                     for i in range(lanes)]
        self.vals = [0] * lanes
        self.i = 0

    def slot(self):
        L = self.i % len(self.sems)
        self.i += 1
        pre = self.vals[L]
        self.vals[L] += 16
        return (self.sems[L], pre, self.vals[L])

    def barrier(self):
        return [(s, v) for s, v in zip(self.sems, self.vals) if v > 0]


def _dma(e, q3, out, in_):
    sem, pre, _ = q3
    if pre:
        e.wait_ge(sem, pre)
    e.dma_start(out=out, in_=in_).then_inc(sem, 16)


def _waitall(e, pairs):
    for s, v in pairs:
        e.wait_ge(s, v)


_STAGES = {"ld": -3, "vec": -2, "g": -1, "init": 0, "foldu": 1, "foldg": 2, "foldd": 3, "fold": 4, "ffn1b0": 5, "phase1": 6, "phase2": 7, "full": 8}


def build_program(taps=(), stage="full"):
    nc = bass.Bass("TRN2", target_bir_lowering=False, debug=False)
    taps = set(taps)
    lvl = _STAGES[stage]

    din = {}
    for name, shape in [
        ("x", [H, NT]),
        ("wqT", [H, 128]), ("wkT", [H, 128]), ("wvT", [H, 128]),
        ("woT", [128, H]),
        ("mz", [128, 129]),
        ("gate1", [1, 1]),
        ("su_ug", [128, HC]),
        ("wsu", [128, IC]), ("wsg", [128, IC]),
        ("wsdp", [128, HC]),
        ("su_dp", [128, IC]),
        ("wu_slice", [H, ISL]), ("wg_slice", [H, ISL]),
        ("wd_slice", [INTER, HSL]),
        ("h128", [128, 128]),
        ("ident", [128, 128]),
        ("trimask", [128, 128]),
        ("pmask", [128, NCORES]),
    ]:
        din[name] = nc.dram_tensor(name, shape, F32, kind="ExternalInput").ap()

    out = nc.dram_tensor("out", [H, NT], F32, kind="ExternalOutput").ap()
    tap_aps = {}
    if "x1" in taps:
        tap_aps["x1"] = nc.dram_tensor("tap_x1", [H, NT], F32, kind="ExternalOutput").ap()
    if "ds" in taps:
        tap_aps["ds"] = nc.dram_tensor("tap_ds", [128, NCH * ST], F32, kind="ExternalOutput").ap()
    if "pq" in taps:
        tap_aps["pq"] = nc.dram_tensor("tap_pq", [128, NT], BF16, kind="ExternalOutput").ap()
    if "sfull" in taps:
        tap_aps["sfull"] = nc.dram_tensor("tap_sfull", [128, NCH * ST], BF16, kind="ExternalOutput").ap()
    if "agu" in taps:
        tap_aps["agu"] = nc.dram_tensor("tap_agu", [128, HC * 128], BF16, kind="ExternalOutput").ap()
        tap_aps["agd"] = nc.dram_tensor("tap_agd", [128, IC * 128], BF16, kind="ExternalOutput").ap()

    x1s = nc.dram_tensor("x1s", [H, NT], F32, kind="Internal").ap()
    agi_u = nc.dram_tensor("agi_u", [ISL // 128, 128, HC, 128], BF16, kind="Internal")
    ago_u = nc.dram_tensor("ago_u", [IC, 128, HC, 128], BF16, kind="Internal", addr_space="Shared")
    agi_g = nc.dram_tensor("agi_g", [ISL // 128, 128, HC, 128], BF16, kind="Internal")
    ago_g = nc.dram_tensor("ago_g", [IC, 128, HC, 128], BF16, kind="Internal", addr_space="Shared")
    agi_d = nc.dram_tensor("agi_d", [HSL // 128, 128, IC, 128], BF16, kind="Internal")
    ago_d = nc.dram_tensor("ago_d", [HC, 128, IC, 128], BF16, kind="Internal", addr_space="Shared")
    wrm_i = nc.dram_tensor("wrm_i", [1, 128], F32, kind="Internal")
    wrm_o = nc.dram_tensor("wrm_o", [NCORES, 128], F32, kind="Internal", addr_space="Shared")
    cc_si = nc.dram_tensor("cc_si", [128, 129], F32, kind="Internal")
    cc_so = nc.dram_tensor("cc_so", [NCORES * 128, 129], F32, kind="Internal", addr_space="Shared")

    st = ExitStack()
    sb = lambda n, s, d: st.enter_context(nc.sbuf_tensor(n, s, d))
    psal = lambda n, s, d=F32: st.enter_context(nc.psum_tensor(n, s, d))

    prog = []

    def E(key, fn):
        prog.append((key, fn))

    with st:
        T = Ctr(st, nc, "T")
        V = Ctr(st, nc, "V")
        S = Ctr(st, nc, "S")
        G = Ctr(st, nc, "G")
        DSY = DQ(st, nc, "DSY", 8)
        DGP = DQ(st, nc, "DGP", 8)
        CC = Ctr(st, nc, "CC")

        h128b = sb("h128b", [128, 128], BF16)
        identb = sb("identb", [128, 128], BF16)
        trimaskf = sb("trimaskf", [128, 128], F32)
        mzb = sb("mzb", [128, 129], BF16)
        wqb = sb("wqb", [128, HC * 128], BF16)
        wkb = sb("wkb", [128, HC * 128], BF16)
        wvb = sb("wvb", [128, HC * 128], BF16)
        wob = sb("wob", [128, HC * 128], BF16)
        suugb = sb("suugb", [128, HC], F32)
        scub = sb("scub", [128, IC], F32)
        scgb = sb("scgb", [128, IC], F32)
        sdpb = sb("sdpb", [128, HC], F32)
        sudpb = sb("sudpb", [128, IC], F32)
        wrmb = sb("wrmb", [1, 128], F32)
        pmaskb = sb("pmaskb", [128, NCORES], F32)
        gate_sb = sb("gate_sb", [1, 1], F32)
        grow = sb("grow", [1, 1], F32)
        g128 = sb("g128", [128, 1], F32)
        g1m = sb("g1m", [128, 1], F32)
        ones_c = sb("ones_c", [128, 1], F32)
        ones_cb = sb("ones_cb", [128, 1], BF16)
        ones_r = sb("ones_r", [1, 128], F32)
        meanrow = sb("meanrow", [1, NB], F32)
        varrow = sb("varrow", [1, NB], F32)
        rstdrow = sb("rstdrow", [1, NB], F32)

        PS = [psal("ps%d" % i, [128, 512]) for i in range(6)]
        PBT = [psal("pbt%d" % i, [128, 128], BF16) for i in range(2)]

        xb = sb("xb", [128, HC * NB], F32)
        xlb = sb("xlb", [128, HC * NB], BF16)
        xsq = sb("xsq", [128, 2 * NB], F32)
        ups = sb("ups", [128, 2 * NB], BF16)
        gsil = sb("gsil", [128, 2 * NB], BF16)
        wub = sb("wub", [128, 2 * HC * 128], BF16)
        wgb = sb("wgb", [128, 2 * HC * 128], BF16)

        _slots_init = [DGP.slot() for _ in range(15)]

        def init_small(e, sl=_slots_init):
            it = iter(sl)
            _dma(e, next(it), h128b[:, :], din["h128"])
            _dma(e, next(it), identb[:, :], din["ident"])
            _dma(e, next(it), mzb[:, :], din["mz"])
            for wsb_, nm in ((wqb, "wqT"), (wkb, "wkT"), (wvb, "wvT")):
                _dma(e, next(it),
                     wsb_[:, :].rearrange("p (c d) -> p c d", d=128),
                     din[nm].rearrange("(c p) d -> p c d", p=128))
            _dma(e, next(it), wob[:, :], din["woT"])
            for dst, nm in (
                (trimaskf, "trimask"), (suugb, "su_ug"), (scub, "wsu"),
                (scgb, "wsg"), (sdpb, "wsdp"), (sudpb, "su_dp"),
                (pmaskb, "pmask"), (gate_sb, "gate1"),
            ):
                _dma(e, next(it), dst[:, :], din[nm])

        E("gp", init_small)
        ld_small = DGP.barrier()

        if lvl <= -3:
            def fin0(e, w=tuple(DGP.barrier())):
                _waitall(e, w)
            E("gp", fin0)
            return _emit(nc, prog)

        def init_vec(e):
            _waitall(e, ld_small)
            e.memset(ones_c[:, :], 1.0)
            e.memset(ones_cb[:, :], 1.0)
            e.memset(ones_r[:, :], 1.0)
            e.tensor_scalar_mul(scub[:, :], scub[:, :], RS2)
            e.tensor_scalar_mul(scgb[:, :], scgb[:, :], RS2)
            e.tensor_scalar_mul(sdpb[:, :], sdpb[:, :], 0.5 * RS8)
            e.drain()
            e.tensor_mul(scub[:, :], scub[:, :], sudpb[:, :])
            e.drain().then_inc(V.sem, 1)

        E("v", init_vec)
        v_init = V.reserve(1)

        if lvl <= -2:
            def fin1(e, w=v_init):
                e.wait_ge(V.sem, w)
            E("v", fin1)
            def fin0(e, w=tuple(DGP.barrier())):
                _waitall(e, w)
            E("gp", fin0)
            return _emit(nc, prog)

        def init_g_s(e, w=v_init):
            e.wait_ge(V.sem, w)
            e.activation(out=grow[:, :], in_=gate_sb[:, :], func=AF.Exp,
                         scale=-1.0).then_inc(S.sem, 1)

        E("s", init_g_s)
        s_g = S.reserve(1)

        def init_g_v(e, w=s_g):
            e.wait_ge(S.sem, w)
            e.tensor_scalar_add(grow[:, :], grow[:, :], 1.0)
            e.drain()
            e.reciprocal(grow[:, :], grow[:, :]).then_inc(V.sem, 1)

        E("v", init_g_v)
        v_g = V.reserve(1)

        def init_g_t(e, w=v_g):
            e.wait_ge(V.sem, w)
            e.matmul(PS[4][:, 0:1], ones_r[:, :], grow[:, :],
                     start=True, stop=True).then_inc(T.sem, 1)

        E("t", init_g_t)
        t_g = T.reserve(1)

        def init_g_v2(e, w=t_g):
            e.wait_ge(T.sem, w)
            e.tensor_copy(g128[:, :], PS[4][:, 0:1])
            e.drain()
            e.tensor_scalar(g1m[:, :], g128[:, :], -1.0, 1.0,
                            ALU.mult, ALU.add).then_inc(V.sem, 1)

        E("v", init_g_v2)
        v_gdone = V.reserve(1)

        if lvl <= -1:
            def fin1(e, w=v_gdone):
                e.wait_ge(V.sem, w)
            E("v", fin1)
            def fin0(e, w=tuple(DGP.barrier())):
                _waitall(e, w)
            E("gp", fin0)
            return _emit(nc, prog)

        trk = {
            "up_mm": [], "gate_mm": [],
            "up_ev": [], "gate_ev": [], "y_done": [],
            "dn_mm_half": [], "dn_mm": [], "dn_ev": [],
            "wu_dma": [], "wg_dma": [],
            "ln_norm": {},
            "last_xlb_reader_T": 0,
            "spill": [None] * NBLK,
            "xload": [None] * NBLK,
            "x1_load": [None] * NBLK,
            "outstore": [None] * NBLK,
            "ds_ev_last": [None] * NBLK,
            "ffn2_dn": [None] * NBLK,
        }

        def emit_ln(bmean, brstd, bsum, bsq, su_fuse, xlb_ready_key,
                    wait_T_before_write):
            wT = wait_T_before_write

            def vcast(e, wT=wT):
                if wT:
                    e.wait_ge(T.sem, wT)
                e.drain()
                for c in range(HC):
                    e.tensor_copy(xlb[:, c * NB:(c + 1) * NB],
                                  xb[:, c * NB:(c + 1) * NB])
                e.drain().then_inc(V.sem, 1)

            E("v", vcast)
            v_cast = V.reserve(1)

            def tsum(e, w=v_cast):
                e.wait_ge(V.sem, w)
                for c in range(HC):
                    e.matmul(PS[bsum][0:1, 0:NB], ones_cb[:, :],
                             xlb[:, c * NB:(c + 1) * NB],
                             start=(c == 0), stop=(c == HC - 1))
                e.nop().then_inc(T.sem, 1)

            E("t", tsum)
            T.reserve(1)

            sq_v = []
            sqmm_t = []
            for c in range(HC):
                par = c % 2

                def vsq(e, c=c, par=par):
                    if c >= 2:
                        e.wait_ge(T.sem, sqmm_t[c - 2])
                    e.tensor_mul(ups[:, par * NB:(par + 1) * NB],
                                 xlb[:, c * NB:(c + 1) * NB],
                                 xlb[:, c * NB:(c + 1) * NB]).then_inc(V.sem, 1)

                E("v", vsq)
                sq_v.append(V.reserve(1))

                def tsqc(e, c=c, par=par, w=sq_v[-1]):
                    e.wait_ge(V.sem, w)
                    e.matmul(PS[bsq][0:1, 0:NB], ones_cb[:, :],
                             ups[:, par * NB:(par + 1) * NB],
                             start=(c == 0), stop=(c == HC - 1),
                             ).then_inc(T.sem, 1)

                E("t", tsqc)
                sqmm_t.append(T.reserve(1))

            t_sq = sqmm_t[-1]

            def vrows(e, w=t_sq):
                e.wait_ge(T.sem, w)
                e.tensor_scalar_mul(meanrow[0:1, :], PS[bsum][0:1, 0:NB], 1.0 / H)
                e.drain()
                e.tensor_mul(varrow[0:1, :], meanrow[0:1, :], meanrow[0:1, :])
                e.drain()
                e.scalar_tensor_tensor(varrow[0:1, :], PS[bsq][0:1, 0:NB], 1.0 / H,
                                       varrow[0:1, :], ALU.mult, ALU.subtract)
                e.drain()
                e.tensor_scalar_add(varrow[0:1, :], varrow[0:1, :],
                                    EPS_LN).then_inc(V.sem, 1)

            E("v", vrows)
            v_rows = V.reserve(1)

            def srows(e, w=v_rows):
                e.wait_ge(V.sem, w)
                e.activation(out=rstdrow[0:1, :], in_=varrow[0:1, :], func=AF.Ln)
                e.drain()
                e.activation(out=rstdrow[0:1, :], in_=rstdrow[0:1, :], func=AF.Exp,
                             scale=-0.5).then_inc(S.sem, 1)

            E("s", srows)
            s_rows = S.reserve(1)

            def tbcast(e, w=s_rows):
                e.wait_ge(S.sem, w)
                e.matmul(PS[bmean][:, 0:NB], ones_r[:, :], meanrow[0:1, :],
                         start=True, stop=True)
                e.matmul(PS[brstd][:, 0:NB], ones_r[:, :], rstdrow[0:1, :],
                         start=True, stop=True).then_inc(T.sem, 1)

            E("t", tbcast)
            t_bcast = T.reserve(1)

            def vnorm(e, w=t_bcast, su=su_fuse):
                e.wait_ge(T.sem, w)
                for c in range(HC):
                    par = c % 2
                    e.tensor_sub(xsq[:, par * NB:(par + 1) * NB],
                                 xb[:, c * NB:(c + 1) * NB], PS[bmean][:, 0:NB])
                    e.drain()
                    if su:
                        e.scalar_tensor_tensor(
                            xlb[:, c * NB:(c + 1) * NB],
                            xsq[:, par * NB:(par + 1) * NB],
                            suugb[:, c:c + 1], PS[brstd][:, 0:NB],
                            ALU.mult, ALU.mult)
                    else:
                        e.tensor_mul(xlb[:, c * NB:(c + 1) * NB],
                                     xsq[:, par * NB:(par + 1) * NB],
                                     PS[brstd][:, 0:NB])
                e.drain().then_inc(V.sem, 1)

            E("v", vnorm)
            v_norm = V.reserve(1)
            trk["ln_norm"][xlb_ready_key] = v_norm
            return v_norm

        def emit_ffn(v_xlb_ready):
            for i in range(IC):
                gi = len(trk["up_mm"])
                par = i % 2
                _su = DSY.slot()

                def wdma(e, m=i, par=par, gi=gi, sl=_su):
                    if gi == 0:
                        e.wait_ge(CC.sem, cc_u)
                    if gi >= 2:
                        e.wait_ge(T.sem, trk["up_mm"][gi - 2])
                    _dma(e, sl,
                         wub[:, par * 2048:(par + 1) * 2048]
                         .rearrange("p (c i) -> p c i", i=128),
                         ago_u.ap()[m])

                E("sy", wdma)
                trk["wu_dma"].append((_su[0], _su[2]))

                def upmm(e, i=i, par=par, gi=gi, v_xlb=v_xlb_ready,
                         wd=trk["wu_dma"][-1]):
                    if i == 0:
                        e.wait_ge(V.sem, v_xlb)
                    e.wait_ge(wd[0], wd[1])
                    if gi >= 2:
                        e.wait_ge(S.sem, trk["up_ev"][gi - 2])
                    for c in range(HC):
                        inst = e.matmul(PS[0 + par][:, 0:NB],
                                        wub[:, par * 2048 + c * 128:
                                             par * 2048 + (c + 1) * 128],
                                        xlb[:, c * NB:(c + 1) * NB],
                                        start=(c == 0), stop=(c == HC - 1))
                    inst.then_inc(T.sem, 1)

                E("t", upmm)
                trk["up_mm"].append(T.reserve(1))

                def upev(e, i=i, par=par, gi=gi, w=trk["up_mm"][-1]):
                    e.wait_ge(T.sem, w)
                    if gi == 0:
                        _waitall(e, tuple(g_dmas)[-8:])
                    e.activation(out=yb[:, i * NB:(i + 1) * NB],
                                 in_=PS[0 + par][:, 0:NB], func=AF.Identity,
                                 scale=scub[:, i:i + 1]).then_inc(S.sem, 1)

                E("s", upev)
                trk["up_ev"].append(S.reserve(1))

            for i in range(IC):
                gi = len(trk["gate_mm"])
                par = i % 2
                _sg = DSY.slot()

                def wdma_g(e, m=i, par=par, gi=gi, sl=_sg):
                    if gi == 0:
                        e.wait_ge(CC.sem, cc_g)
                    if gi >= 2:
                        e.wait_ge(T.sem, trk["gate_mm"][gi - 2])
                    _dma(e, sl,
                         wgb[:, par * 2048:(par + 1) * 2048]
                         .rearrange("p (c i) -> p c i", i=128),
                         ago_g.ap()[m])

                E("sy", wdma_g)
                trk["wg_dma"].append((_sg[0], _sg[2]))

                def gmm(e, par=par, gi=gi, wd=trk["wg_dma"][-1]):
                    if gi == 0:
                        e.wait_ge(S.sem, s1ev_last_v)
                    e.wait_ge(wd[0], wd[1])
                    if gi >= 2:
                        e.wait_ge(S.sem, trk["gate_ev"][gi - 2])
                    for c in range(HC):
                        inst = e.matmul(PS[2 + par][:, 0:NB],
                                        wgb[:, par * 2048 + c * 128:
                                             par * 2048 + (c + 1) * 128],
                                        xlb[:, c * NB:(c + 1) * NB],
                                        start=(c == 0), stop=(c == HC - 1))
                    inst.then_inc(T.sem, 1)

                E("t", gmm)
                trk["gate_mm"].append(T.reserve(1))

                def gev(e, i=i, par=par, gi=gi, w=trk["gate_mm"][-1]):
                    e.wait_ge(T.sem, w)
                    if gi >= 2:
                        e.wait_ge(V.sem, trk["y_done"][gi - 2])
                    e.activation(out=gsil[:, par * NB:(par + 1) * NB],
                                 in_=PS[2 + par][:, 0:NB], func=AF.Silu,
                                 scale=scgb[:, i:i + 1]).then_inc(S.sem, 1)

                E("s", gev)
                trk["gate_ev"].append(S.reserve(1))

                def yprod(e, i=i, par=par, gi=gi, w=trk["gate_ev"][-1],
                          wu=trk["up_ev"][-(IC - i)]):
                    e.wait_ge(S.sem, w)
                    e.tensor_mul(yb[:, i * NB:(i + 1) * NB],
                                 gsil[:, par * NB:(par + 1) * NB],
                                 yb[:, i * NB:(i + 1) * NB]).then_inc(V.sem, 1)

                E("v", yprod)
                trk["y_done"].append(V.reserve(1))

            trk["last_xlb_reader_T"] = trk["gate_mm"][-1]
            y_all = trk["y_done"][-1]

            dn_last = None
            for h in range(HC):
                gh = len(trk["dn_mm"])
                par = h % 2
                halves = []
                for half in range(2):
                    _sd = DSY.slot()

                    def wd_dma(e, h=h, half=half, gh=gh, sl=_sd):
                        if gh == 0 and half == 0:
                            e.wait_ge(CC.sem, cc_d)
                        if gh >= 1:
                            if half == 0:
                                e.wait_ge(T.sem, trk["dn_mm_half"][gh - 1])
                            else:
                                e.wait_ge(T.sem, trk["dn_mm"][gh - 1])
                        _dma(e, sl,
                             wdb[:, half * 32 * 128:(half + 1) * 32 * 128]
                             .rearrange("p (ic hh) -> p ic hh", hh=128),
                             ago_d.ap()[h, :, half * 32:(half + 1) * 32, :])

                    E("sy", wd_dma)
                    halves.append((_sd[0], _sd[2]))

                def dnmm(e, h=h, par=par, gh=gh, halves=tuple(halves),
                         y_all=y_all):
                    if h == 0:
                        e.wait_ge(V.sem, y_all)
                    if gh >= 2:
                        e.wait_ge(V.sem, trk["dn_ev"][gh - 2])
                    e.wait_ge(halves[0][0], halves[0][1])
                    for ic in range(32):
                        inst = e.matmul(PS[4 + par][:, 0:NB],
                                        wdb[:, ic * 128:(ic + 1) * 128],
                                        yb[:, ic * NB:(ic + 1) * NB],
                                        start=(ic == 0), stop=False)
                    inst.then_inc(T.sem, 1)
                    e.wait_ge(halves[1][0], halves[1][1])
                    for ic in range(32, 64):
                        inst = e.matmul(PS[4 + par][:, 0:NB],
                                        wdb[:, ic * 128:(ic + 1) * 128],
                                        yb[:, ic * NB:(ic + 1) * NB],
                                        start=False, stop=(ic == 63))
                    inst.then_inc(T.sem, 1)

                E("t", dnmm)
                trk["dn_mm_half"].append(T.reserve(1))
                trk["dn_mm"].append(T.reserve(1))

                def dnev(e, h=h, par=par, w=trk["dn_mm"][-1]):
                    e.wait_ge(T.sem, w)
                    e.scalar_tensor_tensor(xb[:, h * NB:(h + 1) * NB],
                                           PS[4 + par][:, 0:NB],
                                           sdpb[:, h:h + 1],
                                           xb[:, h * NB:(h + 1) * NB],
                                           ALU.mult, ALU.add).then_inc(V.sem, 1)

                E("v", dnev)
                trk["dn_ev"].append(V.reserve(1))
                dn_last = trk["dn_ev"][-1]
            return dn_last

        fold_stack = ExitStack()
        fdin = fold_stack.enter_context(nc.sbuf_tensor("fdin", [128, IC * HSL], BF16))
        fdsc = fold_stack.enter_context(nc.sbuf_tensor("fdsc", [128, 4 * HSL], BF16))
        fdout = fold_stack.enter_context(nc.sbuf_tensor("fdout", [128, HSL], BF16))
        fwin = fold_stack.enter_context(nc.sbuf_tensor("fwin", [128, HC * ISL], BF16))
        fwsc = fold_stack.enter_context(nc.sbuf_tensor("fwsc", [128, ISL], BF16))

        cc_u = cc_g = cc_d = 0
        u_dmas = g_dmas = ()
        do_ug = lvl >= 1 and lvl != 3

        def warm_v(e):
            e.memset(wrmb[:, :], 1.0)
            e.drain().then_inc(V.sem, 1)

        E("v", warm_v)
        v_warm = V.reserve(1)
        _slw = DGP.slot()

        def ag_warm(e, wv=v_warm, sl=_slw):
            e.wait_ge(V.sem, wv)
            _dma(e, sl, wrm_i.ap(), wrmb[:, :])
            e.wait_ge(sl[0], sl[2])
            e.collective_compute(
                "AllGather", ALU.bypass,
                replica_groups=[list(range(NCORES))],
                ins=[wrm_i.ap().opt()],
                outs=[wrm_o.ap().opt()],
            ).then_inc(CC.sem, 1)

        if WARM:
            E("gp", ag_warm)
            cc_warm = CC.reserve(1)
        else:
            cc_warm = 0

        if do_ug:
            ld_u_done = []
            for uh in range(2):
                _slu = DGP.slot()

                def ldslice_u(e, uh=uh, sl=_slu):
                    _dma(e, sl,
                         fwin[:, uh * 8 * ISL:(uh + 1) * 8 * ISL]
                         .rearrange("p (c i) -> p c i", i=ISL),
                         din["wu_slice"][uh * 1024:(uh + 1) * 1024, :]
                         .rearrange("(c p) i -> p c i", p=128))

                E("gp", ldslice_u)
                ld_u_done.append((_slu[0], _slu[2]))

        _slwd = DGP.slot()

        def ld_wd(e, sl=_slwd):
            _dma(e, sl,
                 fdin[:, :].rearrange("p (c h) -> p c h", h=HSL),
                 din["wd_slice"].rearrange("(c p) h -> p c h", p=128))

        E("gp", ld_wd)
        ld_wd_done = (_slwd[0], _slwd[2])

        _sx0 = DGP.slot()

        def xload0(e, sl=_sx0):
            _dma(e, sl,
                 xb[:, :].rearrange("p (c n) -> p c n", n=NB),
                 din["x"][:, 0:NB].rearrange("(c p) n -> p c n", p=128))

        E("gp", xload0)
        trk["xload"][0] = (_sx0[0], _sx0[2])

        if do_ug:
            ev_vals = []
            t_vals = []
            for g in range(2 * HC):
                c, half = g // 2, g % 2
                par = g % 2

                def sA(e, c=c, half=half, par=par, g=g, w=tuple(ld_u_done)):
                    if g == 0:
                        e.wait_ge(w[0][0], w[0][1])
                        _waitall(e, ld_small[:1])
                    if g == 16:
                        e.wait_ge(w[1][0], w[1][1])
                    if g >= 2:
                        e.wait_ge(V.sem, ev_vals[g - 2])
                    e.matmul(PS[par][:, 0:512], h128b[:, :],
                             fwin[:, c * ISL + half * 512: c * ISL + (half + 1) * 512],
                             start=True, stop=True).then_inc(T.sem, 1)

                E("t", sA)
                t_vals.append(T.reserve(1))

                def sAev(e, c=c, half=half, par=par, tv=t_vals[-1]):
                    e.wait_ge(T.sem, tv)
                    e.tensor_copy(fwin[:, c * ISL + half * 512:
                                       c * ISL + (half + 1) * 512],
                                  PS[par][:, 0:512]).then_inc(V.sem, 1)

                E("v", sAev)
                ev_vals.append(V.reserve(1))

            t_sA_u_last = t_vals[-1]

            def ubfly(e, w=ev_vals[-1]):
                e.drain()
                for bit in range(4):
                    step = 1 << bit
                    pairs = [j for j in range(HC) if not (j & step)]
                    for base in range(0, len(pairs), 2):
                        grp = pairs[base:base + 2]
                        for q, j in enumerate(grp):
                            sc = fwsc if q == 0 else fdsc
                            e.tensor_sub(sc[:, 0:ISL],
                                         fwin[:, j * ISL:(j + 1) * ISL],
                                         fwin[:, (j + step) * ISL:(j + step + 1) * ISL])
                        e.drain()
                        for q, j in enumerate(grp):
                            e.tensor_add(fwin[:, j * ISL:(j + 1) * ISL],
                                         fwin[:, j * ISL:(j + 1) * ISL],
                                         fwin[:, (j + step) * ISL:(j + step + 1) * ISL])
                        e.drain()
                        for q, j in enumerate(grp):
                            sc = fwsc if q == 0 else fdsc
                            e.tensor_copy(fwin[:, (j + step) * ISL:(j + step + 1) * ISL],
                                          sc[:, 0:ISL])
                        e.drain()
                e.drain().then_inc(V.sem, 1)

            E("v", ubfly)
            v_bfly = V.reserve(1)

            dma_vals = []
            for g in range(2 * HC):
                c, ihalf = g // 2, g % 2
                _s2 = DGP.slot()

                def dmf(e, c=c, ihalf=ihalf, g=g, sl=_s2, wb=v_bfly):
                    if g == 0:
                        e.wait_ge(V.sem, wb)
                    _dma(e, sl,
                         agi_u.ap()[ihalf * 4:(ihalf + 1) * 4, :, c, :]
                         .rearrange("li p i -> p li i"),
                         fwin[:, c * ISL + ihalf * 512: c * ISL + (ihalf + 1) * 512]
                         .rearrange("p (li i) -> p li i", i=128))

                E("gp", dmf)
                dma_vals.append((_s2[0], _s2[2]))
            u_dmas = dma_vals

            def ag_u_f(e, w=tuple(u_dmas)):
                if WARM:
                    e.wait_ge(CC.sem, cc_warm)
                _waitall(e, w)
                e.collective_compute(
                    "AllGather", ALU.bypass,
                    replica_groups=[list(range(NCORES))],
                    ins=[agi_u.ap().opt()],
                    outs=[ago_u.ap().opt()],
                ).then_inc(CC.sem, 1)

            E("gp", ag_u_f)
            cc_u = CC.reserve(1)

        s1_ev = []
        for c in range(IC):
            par = c % 2

            def s1mm(e, c=c, par=par, w=ld_wd_done,
                     wv=(s1_ev[c - 2] if c >= 2 else None)):
                if c == 0:
                    e.wait_ge(w[0], w[1])
                    _waitall(e, ld_small[:1])
                if wv is not None:
                    e.wait_ge(S.sem, wv)
                e.matmul(PS[2 + par][:, 0:HSL], h128b[:, :],
                         fdin[:, c * HSL:(c + 1) * HSL],
                         start=True, stop=True).then_inc(T.sem, 1)

            E("t", s1mm)
            t_done = T.reserve(1)

            def s1ev(e, c=c, par=par, tv=t_done):
                e.wait_ge(T.sem, tv)
                e.activation(out=fdin[:, c * HSL:(c + 1) * HSL],
                             in_=PS[2 + par][:, 0:HSL],
                             func=AF.Identity).then_inc(S.sem, 1)

            E("s", s1ev)
            s1_ev.append(S.reserve(1))

        s1ev_last_v = s1_ev[-1]

        def wd_bfly_part(bits):
            def part(e, bits=bits, w=s1ev_last_v):
                if bits[0] == 0:
                    e.wait_ge(S.sem, w)
                for bit in bits:
                    step = 1 << bit
                    pairs = [j for j in range(IC) if not (j & step)]
                    for base in range(0, len(pairs), 4):
                        grp = pairs[base:base + 4]
                        for q, j in enumerate(grp):
                            e.tensor_sub(fdsc[:, q * HSL:(q + 1) * HSL],
                                         fdin[:, j * HSL:(j + 1) * HSL],
                                         fdin[:, (j + step) * HSL:(j + step + 1) * HSL])
                        e.drain()
                        for q, j in enumerate(grp):
                            e.tensor_add(fdin[:, j * HSL:(j + 1) * HSL],
                                         fdin[:, j * HSL:(j + 1) * HSL],
                                         fdin[:, (j + step) * HSL:(j + step + 1) * HSL])
                        e.drain()
                        for q, j in enumerate(grp):
                            e.tensor_copy(fdin[:, (j + step) * HSL:(j + step + 1) * HSL],
                                          fdsc[:, q * HSL:(q + 1) * HSL])
                        e.drain()
                e.drain().then_inc(G.sem, 1)

            return part

        if lvl >= 2 and lvl != 3:
            ldg_vals = []
            for c in range(HC):
                _sg2 = DGP.slot()

                def ldslice_g(e, c=c, sl=_sg2,
                              pd=(tuple(u_dmas[2 * c:2 * c + 2])
                                  if u_dmas else ())):
                    _waitall(e, pd)
                    _dma(e, sl,
                         fwin[:, c * ISL:(c + 1) * ISL]
                         .rearrange("p (o i) -> p o i", o=1),
                         din["wg_slice"][c * 128:(c + 1) * 128, :]
                         .rearrange("(o p) i -> p o i", p=128))

                E("gp", ldslice_g)
                ldg_vals.append((_sg2[0], _sg2[2]))

            ev_vals_g = []
            t_vals_g = []
            for g in range(2 * HC):
                c, half = g // 2, g % 2
                par = g % 2

                def sAg(e, c=c, half=half, par=par, g=g, w=ldg_vals[c]):
                    if half == 0:
                        e.wait_ge(w[0], w[1])
                    if g >= 2:
                        e.wait_ge(V.sem, ev_vals_g[g - 2])
                    e.matmul(PS[par][:, 0:512], h128b[:, :],
                             fwin[:, c * ISL + half * 512: c * ISL + (half + 1) * 512],
                             start=True, stop=True).then_inc(T.sem, 1)

                E("t", sAg)
                t_vals_g.append(T.reserve(1))

                def sAevg(e, c=c, half=half, par=par, tv=t_vals_g[-1]):
                    e.wait_ge(T.sem, tv)
                    e.tensor_copy(fwin[:, c * ISL + half * 512:
                                       c * ISL + (half + 1) * 512],
                                  PS[par][:, 0:512]).then_inc(V.sem, 1)

                E("v", sAevg)
                ev_vals_g.append(V.reserve(1))

            t_sA_g_last = t_vals_g[-1]

            def ubfly_g(e, w=ev_vals_g[-1]):
                e.drain()
                for bit in range(4):
                    step = 1 << bit
                    pairs = [j for j in range(HC) if not (j & step)]
                    for base in range(0, len(pairs), 2):
                        grp = pairs[base:base + 2]
                        for q, j in enumerate(grp):
                            sc = fwsc if q == 0 else fdsc
                            e.tensor_sub(sc[:, 0:ISL],
                                         fwin[:, j * ISL:(j + 1) * ISL],
                                         fwin[:, (j + step) * ISL:(j + step + 1) * ISL])
                        e.drain()
                        for q, j in enumerate(grp):
                            e.tensor_add(fwin[:, j * ISL:(j + 1) * ISL],
                                         fwin[:, j * ISL:(j + 1) * ISL],
                                         fwin[:, (j + step) * ISL:(j + step + 1) * ISL])
                        e.drain()
                        for q, j in enumerate(grp):
                            sc = fwsc if q == 0 else fdsc
                            e.tensor_copy(fwin[:, (j + step) * ISL:(j + step + 1) * ISL],
                                          sc[:, 0:ISL])
                        e.drain()
                e.drain().then_inc(V.sem, 1)

            E("v", ubfly_g)
            v_bfly_g = V.reserve(1)

            g_dmas = []
            for g in range(2 * HC):
                c, ihalf = g // 2, g % 2
                _s2 = DGP.slot()

                def dmfg(e, c=c, ihalf=ihalf, g=g, sl=_s2, wb=v_bfly_g):
                    if g == 0:
                        e.wait_ge(V.sem, wb)
                    _dma(e, sl,
                         agi_g.ap()[ihalf * 4:(ihalf + 1) * 4, :, c, :]
                         .rearrange("li p i -> p li i"),
                         fwin[:, c * ISL + ihalf * 512: c * ISL + (ihalf + 1) * 512]
                         .rearrange("p (li i) -> p li i", i=128))

                E("gp", dmfg)
                g_dmas.append((_s2[0], _s2[2]))

            def ag_g_f(e, w=tuple(g_dmas)):
                _waitall(e, w)
                e.collective_compute(
                    "AllGather", ALU.bypass,
                    replica_groups=[list(range(NCORES))],
                    ins=[agi_g.ap().opt()],
                    outs=[ago_g.ap().opt()],
                ).then_inc(CC.sem, 1)

            E("gp", ag_g_f)
            cc_g = CC.reserve(1)
        else:
            t_sA_g_last = 0

        def xwait_t0(e, w=trk["xload"][0]):
            e.wait_ge(w[0], w[1])

        E("t", xwait_t0)

        def xwait_v0(e, w=trk["xload"][0], g=v_gdone):
            e.wait_ge(w[0], w[1])
            e.wait_ge(V.sem, g)

        E("v", xwait_v0)
        v_ln1_blk0 = emit_ln(4, 5, 4, 5, su_fuse=True,
                             xlb_ready_key=(0, "ln1"),
                             wait_T_before_write=0)

        E("v", wd_bfly_part((0, 1, 2)))
        G.reserve(1)
        E("v", wd_bfly_part((3, 4, 5)))
        g_s2_done = G.reserve(1)

        d_dma_vals = []
        for c in range(IC):
            _s3d = DGP.slot()
            d_dma_vals.append((_s3d[0], _s3d[2]))

            def s3d(e, c=c, wb=g_s2_done, sl=_s3d):
                if c == 0:
                    e.wait_ge(G.sem, wb)
                _dma(e, sl,
                     agi_d.ap()[:, :, c, :].rearrange("m p h -> p m h"),
                     fdin[:, c * HSL:(c + 1) * HSL]
                     .rearrange("p (m h) -> p m h", h=128))

            E("gp", s3d)

        g_s3_last = g_s2_done
        wd_out_dmas = tuple(d_dma_vals[-8:])

        all_fold_dma = tuple(DGP.barrier())

        if lvl == 3:
            def finf(e, w=all_fold_dma):
                _waitall(e, w)
            E("gp", finf)
            fold_stack.close()
            return _emit(nc, prog)

        def ag_d_f(e, w=all_fold_dma):
            _waitall(e, w)
            e.collective_compute(
                "AllGather", ALU.bypass,
                replica_groups=[list(range(NCORES))],
                ins=[agi_d.ap().opt()],
                outs=[ago_d.ap().opt()],
            ).then_inc(CC.sem, 1)

        E("gp", ag_d_f)
        cc_d = CC.reserve(1)

        fold_stack.close()

        pqf = sb("pqf", [128, NT], BF16)
        pkf = sb("pkf", [128, NT], BF16)
        vaugb = sb("vaugb", [128, NCH * ST], BF16)
        dS = sb("dS", [128, NCH * ST], F32)
        sfull = sb("sfull", [128, NCH * ST], BF16)
        rl = sb("rl", [128, 1], F32)
        rm = sb("rm", [128, 1], F32)
        grl = sb("grl", [128, 1], F32)
        grm = sb("grm", [128, 1], F32)
        blendb = sb("blendb", [128, 128], BF16)
        blendT = sb("blendT", [128, NB], BF16)
        wdb = sb("wdb", [128, IC * 128], BF16)
        yb = sb("yb", [128, IC * NB], BF16)

        def attn_safe(e, w=g_s3_last, d=wd_out_dmas, gd=tuple(g_dmas)):
            e.wait_ge(G.sem, w)
            _waitall(e, d)
            _waitall(e, gd)
            if t_sA_g_last:
                e.wait_ge(T.sem, t_sA_g_last)
            for j in range(NCH):
                e.memset(vaugb[:, j * ST + 128: j * ST + 129], 1.0)
            e.drain().then_inc(V.sem, 1)

        E("v", attn_safe)
        V.reserve(1)

        n_p1 = 0 if lvl < 5 else (1 if lvl == 5 else NBLK)
        s_elu_q = {}
        s_elu_k = {}
        for blk in range(n_p1):
            if blk == 0:
                v_ln1 = v_ln1_blk0
            else:
                _sx = DGP.slot()

                def xload(e, blk=blk, sl=_sx):
                    sp = trk["spill"][blk - 1]
                    e.wait_ge(sp[0], sp[1])
                    e.wait_ge(V.sem, trk["ln_norm"][(blk - 1, "ln2")])
                    _dma(e, sl,
                         xb[:, :].rearrange("p (c n) -> p c n", n=NB),
                         din["x"][:, blk * NB:(blk + 1) * NB]
                         .rearrange("(c p) n -> p c n", p=128))

                E("gp", xload)
                trk["xload"][blk] = (_sx[0], _sx[2])

                def xwait_t(e, blk=blk):
                    xl = trk["xload"][blk]
                    e.wait_ge(xl[0], xl[1])
                    e.wait_ge(V.sem, trk["ds_ev_last"][blk - 1])

                E("t", xwait_t)

                def xwait_v(e, blk=blk):
                    xl = trk["xload"][blk]
                    e.wait_ge(xl[0], xl[1])

                E("v", xwait_v)

                v_ln1 = emit_ln(0, 1, 4, 5, su_fuse=True,
                                xlb_ready_key=(blk, "ln1"),
                                wait_T_before_write=trk["last_xlb_reader_T"])

            dn_last = emit_ffn(v_ln1)

            _ssp = DGP.slot()

            def spill(e, blk=blk, w=dn_last, sl=_ssp):
                e.wait_ge(V.sem, w)
                _dma(e, sl,
                     x1s[:, blk * NB:(blk + 1) * NB]
                     .rearrange("(c p) n -> p c n", p=128),
                     xb[:, :].rearrange("p (c n) -> p c n", n=NB))

            E("gp", spill)
            trk["spill"][blk] = (_ssp[0], _ssp[2])

            def xwait_t2(e, w=dn_last):
                e.wait_ge(V.sem, w)

            E("t", xwait_t2)
            v_ln2 = emit_ln(0, 1, 4, 5, su_fuse=False,
                            xlb_ready_key=(blk, "ln2"),
                            wait_T_before_write=trk["last_xlb_reader_T"])

            def qmm(e, w=v_ln2):
                e.wait_ge(V.sem, w)
                for c in range(HC):
                    inst = e.matmul(PS[0][:, 0:NB], wqb[:, c * 128:(c + 1) * 128],
                             xlb[:, c * NB:(c + 1) * NB],
                             start=(c == 0), stop=(c == HC - 1))
                    if c == HC - 1:
                        inst.then_inc(T.sem, 1)

            E("t", qmm)
            t_q = T.reserve(1)

            def kmm(e):
                for c in range(HC):
                    inst = e.matmul(PS[1][:, 0:NB], wkb[:, c * 128:(c + 1) * 128],
                             xlb[:, c * NB:(c + 1) * NB],
                             start=(c == 0), stop=(c == HC - 1))
                    if c == HC - 1:
                        inst.then_inc(T.sem, 1)

            E("t", kmm)
            t_k = T.reserve(1)

            v_qmin = V.at() + 1
            v_qdone = V.at() + 2
            v_kmin = V.at() + 3
            v_kdone = V.at() + 4
            sq_val = S.at() + 1
            sk_val = S.at() + 2

            def elu_v(e, blk=blk, t_q=t_q, t_k=t_k, sq=sq_val, sk=sk_val):
                e.wait_ge(T.sem, t_q)
                e.tensor_scalar_min(xsq[:, 0:NB], PS[0][:, 0:NB],
                                    0.0).then_inc(V.sem, 1)
                e.wait_ge(S.sem, sq)
                e.tensor_scalar_max(xsq[:, NB:2 * NB], PS[0][:, 0:NB], 0.0)
                e.drain()
                e.tensor_add(pqf[:, blk * NB:(blk + 1) * NB],
                             xsq[:, 0:NB], xsq[:, NB:2 * NB]).then_inc(V.sem, 1)
                e.drain()
                e.wait_ge(T.sem, t_k)
                e.tensor_scalar_min(xsq[:, 0:NB], PS[1][:, 0:NB],
                                    0.0).then_inc(V.sem, 1)
                e.wait_ge(S.sem, sk)
                e.tensor_scalar_max(xsq[:, NB:2 * NB], PS[1][:, 0:NB], 0.0)
                e.drain()
                e.tensor_add(pkf[:, blk * NB:(blk + 1) * NB],
                             xsq[:, 0:NB], xsq[:, NB:2 * NB]).then_inc(V.sem, 1)

            def elu_s(e, vq=v_qmin, vk=v_kmin):
                e.wait_ge(V.sem, vq)
                e.activation(out=xsq[:, 0:NB], in_=xsq[:, 0:NB],
                             func=AF.Exp).then_inc(S.sem, 1)
                e.wait_ge(V.sem, vk)
                e.activation(out=xsq[:, 0:NB], in_=xsq[:, 0:NB],
                             func=AF.Exp).then_inc(S.sem, 1)

            E("v", elu_v)
            V.reserve(4)
            E("s", elu_s)
            s_elu_q[blk] = S.reserve(1)
            s_elu_k[blk] = S.reserve(1)
            assert s_elu_q[blk] == sq_val and V.at() == v_kdone
            v_pk_done = v_kdone

            v_vev = []
            t_vv = []
            for jl in range(CPB):
                j = blk * CPB + jl
                par = jl % 2

                def vmm(e, jl=jl, par=par,
                        wv=(v_vev[jl - 2] if jl >= 2 else None)):
                    if wv is not None:
                        e.wait_ge(V.sem, wv)
                    for c in range(HC):
                        inst = e.matmul(PS[2 + par][:, 0:128],
                                 xlb[:, c * NB + jl * CH: c * NB + (jl + 1) * CH],
                                 wvb[:, c * 128:(c + 1) * 128],
                                 start=(c == 0), stop=(c == HC - 1))
                        if c == HC - 1:
                            inst.then_inc(T.sem, 1)

                E("t", vmm)
                t_vv.append(T.reserve(1))

                def vev_f(e, j=j, par=par, w=t_vv[-1]):
                    e.wait_ge(T.sem, w)
                    e.tensor_copy(vaugb[:, j * ST:j * ST + 128],
                                  PS[2 + par][:, 0:128]).then_inc(V.sem, 1)

                E("v", vev_f)
                v_vev.append(V.reserve(1))

            v_pkT = []
            t_pkT = []
            t_ds = []
            ds_ev = []
            for jl in range(CPB):
                j = blk * CPB + jl
                par = jl % 2

                def pktr(e, j=j, jl=jl, par=par, vpk=v_pk_done,
                         wv=(v_pkT[jl - 2] if jl >= 2 else None)):
                    if jl == 0:
                        e.wait_ge(V.sem, vpk)
                    if wv is not None:
                        e.wait_ge(V.sem, wv)
                    e.transpose(PBT[par][:, 0:128],
                                pkf[:, j * CH:(j + 1) * CH],
                                identb[:, :]).then_inc(T.sem, 1)

                E("t", pktr)
                t_pkT.append(T.reserve(1))

                def pktev(e, par=par, w=t_pkT[-1]):
                    e.wait_ge(T.sem, w)
                    e.tensor_copy(blendT[:, par * 128:(par + 1) * 128],
                                  PBT[par][:, 0:128]).then_inc(V.sem, 1)

                E("v", pktev)
                v_pkT.append(V.reserve(1))

                def dsmm(e, j=j, par=par, wv=max(v_pkT[-1], v_vev[jl])):
                    e.wait_ge(V.sem, wv)
                    e.matmul(PS[4 + par][:, 0:129],
                             blendT[:, par * 128:(par + 1) * 128],
                             vaugb[:, j * ST:j * ST + 129],
                             start=True, stop=True).then_inc(T.sem, 1)

                E("t", dsmm)
                t_ds.append(T.reserve(1))

                def dsev(e, j=j, par=par, w=t_ds[-1]):
                    e.wait_ge(T.sem, w)
                    e.tensor_copy(dS[:, j * ST:j * ST + 129],
                                  PS[4 + par][:, 0:129]).then_inc(V.sem, 1)

                E("v", dsev)
                ds_ev.append(V.reserve(1))
            trk["ds_ev_last"][blk] = ds_ev[-1]

            def blkprefix(e, blk=blk):
                e.drain()
                for jl in range(CPB):
                    j = blk * CPB + jl
                    if j == 0:
                        continue
                    e.tensor_add(dS[:, j * ST:j * ST + 129],
                                 dS[:, j * ST:j * ST + 129],
                                 dS[:, (j - 1) * ST:(j - 1) * ST + 129])
                    e.drain()
                e.drain().then_inc(V.sem, 1)

            E("v", blkprefix)
            trk["ds_ev_last"][blk] = V.reserve(1)

        v_phase1_done = V.at()
        last_blk = n_p1 - 1

        if lvl >= 7:
            v_prefix = trk["ds_ev_last"][NBLK - 1]

            _ssi = DGP.slot()
            cc_s = CC.at() + 1

            def cc_state(e, w=v_prefix, sl=_ssi, cc_s=cc_s):
                e.wait_ge(V.sem, w)
                _dma(e, sl, cc_si.ap(),
                     dS[:, (NCH - 1) * ST:(NCH - 1) * ST + 129])
                e.wait_ge(sl[0], sl[2])
                e.collective_compute(
                    "AllGather", ALU.bypass,
                    replica_groups=[list(range(NCORES))],
                    ins=[cc_si.ap().opt()],
                    outs=[cc_so.ap().opt()],
                ).then_inc(CC.sem, 1)
                e.wait_ge(CC.sem, cc_s)

            E("gp", cc_state)
            CC.reserve(1)

            v_acc = []
            for k in range(NCORES):
                par = k % 4

                _sk = DGP.slot()

                def ccg_ld(e, k=k, par=par, sl=_sk,
                           wv=None):
                    if k >= 4:
                        e.wait_ge(V.sem, v_acc[k - 4])
                    _dma(e, sl, xsq[:, par * ST:par * ST + 129],
                         cc_so.ap()[k * 128:(k + 1) * 128, :])

                E("gp", ccg_ld)

                def ccg_acc(e, k=k, par=par, w=(_sk[0], _sk[2])):
                    e.wait_ge(w[0], w[1])
                    if k == 0:
                        e.tensor_scalar_mul(xsq[:, 768:897],
                                            xsq[:, par * ST:par * ST + 129],
                                            pmaskb[:, k:k + 1]).then_inc(V.sem, 1)
                    else:
                        e.drain()
                        e.scalar_tensor_tensor(xsq[:, 768:897],
                                               xsq[:, par * ST:par * ST + 129],
                                               pmaskb[:, k:k + 1],
                                               xsq[:, 768:897],
                                               ALU.mult, ALU.add).then_inc(V.sem, 1)

                E("v", ccg_acc)
                v_acc.append(V.reserve(1))

            def mk_sfull(e, w=v_acc[-1]):
                e.wait_ge(V.sem, w)
                e.tensor_copy(sfull[:, 0:129], xsq[:, 768:897])
                for j in range(1, NCH):
                    e.tensor_add(sfull[:, j * ST:j * ST + 129],
                                 dS[:, (j - 1) * ST:(j - 1) * ST + 129],
                                 xsq[:, 768:897])
                e.drain().then_inc(V.sem, 1)

            E("v", mk_sfull)
            v_sfull = V.reserve(1)

        if lvl >= 8:
            for blk in range(NBLK):
                _sx1 = DGP.slot()

                def x1load(e, blk=blk, w=v_phase1_done, sl=_sx1):
                    if blk == 0:
                        e.wait_ge(V.sem, w)
                        sp = trk["spill"][NBLK - 1]
                        e.wait_ge(sp[0], sp[1])
                    else:
                        os_ = trk["outstore"][blk - 1]
                        e.wait_ge(os_[0], os_[1])
                    _dma(e, sl,
                         xb[:, :].rearrange("p (c n) -> p c n", n=NB),
                         x1s[:, blk * NB:(blk + 1) * NB]
                         .rearrange("(c p) n -> p c n", p=128))

                E("gp", x1load)
                trk["x1_load"][blk] = (_sx1[0], _sx1[2])

                v_at_ev = []
                v_blend_ev = []
                for jl in range(CPB):
                    j = blk * CPB + jl
                    pn = jl % 2
                    bnum = 0 if pn == 0 else 2
                    bmem = 1 if pn == 0 else 3
                    bat = 4 + pn

                    def atmm(e, j=j, jl=jl, bat=bat,
                             w=(trk["ds_ev_last"][NBLK - 1] if blk == 0 else None),
                             wv=(v_at_ev[jl - 2] if jl >= 2 else None),
                             wdn=(trk["ffn2_dn"][blk - 1]
                                  if jl < 2 and blk > 0 else None)):
                        if jl == 0 and w is not None:
                            e.wait_ge(V.sem, w)
                        if wdn is not None:
                            e.wait_ge(V.sem, wdn)
                        if wv is not None:
                            e.wait_ge(V.sem, wv)
                        e.matmul(PS[bat][:, 0:128],
                                 pkf[:, j * CH:(j + 1) * CH],
                                 pqf[:, j * CH:(j + 1) * CH],
                                 start=True, stop=True).then_inc(T.sem, 1)

                    E("t", atmm)
                    t_at = T.reserve(1)

                    def atmask(e, bat=bat, w=t_at):
                        e.wait_ge(T.sem, w)
                        e.tensor_mul(blendb[:, :], PS[bat][:, 0:128],
                                     trimaskf[:, :]).then_inc(V.sem, 1)

                    E("v", atmask)
                    v_at_ev.append(V.reserve(1))

                    def nummm(e, j=j, jl=jl, bnum=bnum, bmem=bmem, w=v_at_ev[-1],
                              wv=(v_blend_ev[jl - 2] if jl >= 2 else None)):
                        e.wait_ge(V.sem, w)
                        if wv is not None:
                            e.wait_ge(V.sem, wv)
                        e.matmul(PS[bnum][:, 0:129],
                                 pqf[:, j * CH:(j + 1) * CH],
                                 sfull[:, j * ST:j * ST + 129],
                                 start=True, stop=False)
                        e.matmul(PS[bnum][:, 0:129], blendb[:, :],
                                 vaugb[:, j * ST:j * ST + 129],
                                 start=False, stop=True)
                        e.matmul(PS[bmem][:, 0:129],
                                 pqf[:, j * CH:(j + 1) * CH],
                                 mzb[:, :], start=True, stop=True).then_inc(T.sem, 1)

                    E("t", nummm)
                    t_num = T.reserve(1)

                    def blendv(e, bnum=bnum, bmem=bmem, w=t_num):
                        e.wait_ge(T.sem, w)
                        e.tensor_scalar_add(rl[:, :], PS[bnum][:, 128:129], EPS_AT)
                        e.tensor_scalar_add(rm[:, :], PS[bmem][:, 128:129], EPS_AT)
                        e.drain()
                        e.reciprocal(rl[:, :], rl[:, :])
                        e.drain()
                        e.reciprocal(rm[:, :], rm[:, :])
                        e.drain()
                        e.tensor_mul(grl[:, :], rl[:, :], g1m[:, :])
                        e.tensor_mul(grm[:, :], rm[:, :], g128[:, :])
                        e.drain()
                        e.tensor_scalar_mul(xsq[:, 640:768], PS[bnum][:, 0:128], grl[:, 0:1])
                        e.drain()
                        e.scalar_tensor_tensor(blendb[:, :], PS[bmem][:, 0:128],
                                               grm[:, 0:1], xsq[:, 640:768],
                                               ALU.mult, ALU.add).then_inc(V.sem, 1)

                    E("v", blendv)
                    v_blend = V.reserve(1)

                    def btr(e, jl=jl, w=v_blend):
                        e.wait_ge(V.sem, w)
                        e.transpose(PBT[jl % 2][:, 0:128], blendb[:, :],
                                    identb[:, :]).then_inc(T.sem, 1)

                    E("t", btr)
                    t_btr = T.reserve(1)

                    def btrev(e, jl=jl, w=t_btr):
                        e.wait_ge(T.sem, w)
                        e.tensor_copy(blendT[:, jl * CH:(jl + 1) * CH],
                                      PBT[jl % 2][:, 0:128]).then_inc(V.sem, 1)

                    E("v", btrev)
                    v_blend_ev.append(V.reserve(1))

                v_op_ev = []
                for h in range(HC):
                    par = h % 2

                    def opmm(e, h=h, par=par, wbl=v_blend_ev[-1],
                             x1ld=trk["x1_load"][blk],
                             wv=(v_op_ev[h - 2] if h >= 2 else None)):
                        if h == 0:
                            e.wait_ge(V.sem, wbl)
                            e.wait_ge(x1ld[0], x1ld[1])
                        if wv is not None:
                            e.wait_ge(V.sem, wv)
                        e.matmul(PS[par][:, 0:NB],
                                 wob[:, h * 128:(h + 1) * 128],
                                 blendT[:, 0:NB],
                                 start=True, stop=True).then_inc(T.sem, 1)

                    E("t", opmm)
                    t_op = T.reserve(1)

                    def opev(e, h=h, par=par, w=t_op):
                        e.wait_ge(T.sem, w)
                        e.tensor_add(xb[:, h * NB:(h + 1) * NB],
                                     PS[par][:, 0:NB],
                                     xb[:, h * NB:(h + 1) * NB]).then_inc(V.sem, 1)

                    E("v", opev)
                    v_op_ev.append(V.reserve(1))

                def xwait_t3(e, w=v_op_ev[-1]):
                    e.wait_ge(V.sem, w)

                E("t", xwait_t3)

                def xwait_v3(e, w=v_op_ev[-1]):
                    e.wait_ge(V.sem, w)

                E("v", xwait_v3)
                v_ln3 = emit_ln(2, 3, 4, 5, su_fuse=True,
                                xlb_ready_key=(blk, "ln3"),
                                wait_T_before_write=trk["last_xlb_reader_T"])
                dn3_last = emit_ffn(v_ln3)
                trk["ffn2_dn"][blk] = dn3_last

                _so = DGP.slot()

                def outstore(e, blk=blk, w=dn3_last, sl=_so):
                    e.wait_ge(V.sem, w)
                    _dma(e, sl,
                         out[:, blk * NB:(blk + 1) * NB]
                         .rearrange("(c p) n -> p c n", p=128),
                         xb[:, :].rearrange("p (c n) -> p c n", n=NB))

                E("gp", outstore)
                trk["outstore"][blk] = (_so[0], _so[2])

        if "x1" in tap_aps:
            _st1 = DGP.slot()

            def tap_x1(e, w=trk["spill"][last_blk], sl=_st1):
                e.wait_ge(w[0], w[1])
                _dma(e, sl, tap_aps["x1"], x1s)

            E("gp", tap_x1)
        if "ds" in tap_aps:
            _st2 = DGP.slot()

            def tap_ds(e, w=v_prefix, sl=_st2):
                e.wait_ge(V.sem, w)
                _dma(e, sl, tap_aps["ds"], dS[:, :])

            E("gp", tap_ds)
        if "pq" in tap_aps:
            _st3 = DGP.slot()

            def tap_pq(e, w=v_phase1_done, sl=_st3):
                e.wait_ge(V.sem, w)
                _dma(e, sl, tap_aps["pq"], pqf[:, :])

            E("gp", tap_pq)
        if "sfull" in tap_aps:
            _st4 = DGP.slot()

            def tap_sf(e, w=v_sfull, sl=_st4):
                e.wait_ge(V.sem, w)
                _dma(e, sl, tap_aps["sfull"], sfull[:, :])

            E("gp", tap_sf)

        if "agu" in tap_aps:
            _st5 = DGP.slot()
            _st6 = DGP.slot()

            def tap_agu(e, w=cc_w, s5=_st5, s6=_st6):
                e.wait_ge(CC.sem, w)
                _dma(e, s5, tap_aps["agu"],
                     ago_u.ap()[7].rearrange("p c i -> p (c i)"))
                _dma(e, s6, tap_aps["agd"],
                     ago_d.ap()[15].rearrange("p ic hh -> p (ic hh)"))

            E("gp", tap_agu)

        def fin(e, w=tuple(DGP.barrier())):
            _waitall(e, w)

        E("gp", fin)

        return _emit(nc, prog)


def _emit(nc, prog):
    with nc.Block() as block:
        @block.sync
        def _(e):
            for k, fn in prog:
                if k == "sy":
                    fn(e)

        @block.gpsimd
        def _(e):
            for k, fn in prog:
                if k == "gp":
                    fn(e)

        @block.tensor
        def _(e):
            for k, fn in prog:
                if k == "t":
                    fn(e)

        @block.vector
        def _(e):
            for k, fn in prog:
                if k == "v":
                    fn(e)

        @block.scalar
        def _(e):
            for k, fn in prog:
                if k == "s":
                    fn(e)

    return nc


def prepare_inputs(inputs):
    f32 = np.float32
    x = np.asarray(inputs["L0_attn_hidden_states"], f32)
    x_t = np.ascontiguousarray(x.reshape(NT * NCORES, H).T)
    wuT = np.ascontiguousarray(np.asarray(inputs["L0_ffn_W_up"], f32).T)
    wgT = np.ascontiguousarray(np.asarray(inputs["L0_ffn_W_gate"], f32).T)
    wdT = np.ascontiguousarray(np.asarray(inputs["L0_ffn_W_down"], f32).T)
    wqT = np.ascontiguousarray(np.asarray(inputs["L0_attn_w_q"], f32).T)
    wkT = np.ascontiguousarray(np.asarray(inputs["L0_attn_w_k"], f32).T)
    wvT = np.ascontiguousarray(np.asarray(inputs["L0_attn_w_v"], f32).T)
    woT = np.ascontiguousarray(np.asarray(inputs["L0_attn_w_o"], f32).T)
    M = np.asarray(inputs["L0_attn_M"], f32)
    z = np.asarray(inputs["L0_attn_z"], f32)
    mz = np.ascontiguousarray(np.concatenate([M, z[:, None]], axis=1))
    gate1 = np.ascontiguousarray(np.asarray(inputs["L0_attn_gate"], f32).reshape(1, 1))
    su_ug = np.ascontiguousarray(
        np.asarray(inputs["L0_ffn_SU_ug"], f32).reshape(HC, 128).T)
    su_dp = np.ascontiguousarray(
        np.asarray(inputs["L0_ffn_SU_dp"], f32).reshape(IC, 128).T)
    wsug = np.asarray(inputs["L0_ffn_Wscale_ug"], f32)
    wsu = np.ascontiguousarray(wsug[:INTER].reshape(IC, 128).T)
    wsg = np.ascontiguousarray(wsug[INTER:].reshape(IC, 128).T)
    wsdp = np.ascontiguousarray(
        np.asarray(inputs["L0_ffn_Wscale_dp"], f32).reshape(HC, 128).T)
    h128 = _hadamard(128)
    ident = np.ascontiguousarray(np.eye(128, dtype=f32))
    trimask = np.ascontiguousarray(np.triu(np.ones((128, 128), f32)))

    shared = dict(wqT=wqT, wkT=wkT, wvT=wvT, woT=woT, mz=mz, gate1=gate1,
                  su_ug=su_ug, su_dp=su_dp, wsu=wsu, wsg=wsg, wsdp=wsdp,
                  h128=h128, ident=ident, trimask=trimask)
    in_maps = []
    for c in range(NCORES):
        pm = np.zeros((128, NCORES), f32)
        for k in range(NCORES):
            if k // 4 == c // 4 and k < c:
                pm[:, k] = 1.0
        m = dict(shared)
        m["x"] = np.ascontiguousarray(x_t[:, c * NT:(c + 1) * NT])
        m["wu_slice"] = np.ascontiguousarray(wuT[:, c * ISL:(c + 1) * ISL])
        m["wg_slice"] = np.ascontiguousarray(wgT[:, c * ISL:(c + 1) * ISL])
        m["wd_slice"] = np.ascontiguousarray(wdT[:, c * HSL:(c + 1) * HSL])
        m["pmask"] = pm
        in_maps.append(m)
    return in_maps


def run(inputs, taps=()):
    in_maps = prepare_inputs(inputs)
    nc = build_program(taps=taps)
    return run_bass_kernel_spmd(nc, in_maps, core_ids=list(range(NCORES)))


def kernel(**inputs):
    res = run(inputs)
    outs = [r["out"] for r in res.results]
    full = np.concatenate(outs, axis=1)
    return np.ascontiguousarray(full.T).reshape(2, NCORES * NT // 2, H)


# version 12
# speedup vs baseline: 1.0098x; 1.0098x over previous
import math
import numpy as np
from contextlib import ExitStack

import concourse.bass as bass
import concourse.mybir as mybir
from concourse.bass_utils import run_bass_kernel_spmd

F32 = mybir.dt.float32
BF16 = mybir.dt.bfloat16
AF = mybir.ActivationFunctionType
ALU = mybir.AluOpType

NCORES = 8
WARM = True
H = 2048
HC = 16
INTER = 8192
IC = 64
NT = 2048
NB = 512
NBLK = NT // NB
CH = 128
NCH = NT // CH
CPB = NB // CH
ISL = INTER // NCORES
HSL = H // NCORES
RS2 = 1.0 / math.sqrt(2048.0)
RS8 = 1.0 / math.sqrt(8192.0)
EPS_LN = 1e-5
EPS_AT = 1e-6
ST = 132


def _hadamard(n):
    Hm = np.array([[1.0]], dtype=np.float32)
    while Hm.shape[0] < n:
        Hm = np.block([[Hm, Hm], [Hm, -Hm]])
    return np.ascontiguousarray(Hm.astype(np.float32))


_H16 = _hadamard(16)


class Ctr:

    def __init__(self, stack, nc, name):
        self.sem = stack.enter_context(nc.semaphore(name))
        self.val = 0

    def reserve(self, n=1):
        self.val += n
        return self.val

    def at(self):
        return self.val


class DQ:

    def __init__(self, stack, nc, name, lanes=8):
        self.sems = [stack.enter_context(nc.semaphore("%s%d" % (name, i)))
                     for i in range(lanes)]
        self.vals = [0] * lanes
        self.i = 0

    def slot(self):
        L = self.i % len(self.sems)
        self.i += 1
        pre = self.vals[L]
        self.vals[L] += 16
        return (self.sems[L], pre, self.vals[L])

    def barrier(self):
        return [(s, v) for s, v in zip(self.sems, self.vals) if v > 0]


def _dma(e, q3, out, in_):
    sem, pre, _ = q3
    if pre:
        e.wait_ge(sem, pre)
    e.dma_start(out=out, in_=in_).then_inc(sem, 16)


def _waitall(e, pairs):
    for s, v in pairs:
        e.wait_ge(s, v)


_STAGES = {"ld": -3, "vec": -2, "g": -1, "init": 0, "foldu": 1, "foldg": 2, "foldd": 3, "fold": 4, "ffn1b0": 5, "phase1": 6, "phase2": 7, "full": 8}


def build_program(taps=(), stage="full"):
    nc = bass.Bass("TRN2", target_bir_lowering=False, debug=False)
    taps = set(taps)
    lvl = _STAGES[stage]

    din = {}
    for name, shape in [
        ("x", [H, NT]),
        ("wqT", [H, 128]), ("wkT", [H, 128]), ("wvT", [H, 128]),
        ("woT", [128, H]),
        ("mz", [128, 129]),
        ("gate1", [1, 1]),
        ("su_ug", [128, HC]),
        ("wsu", [128, IC]), ("wsg", [128, IC]),
        ("wsdp", [128, HC]),
        ("su_dp", [128, IC]),
        ("wu_slice", [H, ISL]), ("wg_slice", [H, ISL]),
        ("wd_slice", [INTER, HSL]),
        ("h128", [128, 128]),
        ("ident", [128, 128]),
        ("trimask", [128, 128]),
        ("pmask", [128, NCORES]),
    ]:
        din[name] = nc.dram_tensor(name, shape, F32, kind="ExternalInput").ap()

    out = nc.dram_tensor("out", [H, NT], F32, kind="ExternalOutput").ap()
    tap_aps = {}
    if "x1" in taps:
        tap_aps["x1"] = nc.dram_tensor("tap_x1", [H, NT], F32, kind="ExternalOutput").ap()
    if "ds" in taps:
        tap_aps["ds"] = nc.dram_tensor("tap_ds", [128, NCH * ST], F32, kind="ExternalOutput").ap()
    if "pq" in taps:
        tap_aps["pq"] = nc.dram_tensor("tap_pq", [128, NT], BF16, kind="ExternalOutput").ap()
    if "sfull" in taps:
        tap_aps["sfull"] = nc.dram_tensor("tap_sfull", [128, NCH * ST], BF16, kind="ExternalOutput").ap()
    if "agu" in taps:
        tap_aps["agu"] = nc.dram_tensor("tap_agu", [128, HC * 128], BF16, kind="ExternalOutput").ap()
        tap_aps["agd"] = nc.dram_tensor("tap_agd", [128, IC * 128], BF16, kind="ExternalOutput").ap()

    x1s = nc.dram_tensor("x1s", [H, NT], F32, kind="Internal").ap()
    agi_u = nc.dram_tensor("agi_u", [ISL // 128, 128, HC, 128], BF16, kind="Internal")
    ago_u = nc.dram_tensor("ago_u", [IC, 128, HC, 128], BF16, kind="Internal", addr_space="Shared")
    agi_g = nc.dram_tensor("agi_g", [ISL // 128, 128, HC, 128], BF16, kind="Internal")
    ago_g = nc.dram_tensor("ago_g", [IC, 128, HC, 128], BF16, kind="Internal", addr_space="Shared")
    agi_d = nc.dram_tensor("agi_d", [HSL // 128, 128, IC, 128], BF16, kind="Internal")
    ago_d = nc.dram_tensor("ago_d", [HC, 128, IC, 128], BF16, kind="Internal", addr_space="Shared")
    wrm_i = nc.dram_tensor("wrm_i", [1, 128], F32, kind="Internal")
    wrm_o = nc.dram_tensor("wrm_o", [NCORES, 128], F32, kind="Internal", addr_space="Shared")
    cc_si = nc.dram_tensor("cc_si", [128, 129], F32, kind="Internal")
    cc_so = nc.dram_tensor("cc_so", [NCORES * 128, 129], F32, kind="Internal", addr_space="Shared")

    st = ExitStack()
    sb = lambda n, s, d: st.enter_context(nc.sbuf_tensor(n, s, d))
    psal = lambda n, s, d=F32: st.enter_context(nc.psum_tensor(n, s, d))

    prog = []

    def E(key, fn):
        prog.append((key, fn))

    with st:
        T = Ctr(st, nc, "T")
        V = Ctr(st, nc, "V")
        S = Ctr(st, nc, "S")
        G = Ctr(st, nc, "G")
        DSY = DQ(st, nc, "DSY", 8)
        DGP = DQ(st, nc, "DGP", 8)
        CC = Ctr(st, nc, "CC")

        h128b = sb("h128b", [128, 128], BF16)
        identb = sb("identb", [128, 128], BF16)
        trimaskf = sb("trimaskf", [128, 128], F32)
        mzb = sb("mzb", [128, 129], BF16)
        wqb = sb("wqb", [128, HC * 128], BF16)
        wkb = sb("wkb", [128, HC * 128], BF16)
        wvb = sb("wvb", [128, HC * 128], BF16)
        wob = sb("wob", [128, HC * 128], BF16)
        suugb = sb("suugb", [128, HC], F32)
        scub = sb("scub", [128, IC], F32)
        scgb = sb("scgb", [128, IC], F32)
        sdpb = sb("sdpb", [128, HC], F32)
        sudpb = sb("sudpb", [128, IC], F32)
        wrmb = sb("wrmb", [1, 128], F32)
        pmaskb = sb("pmaskb", [128, NCORES], F32)
        gate_sb = sb("gate_sb", [1, 1], F32)
        grow = sb("grow", [1, 1], F32)
        g128 = sb("g128", [128, 1], F32)
        g1m = sb("g1m", [128, 1], F32)
        ones_c = sb("ones_c", [128, 1], F32)
        ones_cb = sb("ones_cb", [128, 1], BF16)
        ones_r = sb("ones_r", [1, 128], F32)
        meanrow = sb("meanrow", [1, NB], F32)
        varrow = sb("varrow", [1, NB], F32)
        rstdrow = sb("rstdrow", [1, NB], F32)

        PS = [psal("ps%d" % i, [128, 512]) for i in range(6)]
        PBT = [psal("pbt%d" % i, [128, 128], BF16) for i in range(2)]

        xb = sb("xb", [128, HC * NB], F32)
        xlb = sb("xlb", [128, HC * NB], BF16)
        xsq = sb("xsq", [128, 2 * NB], F32)
        ups = sb("ups", [128, 2 * NB], BF16)
        gsil = sb("gsil", [128, 2 * NB], BF16)
        wub = sb("wub", [128, 2 * HC * 128], BF16)
        wgb = sb("wgb", [128, 2 * HC * 128], BF16)

        _slots_init = [DGP.slot() for _ in range(15)]

        def init_small(e, sl=_slots_init):
            it = iter(sl)
            _dma(e, next(it), h128b[:, :], din["h128"])
            _dma(e, next(it), identb[:, :], din["ident"])
            _dma(e, next(it), mzb[:, :], din["mz"])
            for wsb_, nm in ((wqb, "wqT"), (wkb, "wkT"), (wvb, "wvT")):
                _dma(e, next(it),
                     wsb_[:, :].rearrange("p (c d) -> p c d", d=128),
                     din[nm].rearrange("(c p) d -> p c d", p=128))
            _dma(e, next(it), wob[:, :], din["woT"])
            for dst, nm in (
                (trimaskf, "trimask"), (suugb, "su_ug"), (scub, "wsu"),
                (scgb, "wsg"), (sdpb, "wsdp"), (sudpb, "su_dp"),
                (pmaskb, "pmask"), (gate_sb, "gate1"),
            ):
                _dma(e, next(it), dst[:, :], din[nm])

        E("gp", init_small)
        ld_small = DGP.barrier()

        if lvl <= -3:
            def fin0(e, w=tuple(DGP.barrier())):
                _waitall(e, w)
            E("gp", fin0)
            return _emit(nc, prog)

        def init_vec(e):
            _waitall(e, ld_small)
            e.memset(ones_c[:, :], 1.0)
            e.memset(ones_cb[:, :], 1.0)
            e.memset(ones_r[:, :], 1.0)
            e.tensor_scalar_mul(scub[:, :], scub[:, :], RS2)
            e.tensor_scalar_mul(scgb[:, :], scgb[:, :], RS2)
            e.tensor_scalar_mul(sdpb[:, :], sdpb[:, :], 0.5 * RS8)
            e.drain()
            e.tensor_mul(scub[:, :], scub[:, :], sudpb[:, :])
            e.drain().then_inc(V.sem, 1)

        E("v", init_vec)
        v_init = V.reserve(1)

        if lvl <= -2:
            def fin1(e, w=v_init):
                e.wait_ge(V.sem, w)
            E("v", fin1)
            def fin0(e, w=tuple(DGP.barrier())):
                _waitall(e, w)
            E("gp", fin0)
            return _emit(nc, prog)

        def init_g_s(e, w=v_init):
            e.wait_ge(V.sem, w)
            e.activation(out=grow[:, :], in_=gate_sb[:, :], func=AF.Exp,
                         scale=-1.0).then_inc(S.sem, 1)

        E("s", init_g_s)
        s_g = S.reserve(1)

        def init_g_v(e, w=s_g):
            e.wait_ge(S.sem, w)
            e.tensor_scalar_add(grow[:, :], grow[:, :], 1.0)
            e.drain()
            e.reciprocal(grow[:, :], grow[:, :]).then_inc(V.sem, 1)

        E("v", init_g_v)
        v_g = V.reserve(1)

        def init_g_t(e, w=v_g):
            e.wait_ge(V.sem, w)
            e.matmul(PS[4][:, 0:1], ones_r[:, :], grow[:, :],
                     start=True, stop=True).then_inc(T.sem, 1)

        E("t", init_g_t)
        t_g = T.reserve(1)

        def init_g_v2(e, w=t_g):
            e.wait_ge(T.sem, w)
            e.tensor_copy(g128[:, :], PS[4][:, 0:1])
            e.drain()
            e.tensor_scalar(g1m[:, :], g128[:, :], -1.0, 1.0,
                            ALU.mult, ALU.add).then_inc(V.sem, 1)

        E("v", init_g_v2)
        v_gdone = V.reserve(1)

        if lvl <= -1:
            def fin1(e, w=v_gdone):
                e.wait_ge(V.sem, w)
            E("v", fin1)
            def fin0(e, w=tuple(DGP.barrier())):
                _waitall(e, w)
            E("gp", fin0)
            return _emit(nc, prog)

        trk = {
            "up_mm": [], "gate_mm": [],
            "up_ev": [], "gate_ev": [], "y_done": [],
            "dn_mm_half": [], "dn_mm": [], "dn_ev": [],
            "wu_dma": [], "wg_dma": [],
            "ln_norm": {},
            "ln_base": {},
            "last_xlb_reader_T": 0,
            "spill": [None] * NBLK,
            "xload": [None] * NBLK,
            "x1_load": [None] * NBLK,
            "outstore": [None] * NBLK,
            "ds_ev_last": [None] * NBLK,
            "ffn2_dn": [None] * NBLK,
        }

        def emit_ln(bmean, brstd, bsum, bsq, su_fuse, xlb_ready_key,
                    wait_T_before_write):
            wT = wait_T_before_write

            def vcast(e, wT=wT):
                if wT:
                    e.wait_ge(T.sem, wT)
                e.drain()
                for c in range(HC):
                    e.tensor_copy(xlb[:, c * NB:(c + 1) * NB],
                                  xb[:, c * NB:(c + 1) * NB])
                e.drain().then_inc(V.sem, 1)

            E("v", vcast)
            v_cast = V.reserve(1)

            def tsum(e, w=v_cast):
                e.wait_ge(V.sem, w)
                for c in range(HC):
                    e.matmul(PS[bsum][0:1, 0:NB], ones_cb[:, :],
                             xlb[:, c * NB:(c + 1) * NB],
                             start=(c == 0), stop=(c == HC - 1))
                e.nop().then_inc(T.sem, 1)

            E("t", tsum)
            T.reserve(1)

            sq_v = []
            sqmm_t = []
            for c in range(HC):
                par = c % 2

                def vsq(e, c=c, par=par):
                    if c >= 2:
                        e.wait_ge(T.sem, sqmm_t[c - 2])
                    e.tensor_mul(ups[:, par * NB:(par + 1) * NB],
                                 xlb[:, c * NB:(c + 1) * NB],
                                 xlb[:, c * NB:(c + 1) * NB]).then_inc(V.sem, 1)

                E("v", vsq)
                sq_v.append(V.reserve(1))

                def tsqc(e, c=c, par=par, w=sq_v[-1]):
                    e.wait_ge(V.sem, w)
                    e.matmul(PS[bsq][0:1, 0:NB], ones_cb[:, :],
                             ups[:, par * NB:(par + 1) * NB],
                             start=(c == 0), stop=(c == HC - 1),
                             ).then_inc(T.sem, 1)

                E("t", tsqc)
                sqmm_t.append(T.reserve(1))

            t_sq = sqmm_t[-1]

            def vrows(e, w=t_sq):
                e.wait_ge(T.sem, w)
                e.tensor_scalar_mul(meanrow[0:1, :], PS[bsum][0:1, 0:NB], 1.0 / H)
                e.drain()
                e.tensor_mul(varrow[0:1, :], meanrow[0:1, :], meanrow[0:1, :])
                e.drain()
                e.scalar_tensor_tensor(varrow[0:1, :], PS[bsq][0:1, 0:NB], 1.0 / H,
                                       varrow[0:1, :], ALU.mult, ALU.subtract)
                e.drain()
                e.tensor_scalar_add(varrow[0:1, :], varrow[0:1, :],
                                    EPS_LN).then_inc(V.sem, 1)

            E("v", vrows)
            v_rows = V.reserve(1)

            def srows(e, w=v_rows):
                e.wait_ge(V.sem, w)
                e.activation(out=rstdrow[0:1, :], in_=varrow[0:1, :], func=AF.Ln)
                e.drain()
                e.activation(out=rstdrow[0:1, :], in_=rstdrow[0:1, :], func=AF.Exp,
                             scale=-0.5).then_inc(S.sem, 1)

            E("s", srows)
            s_rows = S.reserve(1)

            def tbcast(e, w=s_rows):
                e.wait_ge(S.sem, w)
                e.matmul(PS[bmean][:, 0:NB], ones_r[:, :], meanrow[0:1, :],
                         start=True, stop=True)
                e.matmul(PS[brstd][:, 0:NB], ones_r[:, :], rstdrow[0:1, :],
                         start=True, stop=True).then_inc(T.sem, 1)

            E("t", tbcast)
            t_bcast = T.reserve(1)

            def vnorm(e, w=t_bcast, su=su_fuse):
                e.wait_ge(T.sem, w)
                for c in range(HC):
                    par = c % 2
                    e.tensor_sub(xsq[:, par * NB:(par + 1) * NB],
                                 xb[:, c * NB:(c + 1) * NB], PS[bmean][:, 0:NB])
                    e.drain()
                    if su:
                        e.scalar_tensor_tensor(
                            xlb[:, c * NB:(c + 1) * NB],
                            xsq[:, par * NB:(par + 1) * NB],
                            suugb[:, c:c + 1], PS[brstd][:, 0:NB],
                            ALU.mult, ALU.mult).then_inc(V.sem, 1)
                    else:
                        e.tensor_mul(xlb[:, c * NB:(c + 1) * NB],
                                     xsq[:, par * NB:(par + 1) * NB],
                                     PS[brstd][:, 0:NB]).then_inc(V.sem, 1)
                e.drain()

            E("v", vnorm)
            v_norm = V.reserve(HC)
            trk["ln_norm"][xlb_ready_key] = v_norm
            trk["ln_base"][xlb_ready_key] = v_norm - HC
            return v_norm

        def emit_ffn(v_xlb_ready, xlb_base=None):
            for i in range(IC):
                gi = len(trk["up_mm"])
                par = i % 2
                _su = DSY.slot()

                def wdma(e, m=i, par=par, gi=gi, sl=_su):
                    if gi == 0:
                        e.wait_ge(CC.sem, cc_u)
                    if gi >= 2:
                        e.wait_ge(T.sem, trk["up_mm"][gi - 2])
                    _dma(e, sl,
                         wub[:, par * 2048:(par + 1) * 2048]
                         .rearrange("p (c i) -> p c i", i=128),
                         ago_u.ap()[m])

                E("sy", wdma)
                trk["wu_dma"].append((_su[0], _su[2]))

                def upmm(e, i=i, par=par, gi=gi, v_xlb=v_xlb_ready,
                         xlb_base=xlb_base, wd=trk["wu_dma"][-1]):
                    if i == 0 and xlb_base is None:
                        e.wait_ge(V.sem, v_xlb)
                    e.wait_ge(wd[0], wd[1])
                    if gi >= 2:
                        e.wait_ge(S.sem, trk["up_ev"][gi - 2])
                    for c in range(HC):
                        if i == 0 and xlb_base is not None:
                            e.wait_ge(V.sem, xlb_base + c + 1)
                        inst = e.matmul(PS[0 + par][:, 0:NB],
                                        wub[:, par * 2048 + c * 128:
                                             par * 2048 + (c + 1) * 128],
                                        xlb[:, c * NB:(c + 1) * NB],
                                        start=(c == 0), stop=(c == HC - 1))
                    inst.then_inc(T.sem, 1)

                E("t", upmm)
                trk["up_mm"].append(T.reserve(1))

                def upev(e, i=i, par=par, gi=gi, w=trk["up_mm"][-1]):
                    e.wait_ge(T.sem, w)
                    if gi == 0:
                        _waitall(e, tuple(g_dmas)[-8:])
                    e.activation(out=yb[:, i * NB:(i + 1) * NB],
                                 in_=PS[0 + par][:, 0:NB], func=AF.Identity,
                                 scale=scub[:, i:i + 1]).then_inc(S.sem, 1)

                E("s", upev)
                trk["up_ev"].append(S.reserve(1))

            for i in range(IC):
                gi = len(trk["gate_mm"])
                par = i % 2
                _sg = DSY.slot()

                def wdma_g(e, m=i, par=par, gi=gi, sl=_sg):
                    if gi == 0:
                        e.wait_ge(CC.sem, cc_g)
                    if gi >= 2:
                        e.wait_ge(T.sem, trk["gate_mm"][gi - 2])
                    _dma(e, sl,
                         wgb[:, par * 2048:(par + 1) * 2048]
                         .rearrange("p (c i) -> p c i", i=128),
                         ago_g.ap()[m])

                E("sy", wdma_g)
                trk["wg_dma"].append((_sg[0], _sg[2]))

                def gmm(e, par=par, gi=gi, wd=trk["wg_dma"][-1]):
                    if gi == 0:
                        e.wait_ge(S.sem, s1ev_last_v)
                    e.wait_ge(wd[0], wd[1])
                    if gi >= 2:
                        e.wait_ge(S.sem, trk["gate_ev"][gi - 2])
                    for c in range(HC):
                        inst = e.matmul(PS[2 + par][:, 0:NB],
                                        wgb[:, par * 2048 + c * 128:
                                             par * 2048 + (c + 1) * 128],
                                        xlb[:, c * NB:(c + 1) * NB],
                                        start=(c == 0), stop=(c == HC - 1))
                    inst.then_inc(T.sem, 1)

                E("t", gmm)
                trk["gate_mm"].append(T.reserve(1))

                def gev(e, i=i, par=par, gi=gi, w=trk["gate_mm"][-1]):
                    e.wait_ge(T.sem, w)
                    if gi >= 2:
                        e.wait_ge(V.sem, trk["y_done"][gi - 2])
                    e.activation(out=gsil[:, par * NB:(par + 1) * NB],
                                 in_=PS[2 + par][:, 0:NB], func=AF.Silu,
                                 scale=scgb[:, i:i + 1]).then_inc(S.sem, 1)

                E("s", gev)
                trk["gate_ev"].append(S.reserve(1))

                def yprod(e, i=i, par=par, gi=gi, w=trk["gate_ev"][-1],
                          wu=trk["up_ev"][-(IC - i)]):
                    e.wait_ge(S.sem, w)
                    e.tensor_mul(yb[:, i * NB:(i + 1) * NB],
                                 gsil[:, par * NB:(par + 1) * NB],
                                 yb[:, i * NB:(i + 1) * NB]).then_inc(V.sem, 1)

                E("v", yprod)
                trk["y_done"].append(V.reserve(1))

            trk["last_xlb_reader_T"] = trk["gate_mm"][-1]
            y_all = trk["y_done"][-1]

            dn_last = None
            for h in range(HC):
                gh = len(trk["dn_mm"])
                par = h % 2
                halves = []
                for half in range(2):
                    _sd = DSY.slot()

                    def wd_dma(e, h=h, half=half, gh=gh, sl=_sd):
                        if gh == 0 and half == 0:
                            e.wait_ge(CC.sem, cc_d)
                        if gh >= 1:
                            if half == 0:
                                e.wait_ge(T.sem, trk["dn_mm_half"][gh - 1])
                            else:
                                e.wait_ge(T.sem, trk["dn_mm"][gh - 1])
                        _dma(e, sl,
                             wdb[:, half * 32 * 128:(half + 1) * 32 * 128]
                             .rearrange("p (ic hh) -> p ic hh", hh=128),
                             ago_d.ap()[h, :, half * 32:(half + 1) * 32, :])

                    E("sy", wd_dma)
                    halves.append((_sd[0], _sd[2]))

                def dnmm(e, h=h, par=par, gh=gh, halves=tuple(halves),
                         y_all=y_all):
                    if h == 0:
                        e.wait_ge(V.sem, y_all)
                    if gh >= 2:
                        e.wait_ge(V.sem, trk["dn_ev"][gh - 2])
                    e.wait_ge(halves[0][0], halves[0][1])
                    for ic in range(32):
                        inst = e.matmul(PS[4 + par][:, 0:NB],
                                        wdb[:, ic * 128:(ic + 1) * 128],
                                        yb[:, ic * NB:(ic + 1) * NB],
                                        start=(ic == 0), stop=False)
                    inst.then_inc(T.sem, 1)
                    e.wait_ge(halves[1][0], halves[1][1])
                    for ic in range(32, 64):
                        inst = e.matmul(PS[4 + par][:, 0:NB],
                                        wdb[:, ic * 128:(ic + 1) * 128],
                                        yb[:, ic * NB:(ic + 1) * NB],
                                        start=False, stop=(ic == 63))
                    inst.then_inc(T.sem, 1)

                E("t", dnmm)
                trk["dn_mm_half"].append(T.reserve(1))
                trk["dn_mm"].append(T.reserve(1))

                def dnev(e, h=h, par=par, w=trk["dn_mm"][-1]):
                    e.wait_ge(T.sem, w)
                    e.scalar_tensor_tensor(xb[:, h * NB:(h + 1) * NB],
                                           PS[4 + par][:, 0:NB],
                                           sdpb[:, h:h + 1],
                                           xb[:, h * NB:(h + 1) * NB],
                                           ALU.mult, ALU.add).then_inc(V.sem, 1)

                E("v", dnev)
                trk["dn_ev"].append(V.reserve(1))
                dn_last = trk["dn_ev"][-1]
            return dn_last

        fold_stack = ExitStack()
        fdin = fold_stack.enter_context(nc.sbuf_tensor("fdin", [128, IC * HSL], BF16))
        fdsc = fold_stack.enter_context(nc.sbuf_tensor("fdsc", [128, 4 * HSL], BF16))
        fdout = fold_stack.enter_context(nc.sbuf_tensor("fdout", [128, HSL], BF16))
        fwin = fold_stack.enter_context(nc.sbuf_tensor("fwin", [128, HC * ISL], BF16))
        fwsc = fold_stack.enter_context(nc.sbuf_tensor("fwsc", [128, ISL], BF16))

        cc_u = cc_g = cc_d = 0
        u_dmas = g_dmas = ()
        do_ug = lvl >= 1 and lvl != 3

        def warm_v(e):
            e.memset(wrmb[:, :], 1.0)
            e.drain().then_inc(V.sem, 1)

        E("v", warm_v)
        v_warm = V.reserve(1)
        _slw = DGP.slot()

        def ag_warm(e, wv=v_warm, sl=_slw):
            e.wait_ge(V.sem, wv)
            _dma(e, sl, wrm_i.ap(), wrmb[:, :])
            e.wait_ge(sl[0], sl[2])
            e.collective_compute(
                "AllGather", ALU.bypass,
                replica_groups=[list(range(NCORES))],
                ins=[wrm_i.ap().opt()],
                outs=[wrm_o.ap().opt()],
            ).then_inc(CC.sem, 1)

        if WARM:
            E("gp", ag_warm)
            cc_warm = CC.reserve(1)
        else:
            cc_warm = 0

        if do_ug:
            ld_u_done = []
            for uh in range(2):
                _slu = DGP.slot()

                def ldslice_u(e, uh=uh, sl=_slu):
                    _dma(e, sl,
                         fwin[:, uh * 8 * ISL:(uh + 1) * 8 * ISL]
                         .rearrange("p (c i) -> p c i", i=ISL),
                         din["wu_slice"][uh * 1024:(uh + 1) * 1024, :]
                         .rearrange("(c p) i -> p c i", p=128))

                E("gp", ldslice_u)
                ld_u_done.append((_slu[0], _slu[2]))

        _slwd = DGP.slot()

        def ld_wd(e, sl=_slwd):
            _dma(e, sl,
                 fdin[:, :].rearrange("p (c h) -> p c h", h=HSL),
                 din["wd_slice"].rearrange("(c p) h -> p c h", p=128))

        E("gp", ld_wd)
        ld_wd_done = (_slwd[0], _slwd[2])

        _sx0 = DGP.slot()

        def xload0(e, sl=_sx0):
            _dma(e, sl,
                 xb[:, :].rearrange("p (c n) -> p c n", n=NB),
                 din["x"][:, 0:NB].rearrange("(c p) n -> p c n", p=128))

        E("gp", xload0)
        trk["xload"][0] = (_sx0[0], _sx0[2])

        if do_ug:
            ev_vals = []
            t_vals = []
            for g in range(2 * HC):
                c, half = g // 2, g % 2
                par = g % 2

                def sA(e, c=c, half=half, par=par, g=g, w=tuple(ld_u_done)):
                    if g == 0:
                        e.wait_ge(w[0][0], w[0][1])
                        _waitall(e, ld_small[:1])
                    if g == 16:
                        e.wait_ge(w[1][0], w[1][1])
                    if g >= 2:
                        e.wait_ge(V.sem, ev_vals[g - 2])
                    e.matmul(PS[par][:, 0:512], h128b[:, :],
                             fwin[:, c * ISL + half * 512: c * ISL + (half + 1) * 512],
                             start=True, stop=True).then_inc(T.sem, 1)

                E("t", sA)
                t_vals.append(T.reserve(1))

                def sAev(e, c=c, half=half, par=par, tv=t_vals[-1]):
                    e.wait_ge(T.sem, tv)
                    e.tensor_copy(fwin[:, c * ISL + half * 512:
                                       c * ISL + (half + 1) * 512],
                                  PS[par][:, 0:512]).then_inc(V.sem, 1)

                E("v", sAev)
                ev_vals.append(V.reserve(1))

            t_sA_u_last = t_vals[-1]

            def ubfly(e, w=ev_vals[-1]):
                e.drain()
                for bit in range(4):
                    step = 1 << bit
                    pairs = [j for j in range(HC) if not (j & step)]
                    for base in range(0, len(pairs), 2):
                        grp = pairs[base:base + 2]
                        for q, j in enumerate(grp):
                            sc = fwsc if q == 0 else fdsc
                            e.tensor_sub(sc[:, 0:ISL],
                                         fwin[:, j * ISL:(j + 1) * ISL],
                                         fwin[:, (j + step) * ISL:(j + step + 1) * ISL])
                        e.drain()
                        for q, j in enumerate(grp):
                            e.tensor_add(fwin[:, j * ISL:(j + 1) * ISL],
                                         fwin[:, j * ISL:(j + 1) * ISL],
                                         fwin[:, (j + step) * ISL:(j + step + 1) * ISL])
                        e.drain()
                        for q, j in enumerate(grp):
                            sc = fwsc if q == 0 else fdsc
                            e.tensor_copy(fwin[:, (j + step) * ISL:(j + step + 1) * ISL],
                                          sc[:, 0:ISL])
                        e.drain()
                e.drain().then_inc(V.sem, 1)

            E("v", ubfly)
            v_bfly = V.reserve(1)

            dma_vals = []
            for g in range(2 * HC):
                c, ihalf = g // 2, g % 2
                _s2 = DGP.slot()

                def dmf(e, c=c, ihalf=ihalf, g=g, sl=_s2, wb=v_bfly):
                    if g == 0:
                        e.wait_ge(V.sem, wb)
                    _dma(e, sl,
                         agi_u.ap()[ihalf * 4:(ihalf + 1) * 4, :, c, :]
                         .rearrange("li p i -> p li i"),
                         fwin[:, c * ISL + ihalf * 512: c * ISL + (ihalf + 1) * 512]
                         .rearrange("p (li i) -> p li i", i=128))

                E("gp", dmf)
                dma_vals.append((_s2[0], _s2[2]))
            u_dmas = dma_vals

            def ag_u_f(e, w=tuple(u_dmas)):
                if WARM:
                    e.wait_ge(CC.sem, cc_warm)
                _waitall(e, w)
                e.collective_compute(
                    "AllGather", ALU.bypass,
                    replica_groups=[list(range(NCORES))],
                    ins=[agi_u.ap().opt()],
                    outs=[ago_u.ap().opt()],
                ).then_inc(CC.sem, 1)

            E("gp", ag_u_f)
            cc_u = CC.reserve(1)

        s1_ev = []
        for c in range(IC):
            par = c % 2

            def s1mm(e, c=c, par=par, w=ld_wd_done,
                     wv=(s1_ev[c - 2] if c >= 2 else None)):
                if c == 0:
                    e.wait_ge(w[0], w[1])
                    _waitall(e, ld_small[:1])
                if wv is not None:
                    e.wait_ge(S.sem, wv)
                e.matmul(PS[2 + par][:, 0:HSL], h128b[:, :],
                         fdin[:, c * HSL:(c + 1) * HSL],
                         start=True, stop=True).then_inc(T.sem, 1)

            E("t", s1mm)
            t_done = T.reserve(1)

            def s1ev(e, c=c, par=par, tv=t_done):
                e.wait_ge(T.sem, tv)
                e.activation(out=fdin[:, c * HSL:(c + 1) * HSL],
                             in_=PS[2 + par][:, 0:HSL],
                             func=AF.Identity).then_inc(S.sem, 1)

            E("s", s1ev)
            s1_ev.append(S.reserve(1))

        s1ev_last_v = s1_ev[-1]

        def wd_bfly_part(bits):
            def part(e, bits=bits, w=s1ev_last_v):
                if bits[0] == 0:
                    e.wait_ge(S.sem, w)
                for bit in bits:
                    step = 1 << bit
                    pairs = [j for j in range(IC) if not (j & step)]
                    for base in range(0, len(pairs), 4):
                        grp = pairs[base:base + 4]
                        for q, j in enumerate(grp):
                            e.tensor_sub(fdsc[:, q * HSL:(q + 1) * HSL],
                                         fdin[:, j * HSL:(j + 1) * HSL],
                                         fdin[:, (j + step) * HSL:(j + step + 1) * HSL])
                        e.drain()
                        for q, j in enumerate(grp):
                            e.tensor_add(fdin[:, j * HSL:(j + 1) * HSL],
                                         fdin[:, j * HSL:(j + 1) * HSL],
                                         fdin[:, (j + step) * HSL:(j + step + 1) * HSL])
                        e.drain()
                        for q, j in enumerate(grp):
                            e.tensor_copy(fdin[:, (j + step) * HSL:(j + step + 1) * HSL],
                                          fdsc[:, q * HSL:(q + 1) * HSL])
                        e.drain()
                e.drain().then_inc(G.sem, 1)

            return part

        if lvl >= 2 and lvl != 3:
            ldg_vals = []
            for c in range(HC):
                _sg2 = DGP.slot()

                def ldslice_g(e, c=c, sl=_sg2,
                              pd=(tuple(u_dmas[2 * c:2 * c + 2])
                                  if u_dmas else ())):
                    _waitall(e, pd)
                    _dma(e, sl,
                         fwin[:, c * ISL:(c + 1) * ISL]
                         .rearrange("p (o i) -> p o i", o=1),
                         din["wg_slice"][c * 128:(c + 1) * 128, :]
                         .rearrange("(o p) i -> p o i", p=128))

                E("gp", ldslice_g)
                ldg_vals.append((_sg2[0], _sg2[2]))

            ev_vals_g = []
            t_vals_g = []
            for g in range(2 * HC):
                c, half = g // 2, g % 2
                par = g % 2

                def sAg(e, c=c, half=half, par=par, g=g, w=ldg_vals[c]):
                    if half == 0:
                        e.wait_ge(w[0], w[1])
                    if g >= 2:
                        e.wait_ge(V.sem, ev_vals_g[g - 2])
                    e.matmul(PS[par][:, 0:512], h128b[:, :],
                             fwin[:, c * ISL + half * 512: c * ISL + (half + 1) * 512],
                             start=True, stop=True).then_inc(T.sem, 1)

                E("t", sAg)
                t_vals_g.append(T.reserve(1))

                def sAevg(e, c=c, half=half, par=par, tv=t_vals_g[-1]):
                    e.wait_ge(T.sem, tv)
                    e.tensor_copy(fwin[:, c * ISL + half * 512:
                                       c * ISL + (half + 1) * 512],
                                  PS[par][:, 0:512]).then_inc(V.sem, 1)

                E("v", sAevg)
                ev_vals_g.append(V.reserve(1))

            t_sA_g_last = t_vals_g[-1]

            def ubfly_g(e, w=ev_vals_g[-1]):
                e.drain()
                for bit in range(4):
                    step = 1 << bit
                    pairs = [j for j in range(HC) if not (j & step)]
                    for base in range(0, len(pairs), 2):
                        grp = pairs[base:base + 2]
                        for q, j in enumerate(grp):
                            sc = fwsc if q == 0 else fdsc
                            e.tensor_sub(sc[:, 0:ISL],
                                         fwin[:, j * ISL:(j + 1) * ISL],
                                         fwin[:, (j + step) * ISL:(j + step + 1) * ISL])
                        e.drain()
                        for q, j in enumerate(grp):
                            e.tensor_add(fwin[:, j * ISL:(j + 1) * ISL],
                                         fwin[:, j * ISL:(j + 1) * ISL],
                                         fwin[:, (j + step) * ISL:(j + step + 1) * ISL])
                        e.drain()
                        for q, j in enumerate(grp):
                            sc = fwsc if q == 0 else fdsc
                            e.tensor_copy(fwin[:, (j + step) * ISL:(j + step + 1) * ISL],
                                          sc[:, 0:ISL])
                        e.drain()
                e.drain().then_inc(V.sem, 1)

            E("v", ubfly_g)
            v_bfly_g = V.reserve(1)

            g_dmas = []
            for g in range(2 * HC):
                c, ihalf = g // 2, g % 2
                _s2 = DGP.slot()

                def dmfg(e, c=c, ihalf=ihalf, g=g, sl=_s2, wb=v_bfly_g):
                    if g == 0:
                        e.wait_ge(V.sem, wb)
                    _dma(e, sl,
                         agi_g.ap()[ihalf * 4:(ihalf + 1) * 4, :, c, :]
                         .rearrange("li p i -> p li i"),
                         fwin[:, c * ISL + ihalf * 512: c * ISL + (ihalf + 1) * 512]
                         .rearrange("p (li i) -> p li i", i=128))

                E("gp", dmfg)
                g_dmas.append((_s2[0], _s2[2]))

            def ag_g_f(e, w=tuple(g_dmas)):
                _waitall(e, w)
                e.collective_compute(
                    "AllGather", ALU.bypass,
                    replica_groups=[list(range(NCORES))],
                    ins=[agi_g.ap().opt()],
                    outs=[ago_g.ap().opt()],
                ).then_inc(CC.sem, 1)

            E("gp", ag_g_f)
            cc_g = CC.reserve(1)
        else:
            t_sA_g_last = 0

        def xwait_t0(e, w=trk["xload"][0]):
            e.wait_ge(w[0], w[1])

        E("t", xwait_t0)

        def xwait_v0(e, w=trk["xload"][0], g=v_gdone):
            e.wait_ge(w[0], w[1])
            e.wait_ge(V.sem, g)

        E("v", xwait_v0)
        v_ln1_blk0 = emit_ln(4, 5, 4, 5, su_fuse=True,
                             xlb_ready_key=(0, "ln1"),
                             wait_T_before_write=0)

        E("v", wd_bfly_part((0, 1, 2)))
        G.reserve(1)
        E("v", wd_bfly_part((3, 4, 5)))
        g_s2_done = G.reserve(1)

        d_dma_vals = []
        for c in range(IC):
            _s3d = DGP.slot()
            d_dma_vals.append((_s3d[0], _s3d[2]))

            def s3d(e, c=c, wb=g_s2_done, sl=_s3d):
                if c == 0:
                    e.wait_ge(G.sem, wb)
                _dma(e, sl,
                     agi_d.ap()[:, :, c, :].rearrange("m p h -> p m h"),
                     fdin[:, c * HSL:(c + 1) * HSL]
                     .rearrange("p (m h) -> p m h", h=128))

            E("gp", s3d)

        g_s3_last = g_s2_done
        wd_out_dmas = tuple(d_dma_vals[-8:])

        all_fold_dma = tuple(DGP.barrier())

        if lvl == 3:
            def finf(e, w=all_fold_dma):
                _waitall(e, w)
            E("gp", finf)
            fold_stack.close()
            return _emit(nc, prog)

        def ag_d_f(e, w=all_fold_dma):
            _waitall(e, w)
            e.collective_compute(
                "AllGather", ALU.bypass,
                replica_groups=[list(range(NCORES))],
                ins=[agi_d.ap().opt()],
                outs=[ago_d.ap().opt()],
            ).then_inc(CC.sem, 1)

        E("gp", ag_d_f)
        cc_d = CC.reserve(1)

        fold_stack.close()

        pqf = sb("pqf", [128, NT], BF16)
        pkf = sb("pkf", [128, NT], BF16)
        vaugb = sb("vaugb", [128, NCH * ST], BF16)
        dS = sb("dS", [128, NCH * ST], F32)
        sfull = sb("sfull", [128, NCH * ST], BF16)
        rl = sb("rl", [128, 1], F32)
        rm = sb("rm", [128, 1], F32)
        grl = sb("grl", [128, 1], F32)
        grm = sb("grm", [128, 1], F32)
        blendb = sb("blendb", [128, 128], BF16)
        blendT = sb("blendT", [128, NB], BF16)
        wdb = sb("wdb", [128, IC * 128], BF16)
        yb = sb("yb", [128, IC * NB], BF16)

        def attn_safe(e, w=g_s3_last, d=wd_out_dmas, gd=tuple(g_dmas)):
            e.wait_ge(G.sem, w)
            _waitall(e, d)
            _waitall(e, gd)
            if t_sA_g_last:
                e.wait_ge(T.sem, t_sA_g_last)
            for j in range(NCH):
                e.memset(vaugb[:, j * ST + 128: j * ST + 129], 1.0)
            e.drain().then_inc(V.sem, 1)

        E("v", attn_safe)
        V.reserve(1)

        n_p1 = 0 if lvl < 5 else (1 if lvl == 5 else NBLK)
        s_elu_q = {}
        s_elu_k = {}
        for blk in range(n_p1):
            if blk == 0:
                v_ln1 = v_ln1_blk0
            else:
                _sx = DGP.slot()

                def xload(e, blk=blk, sl=_sx):
                    sp = trk["spill"][blk - 1]
                    e.wait_ge(sp[0], sp[1])
                    e.wait_ge(V.sem, trk["ln_norm"][(blk - 1, "ln2")])
                    _dma(e, sl,
                         xb[:, :].rearrange("p (c n) -> p c n", n=NB),
                         din["x"][:, blk * NB:(blk + 1) * NB]
                         .rearrange("(c p) n -> p c n", p=128))

                E("gp", xload)
                trk["xload"][blk] = (_sx[0], _sx[2])

                def xwait_t(e, blk=blk):
                    xl = trk["xload"][blk]
                    e.wait_ge(xl[0], xl[1])
                    e.wait_ge(V.sem, trk["ds_ev_last"][blk - 1])

                E("t", xwait_t)

                def xwait_v(e, blk=blk):
                    xl = trk["xload"][blk]
                    e.wait_ge(xl[0], xl[1])

                E("v", xwait_v)

                v_ln1 = emit_ln(2, 3, 4, 5, su_fuse=True,
                                xlb_ready_key=(blk, "ln1"),
                                wait_T_before_write=trk["last_xlb_reader_T"])

            dn_last = emit_ffn(v_ln1, xlb_base=trk["ln_base"][(blk, "ln1")])

            _ssp = DGP.slot()

            def spill(e, blk=blk, w=dn_last, sl=_ssp):
                e.wait_ge(V.sem, w)
                _dma(e, sl,
                     x1s[:, blk * NB:(blk + 1) * NB]
                     .rearrange("(c p) n -> p c n", p=128),
                     xb[:, :].rearrange("p (c n) -> p c n", n=NB))

            E("gp", spill)
            trk["spill"][blk] = (_ssp[0], _ssp[2])

            def xwait_t2(e, w=dn_last):
                e.wait_ge(V.sem, w)

            E("t", xwait_t2)
            v_ln2 = emit_ln(0, 1, 4, 5, su_fuse=False,
                            xlb_ready_key=(blk, "ln2"),
                            wait_T_before_write=trk["last_xlb_reader_T"])

            def qmm(e, w=v_ln2):
                for c in range(HC):
                    e.wait_ge(V.sem, w - HC + c + 1)
                    inst = e.matmul(PS[2][:, 0:NB], wqb[:, c * 128:(c + 1) * 128],
                             xlb[:, c * NB:(c + 1) * NB],
                             start=(c == 0), stop=(c == HC - 1))
                    if c == HC - 1:
                        inst.then_inc(T.sem, 1)

            E("t", qmm)
            t_q = T.reserve(1)

            def kmm(e):
                for c in range(HC):
                    inst = e.matmul(PS[3][:, 0:NB], wkb[:, c * 128:(c + 1) * 128],
                             xlb[:, c * NB:(c + 1) * NB],
                             start=(c == 0), stop=(c == HC - 1))
                    if c == HC - 1:
                        inst.then_inc(T.sem, 1)

            E("t", kmm)
            t_k = T.reserve(1)

            v_qmin = V.at() + 1
            v_qdone = V.at() + 2
            v_kmin = V.at() + 3
            v_kdone = V.at() + 4
            sq_val = S.at() + 1
            sk_val = S.at() + 2

            def elu_v(e, blk=blk, t_q=t_q, t_k=t_k, sq=sq_val, sk=sk_val):
                e.wait_ge(T.sem, t_q)
                e.tensor_scalar_min(xsq[:, 0:NB], PS[2][:, 0:NB],
                                    0.0).then_inc(V.sem, 1)
                e.wait_ge(S.sem, sq)
                e.tensor_scalar_max(xsq[:, NB:2 * NB], PS[2][:, 0:NB], 0.0)
                e.drain()
                e.tensor_add(pqf[:, blk * NB:(blk + 1) * NB],
                             xsq[:, 0:NB], xsq[:, NB:2 * NB]).then_inc(V.sem, 1)
                e.drain()
                e.wait_ge(T.sem, t_k)
                e.tensor_scalar_min(xsq[:, 0:NB], PS[3][:, 0:NB],
                                    0.0).then_inc(V.sem, 1)
                e.wait_ge(S.sem, sk)
                e.tensor_scalar_max(xsq[:, NB:2 * NB], PS[3][:, 0:NB], 0.0)
                e.drain()
                e.tensor_add(pkf[:, blk * NB:(blk + 1) * NB],
                             xsq[:, 0:NB], xsq[:, NB:2 * NB]).then_inc(V.sem, 1)

            def elu_s(e, vq=v_qmin, vk=v_kmin):
                e.wait_ge(V.sem, vq)
                e.activation(out=xsq[:, 0:NB], in_=xsq[:, 0:NB],
                             func=AF.Exp).then_inc(S.sem, 1)
                e.wait_ge(V.sem, vk)
                e.activation(out=xsq[:, 0:NB], in_=xsq[:, 0:NB],
                             func=AF.Exp).then_inc(S.sem, 1)

            E("v", elu_v)
            V.reserve(4)
            E("s", elu_s)
            s_elu_q[blk] = S.reserve(1)
            s_elu_k[blk] = S.reserve(1)
            assert s_elu_q[blk] == sq_val and V.at() == v_kdone
            v_pk_done = v_kdone

            v_vev = []
            t_vv = []
            for jl in range(CPB):
                j = blk * CPB + jl
                par = jl % 2

                def vmm(e, jl=jl, par=par, vn=v_ln2,
                        wv=(v_vev[jl - 2] if jl >= 2 else None)):
                    if jl == 0:
                        e.wait_ge(V.sem, vn)
                    if wv is not None:
                        e.wait_ge(V.sem, wv)
                    for c in range(HC):
                        inst = e.matmul(PS[0 + par][:, 0:128],
                                 xlb[:, c * NB + jl * CH: c * NB + (jl + 1) * CH],
                                 wvb[:, c * 128:(c + 1) * 128],
                                 start=(c == 0), stop=(c == HC - 1))
                        if c == HC - 1:
                            inst.then_inc(T.sem, 1)

                E("t", vmm)
                t_vv.append(T.reserve(1))

                def vev_f(e, j=j, par=par, w=t_vv[-1]):
                    e.wait_ge(T.sem, w)
                    e.tensor_copy(vaugb[:, j * ST:j * ST + 128],
                                  PS[0 + par][:, 0:128]).then_inc(V.sem, 1)

                E("v", vev_f)
                v_vev.append(V.reserve(1))

            v_pkT = []
            t_pkT = []
            t_ds = []
            ds_ev = []
            for jl in range(CPB):
                j = blk * CPB + jl
                par = jl % 2

                def pktr(e, j=j, jl=jl, par=par, vpk=v_pk_done,
                         wv=(v_pkT[jl - 2] if jl >= 2 else None)):
                    if jl == 0:
                        e.wait_ge(V.sem, vpk)
                    if wv is not None:
                        e.wait_ge(V.sem, wv)
                    e.transpose(PBT[par][:, 0:128],
                                pkf[:, j * CH:(j + 1) * CH],
                                identb[:, :]).then_inc(T.sem, 1)

                E("t", pktr)
                t_pkT.append(T.reserve(1))

                def pktev(e, par=par, w=t_pkT[-1]):
                    e.wait_ge(T.sem, w)
                    e.tensor_copy(blendT[:, par * 128:(par + 1) * 128],
                                  PBT[par][:, 0:128]).then_inc(V.sem, 1)

                E("v", pktev)
                v_pkT.append(V.reserve(1))

                def dsmm(e, j=j, par=par, wv=max(v_pkT[-1], v_vev[jl])):
                    e.wait_ge(V.sem, wv)
                    e.matmul(PS[4 + par][:, 0:129],
                             blendT[:, par * 128:(par + 1) * 128],
                             vaugb[:, j * ST:j * ST + 129],
                             start=True, stop=True).then_inc(T.sem, 1)

                E("t", dsmm)
                t_ds.append(T.reserve(1))

                def dsev(e, j=j, par=par, w=t_ds[-1]):
                    e.wait_ge(T.sem, w)
                    e.tensor_copy(dS[:, j * ST:j * ST + 129],
                                  PS[4 + par][:, 0:129]).then_inc(V.sem, 1)

                E("v", dsev)
                ds_ev.append(V.reserve(1))
            trk["ds_ev_last"][blk] = ds_ev[-1]

            def blkprefix(e, blk=blk):
                e.drain()
                for jl in range(CPB):
                    j = blk * CPB + jl
                    if j == 0:
                        continue
                    e.tensor_add(dS[:, j * ST:j * ST + 129],
                                 dS[:, j * ST:j * ST + 129],
                                 dS[:, (j - 1) * ST:(j - 1) * ST + 129])
                    e.drain()
                e.drain().then_inc(V.sem, 1)

            E("v", blkprefix)
            trk["ds_ev_last"][blk] = V.reserve(1)

        v_phase1_done = V.at()
        last_blk = n_p1 - 1

        if lvl >= 7:
            v_prefix = trk["ds_ev_last"][NBLK - 1]

            _ssi = DGP.slot()
            cc_s = CC.at() + 1

            def cc_state(e, w=v_prefix, sl=_ssi, cc_s=cc_s):
                e.wait_ge(V.sem, w)
                _dma(e, sl, cc_si.ap(),
                     dS[:, (NCH - 1) * ST:(NCH - 1) * ST + 129])
                e.wait_ge(sl[0], sl[2])
                e.collective_compute(
                    "AllGather", ALU.bypass,
                    replica_groups=[list(range(NCORES))],
                    ins=[cc_si.ap().opt()],
                    outs=[cc_so.ap().opt()],
                ).then_inc(CC.sem, 1)
                e.wait_ge(CC.sem, cc_s)

            E("gp", cc_state)
            CC.reserve(1)

            v_acc = []
            for k in range(NCORES):
                par = k % 4

                _sk = DGP.slot()

                def ccg_ld(e, k=k, par=par, sl=_sk,
                           wv=None):
                    if k >= 4:
                        e.wait_ge(V.sem, v_acc[k - 4])
                    _dma(e, sl, xsq[:, par * ST:par * ST + 129],
                         cc_so.ap()[k * 128:(k + 1) * 128, :])

                E("gp", ccg_ld)

                def ccg_acc(e, k=k, par=par, w=(_sk[0], _sk[2])):
                    e.wait_ge(w[0], w[1])
                    if k == 0:
                        e.tensor_scalar_mul(xsq[:, 768:897],
                                            xsq[:, par * ST:par * ST + 129],
                                            pmaskb[:, k:k + 1]).then_inc(V.sem, 1)
                    else:
                        e.drain()
                        e.scalar_tensor_tensor(xsq[:, 768:897],
                                               xsq[:, par * ST:par * ST + 129],
                                               pmaskb[:, k:k + 1],
                                               xsq[:, 768:897],
                                               ALU.mult, ALU.add).then_inc(V.sem, 1)

                E("v", ccg_acc)
                v_acc.append(V.reserve(1))

            def mk_sfull(e, w=v_acc[-1]):
                e.wait_ge(V.sem, w)
                e.tensor_copy(sfull[:, 0:129], xsq[:, 768:897])
                for j in range(1, NCH):
                    e.tensor_add(sfull[:, j * ST:j * ST + 129],
                                 dS[:, (j - 1) * ST:(j - 1) * ST + 129],
                                 xsq[:, 768:897])
                e.drain().then_inc(V.sem, 1)

            E("v", mk_sfull)
            v_sfull = V.reserve(1)

        if lvl >= 8:
            for blk in range(NBLK):
                _sx1 = DGP.slot()

                def x1load(e, blk=blk, w=v_phase1_done, sl=_sx1):
                    if blk == 0:
                        e.wait_ge(V.sem, w)
                        sp = trk["spill"][NBLK - 1]
                        e.wait_ge(sp[0], sp[1])
                    else:
                        os_ = trk["outstore"][blk - 1]
                        e.wait_ge(os_[0], os_[1])
                    _dma(e, sl,
                         xb[:, :].rearrange("p (c n) -> p c n", n=NB),
                         x1s[:, blk * NB:(blk + 1) * NB]
                         .rearrange("(c p) n -> p c n", p=128))

                E("gp", x1load)
                trk["x1_load"][blk] = (_sx1[0], _sx1[2])

                v_at_ev = []
                v_blend_ev = []
                for jl in range(CPB):
                    j = blk * CPB + jl
                    pn = jl % 2
                    bnum = 0 if pn == 0 else 2
                    bmem = 1 if pn == 0 else 3
                    bat = 4 + pn

                    def atmm(e, j=j, jl=jl, bat=bat,
                             w=(trk["ds_ev_last"][NBLK - 1] if blk == 0 else None),
                             wv=(v_at_ev[jl - 2] if jl >= 2 else None),
                             wdn=(trk["ffn2_dn"][blk - 1]
                                  if jl < 2 and blk > 0 else None)):
                        if jl == 0 and w is not None:
                            e.wait_ge(V.sem, w)
                        if wdn is not None:
                            e.wait_ge(V.sem, wdn)
                        if wv is not None:
                            e.wait_ge(V.sem, wv)
                        e.matmul(PS[bat][:, 0:128],
                                 pkf[:, j * CH:(j + 1) * CH],
                                 pqf[:, j * CH:(j + 1) * CH],
                                 start=True, stop=True).then_inc(T.sem, 1)

                    E("t", atmm)
                    t_at = T.reserve(1)

                    def atmask(e, bat=bat, w=t_at):
                        e.wait_ge(T.sem, w)
                        e.tensor_mul(blendb[:, :], PS[bat][:, 0:128],
                                     trimaskf[:, :]).then_inc(V.sem, 1)

                    E("v", atmask)
                    v_at_ev.append(V.reserve(1))

                    def nummm(e, j=j, jl=jl, bnum=bnum, bmem=bmem, w=v_at_ev[-1],
                              wv=(v_blend_ev[jl - 2] if jl >= 2 else None)):
                        e.wait_ge(V.sem, w)
                        if wv is not None:
                            e.wait_ge(V.sem, wv)
                        e.matmul(PS[bnum][:, 0:129],
                                 pqf[:, j * CH:(j + 1) * CH],
                                 sfull[:, j * ST:j * ST + 129],
                                 start=True, stop=False)
                        e.matmul(PS[bnum][:, 0:129], blendb[:, :],
                                 vaugb[:, j * ST:j * ST + 129],
                                 start=False, stop=True)
                        e.matmul(PS[bmem][:, 0:129],
                                 pqf[:, j * CH:(j + 1) * CH],
                                 mzb[:, :], start=True, stop=True).then_inc(T.sem, 1)

                    E("t", nummm)
                    t_num = T.reserve(1)

                    def blendv(e, bnum=bnum, bmem=bmem, w=t_num):
                        e.wait_ge(T.sem, w)
                        e.tensor_scalar_add(rl[:, :], PS[bnum][:, 128:129], EPS_AT)
                        e.tensor_scalar_add(rm[:, :], PS[bmem][:, 128:129], EPS_AT)
                        e.drain()
                        e.reciprocal(rl[:, :], rl[:, :])
                        e.drain()
                        e.reciprocal(rm[:, :], rm[:, :])
                        e.drain()
                        e.tensor_mul(grl[:, :], rl[:, :], g1m[:, :])
                        e.tensor_mul(grm[:, :], rm[:, :], g128[:, :])
                        e.drain()
                        e.tensor_scalar_mul(xsq[:, 640:768], PS[bnum][:, 0:128], grl[:, 0:1])
                        e.drain()
                        e.scalar_tensor_tensor(blendb[:, :], PS[bmem][:, 0:128],
                                               grm[:, 0:1], xsq[:, 640:768],
                                               ALU.mult, ALU.add).then_inc(V.sem, 1)

                    E("v", blendv)
                    v_blend = V.reserve(1)

                    def btr(e, jl=jl, w=v_blend):
                        e.wait_ge(V.sem, w)
                        e.transpose(PBT[jl % 2][:, 0:128], blendb[:, :],
                                    identb[:, :]).then_inc(T.sem, 1)

                    E("t", btr)
                    t_btr = T.reserve(1)

                    def btrev(e, jl=jl, w=t_btr):
                        e.wait_ge(T.sem, w)
                        e.tensor_copy(blendT[:, jl * CH:(jl + 1) * CH],
                                      PBT[jl % 2][:, 0:128]).then_inc(V.sem, 1)

                    E("v", btrev)
                    v_blend_ev.append(V.reserve(1))

                v_op_ev = []
                for h in range(HC):
                    par = h % 2

                    def opmm(e, h=h, par=par, wbl=v_blend_ev[-1],
                             x1ld=trk["x1_load"][blk],
                             wv=(v_op_ev[h - 2] if h >= 2 else None)):
                        if h == 0:
                            e.wait_ge(V.sem, wbl)
                            e.wait_ge(x1ld[0], x1ld[1])
                        if wv is not None:
                            e.wait_ge(V.sem, wv)
                        e.matmul(PS[par][:, 0:NB],
                                 wob[:, h * 128:(h + 1) * 128],
                                 blendT[:, 0:NB],
                                 start=True, stop=True).then_inc(T.sem, 1)

                    E("t", opmm)
                    t_op = T.reserve(1)

                    def opev(e, h=h, par=par, w=t_op):
                        e.wait_ge(T.sem, w)
                        e.tensor_add(xb[:, h * NB:(h + 1) * NB],
                                     PS[par][:, 0:NB],
                                     xb[:, h * NB:(h + 1) * NB]).then_inc(V.sem, 1)

                    E("v", opev)
                    v_op_ev.append(V.reserve(1))

                def xwait_t3(e, w=v_op_ev[-1]):
                    e.wait_ge(V.sem, w)

                E("t", xwait_t3)

                def xwait_v3(e, w=v_op_ev[-1]):
                    e.wait_ge(V.sem, w)

                E("v", xwait_v3)
                v_ln3 = emit_ln(2, 3, 4, 5, su_fuse=True,
                                xlb_ready_key=(blk, "ln3"),
                                wait_T_before_write=trk["last_xlb_reader_T"])
                dn3_last = emit_ffn(v_ln3, xlb_base=trk["ln_base"][(blk, "ln3")])
                trk["ffn2_dn"][blk] = dn3_last

                _so = DGP.slot()

                def outstore(e, blk=blk, w=dn3_last, sl=_so):
                    e.wait_ge(V.sem, w)
                    _dma(e, sl,
                         out[:, blk * NB:(blk + 1) * NB]
                         .rearrange("(c p) n -> p c n", p=128),
                         xb[:, :].rearrange("p (c n) -> p c n", n=NB))

                E("gp", outstore)
                trk["outstore"][blk] = (_so[0], _so[2])

        if "x1" in tap_aps:
            _st1 = DGP.slot()

            def tap_x1(e, w=trk["spill"][last_blk], sl=_st1):
                e.wait_ge(w[0], w[1])
                _dma(e, sl, tap_aps["x1"], x1s)

            E("gp", tap_x1)
        if "ds" in tap_aps:
            _st2 = DGP.slot()

            def tap_ds(e, w=v_prefix, sl=_st2):
                e.wait_ge(V.sem, w)
                _dma(e, sl, tap_aps["ds"], dS[:, :])

            E("gp", tap_ds)
        if "pq" in tap_aps:
            _st3 = DGP.slot()

            def tap_pq(e, w=v_phase1_done, sl=_st3):
                e.wait_ge(V.sem, w)
                _dma(e, sl, tap_aps["pq"], pqf[:, :])

            E("gp", tap_pq)
        if "sfull" in tap_aps:
            _st4 = DGP.slot()

            def tap_sf(e, w=v_sfull, sl=_st4):
                e.wait_ge(V.sem, w)
                _dma(e, sl, tap_aps["sfull"], sfull[:, :])

            E("gp", tap_sf)

        if "agu" in tap_aps:
            _st5 = DGP.slot()
            _st6 = DGP.slot()

            def tap_agu(e, w=cc_w, s5=_st5, s6=_st6):
                e.wait_ge(CC.sem, w)
                _dma(e, s5, tap_aps["agu"],
                     ago_u.ap()[7].rearrange("p c i -> p (c i)"))
                _dma(e, s6, tap_aps["agd"],
                     ago_d.ap()[15].rearrange("p ic hh -> p (ic hh)"))

            E("gp", tap_agu)

        def fin(e, w=tuple(DGP.barrier())):
            _waitall(e, w)

        E("gp", fin)

        return _emit(nc, prog)


def _emit(nc, prog):
    with nc.Block() as block:
        @block.sync
        def _(e):
            for k, fn in prog:
                if k == "sy":
                    fn(e)

        @block.gpsimd
        def _(e):
            for k, fn in prog:
                if k == "gp":
                    fn(e)

        @block.tensor
        def _(e):
            for k, fn in prog:
                if k == "t":
                    fn(e)

        @block.vector
        def _(e):
            for k, fn in prog:
                if k == "v":
                    fn(e)

        @block.scalar
        def _(e):
            for k, fn in prog:
                if k == "s":
                    fn(e)

    return nc


def prepare_inputs(inputs):
    f32 = np.float32
    x = np.asarray(inputs["L0_attn_hidden_states"], f32)
    x_t = np.ascontiguousarray(x.reshape(NT * NCORES, H).T)
    wuT = np.ascontiguousarray(np.asarray(inputs["L0_ffn_W_up"], f32).T)
    wgT = np.ascontiguousarray(np.asarray(inputs["L0_ffn_W_gate"], f32).T)
    wdT = np.ascontiguousarray(np.asarray(inputs["L0_ffn_W_down"], f32).T)
    wqT = np.ascontiguousarray(np.asarray(inputs["L0_attn_w_q"], f32).T)
    wkT = np.ascontiguousarray(np.asarray(inputs["L0_attn_w_k"], f32).T)
    wvT = np.ascontiguousarray(np.asarray(inputs["L0_attn_w_v"], f32).T)
    woT = np.ascontiguousarray(np.asarray(inputs["L0_attn_w_o"], f32).T)
    M = np.asarray(inputs["L0_attn_M"], f32)
    z = np.asarray(inputs["L0_attn_z"], f32)
    mz = np.ascontiguousarray(np.concatenate([M, z[:, None]], axis=1))
    gate1 = np.ascontiguousarray(np.asarray(inputs["L0_attn_gate"], f32).reshape(1, 1))
    su_ug = np.ascontiguousarray(
        np.asarray(inputs["L0_ffn_SU_ug"], f32).reshape(HC, 128).T)
    su_dp = np.ascontiguousarray(
        np.asarray(inputs["L0_ffn_SU_dp"], f32).reshape(IC, 128).T)
    wsug = np.asarray(inputs["L0_ffn_Wscale_ug"], f32)
    wsu = np.ascontiguousarray(wsug[:INTER].reshape(IC, 128).T)
    wsg = np.ascontiguousarray(wsug[INTER:].reshape(IC, 128).T)
    wsdp = np.ascontiguousarray(
        np.asarray(inputs["L0_ffn_Wscale_dp"], f32).reshape(HC, 128).T)
    h128 = _hadamard(128)
    ident = np.ascontiguousarray(np.eye(128, dtype=f32))
    trimask = np.ascontiguousarray(np.triu(np.ones((128, 128), f32)))

    shared = dict(wqT=wqT, wkT=wkT, wvT=wvT, woT=woT, mz=mz, gate1=gate1,
                  su_ug=su_ug, su_dp=su_dp, wsu=wsu, wsg=wsg, wsdp=wsdp,
                  h128=h128, ident=ident, trimask=trimask)
    in_maps = []
    for c in range(NCORES):
        pm = np.zeros((128, NCORES), f32)
        for k in range(NCORES):
            if k // 4 == c // 4 and k < c:
                pm[:, k] = 1.0
        m = dict(shared)
        m["x"] = np.ascontiguousarray(x_t[:, c * NT:(c + 1) * NT])
        m["wu_slice"] = np.ascontiguousarray(wuT[:, c * ISL:(c + 1) * ISL])
        m["wg_slice"] = np.ascontiguousarray(wgT[:, c * ISL:(c + 1) * ISL])
        m["wd_slice"] = np.ascontiguousarray(wdT[:, c * HSL:(c + 1) * HSL])
        m["pmask"] = pm
        in_maps.append(m)
    return in_maps


def run(inputs, taps=()):
    in_maps = prepare_inputs(inputs)
    nc = build_program(taps=taps)
    return run_bass_kernel_spmd(nc, in_maps, core_ids=list(range(NCORES)))


def kernel(**inputs):
    res = run(inputs)
    outs = [r["out"] for r in res.results]
    full = np.concatenate(outs, axis=1)
    return np.ascontiguousarray(full.T).reshape(2, NCORES * NT // 2, H)


# version 13 (final)
# speedup vs baseline: 1.0103x; 1.0006x over previous
import math
import numpy as np
from contextlib import ExitStack

import concourse.bass as bass
import concourse.mybir as mybir
from concourse.bass_utils import run_bass_kernel_spmd

F32 = mybir.dt.float32
BF16 = mybir.dt.bfloat16
AF = mybir.ActivationFunctionType
ALU = mybir.AluOpType

NCORES = 8
WARM = True
H = 2048
HC = 16
INTER = 8192
IC = 64
NT = 2048
NB = 512
NBLK = NT // NB
CH = 128
NCH = NT // CH
CPB = NB // CH
ISL = INTER // NCORES
HSL = H // NCORES
RS2 = 1.0 / math.sqrt(2048.0)
RS8 = 1.0 / math.sqrt(8192.0)
EPS_LN = 1e-5
EPS_AT = 1e-6
ST = 132


def _hadamard(n):
    Hm = np.array([[1.0]], dtype=np.float32)
    while Hm.shape[0] < n:
        Hm = np.block([[Hm, Hm], [Hm, -Hm]])
    return np.ascontiguousarray(Hm.astype(np.float32))


_H16 = _hadamard(16)


class Ctr:

    def __init__(self, stack, nc, name):
        self.sem = stack.enter_context(nc.semaphore(name))
        self.val = 0

    def reserve(self, n=1):
        self.val += n
        return self.val

    def at(self):
        return self.val


class DQ:

    def __init__(self, stack, nc, name, lanes=8):
        self.sems = [stack.enter_context(nc.semaphore("%s%d" % (name, i)))
                     for i in range(lanes)]
        self.vals = [0] * lanes
        self.i = 0

    def slot(self):
        L = self.i % len(self.sems)
        self.i += 1
        pre = self.vals[L]
        self.vals[L] += 16
        return (self.sems[L], pre, self.vals[L])

    def barrier(self):
        return [(s, v) for s, v in zip(self.sems, self.vals) if v > 0]


def _dma(e, q3, out, in_):
    sem, pre, _ = q3
    if pre:
        e.wait_ge(sem, pre)
    e.dma_start(out=out, in_=in_).then_inc(sem, 16)


def _waitall(e, pairs):
    for s, v in pairs:
        e.wait_ge(s, v)


_STAGES = {"ld": -3, "vec": -2, "g": -1, "init": 0, "foldu": 1, "foldg": 2, "foldd": 3, "fold": 4, "ffn1b0": 5, "phase1": 6, "phase2": 7, "full": 8}


def build_program(taps=(), stage="full"):
    nc = bass.Bass("TRN2", target_bir_lowering=False, debug=False)
    taps = set(taps)
    lvl = _STAGES[stage]

    din = {}
    for name, shape in [
        ("x", [H, NT]),
        ("wqT", [H, 128]), ("wkT", [H, 128]), ("wvT", [H, 128]),
        ("woT", [128, H]),
        ("mz", [128, 129]),
        ("gate1", [1, 1]),
        ("su_ug", [128, HC]),
        ("wsu", [128, IC]), ("wsg", [128, IC]),
        ("wsdp", [128, HC]),
        ("su_dp", [128, IC]),
        ("wu_slice", [H, ISL]), ("wg_slice", [H, ISL]),
        ("wd_slice", [INTER, HSL]),
        ("h128", [128, 128]),
        ("ident", [128, 128]),
        ("trimask", [128, 128]),
        ("pmask", [128, NCORES]),
    ]:
        din[name] = nc.dram_tensor(name, shape, F32, kind="ExternalInput").ap()

    out = nc.dram_tensor("out", [H, NT], F32, kind="ExternalOutput").ap()
    tap_aps = {}
    if "x1" in taps:
        tap_aps["x1"] = nc.dram_tensor("tap_x1", [H, NT], F32, kind="ExternalOutput").ap()
    if "ds" in taps:
        tap_aps["ds"] = nc.dram_tensor("tap_ds", [128, NCH * ST], F32, kind="ExternalOutput").ap()
    if "pq" in taps:
        tap_aps["pq"] = nc.dram_tensor("tap_pq", [128, NT], BF16, kind="ExternalOutput").ap()
    if "sfull" in taps:
        tap_aps["sfull"] = nc.dram_tensor("tap_sfull", [128, NCH * ST], BF16, kind="ExternalOutput").ap()
    if "agu" in taps:
        tap_aps["agu"] = nc.dram_tensor("tap_agu", [128, HC * 128], BF16, kind="ExternalOutput").ap()
        tap_aps["agd"] = nc.dram_tensor("tap_agd", [128, IC * 128], BF16, kind="ExternalOutput").ap()

    x1s = nc.dram_tensor("x1s", [H, NT], F32, kind="Internal").ap()
    agi_u = nc.dram_tensor("agi_u", [ISL // 128, 128, HC, 128], BF16, kind="Internal")
    ago_u = nc.dram_tensor("ago_u", [IC, 128, HC, 128], BF16, kind="Internal", addr_space="Shared")
    agi_g = nc.dram_tensor("agi_g", [ISL // 128, 128, HC, 128], BF16, kind="Internal")
    ago_g = nc.dram_tensor("ago_g", [IC, 128, HC, 128], BF16, kind="Internal", addr_space="Shared")
    agi_d = nc.dram_tensor("agi_d", [HSL // 128, 128, IC, 128], BF16, kind="Internal")
    ago_d = nc.dram_tensor("ago_d", [HC, 128, IC, 128], BF16, kind="Internal", addr_space="Shared")
    wrm_i = nc.dram_tensor("wrm_i", [1, 16], F32, kind="Internal")
    wrm_o = nc.dram_tensor("wrm_o", [NCORES, 16], F32, kind="Internal", addr_space="Shared")
    cc_si = nc.dram_tensor("cc_si", [128, 129], F32, kind="Internal")
    cc_so = nc.dram_tensor("cc_so", [NCORES * 128, 129], F32, kind="Internal", addr_space="Shared")

    st = ExitStack()
    sb = lambda n, s, d: st.enter_context(nc.sbuf_tensor(n, s, d))
    psal = lambda n, s, d=F32: st.enter_context(nc.psum_tensor(n, s, d))

    prog = []

    def E(key, fn):
        prog.append((key, fn))

    with st:
        T = Ctr(st, nc, "T")
        V = Ctr(st, nc, "V")
        S = Ctr(st, nc, "S")
        G = Ctr(st, nc, "G")
        DSY = DQ(st, nc, "DSY", 8)
        DGP = DQ(st, nc, "DGP", 8)
        CC = Ctr(st, nc, "CC")

        h128b = sb("h128b", [128, 128], BF16)
        identb = sb("identb", [128, 128], BF16)
        trimaskf = sb("trimaskf", [128, 128], F32)
        mzb = sb("mzb", [128, 129], BF16)
        wqb = sb("wqb", [128, HC * 128], BF16)
        wkb = sb("wkb", [128, HC * 128], BF16)
        wvb = sb("wvb", [128, HC * 128], BF16)
        wob = sb("wob", [128, HC * 128], BF16)
        suugb = sb("suugb", [128, HC], F32)
        scub = sb("scub", [128, IC], F32)
        scgb = sb("scgb", [128, IC], F32)
        sdpb = sb("sdpb", [128, HC], F32)
        sudpb = sb("sudpb", [128, IC], F32)
        wrmb = sb("wrmb", [1, 16], F32)
        pmaskb = sb("pmaskb", [128, NCORES], F32)
        gate_sb = sb("gate_sb", [1, 1], F32)
        grow = sb("grow", [1, 1], F32)
        g128 = sb("g128", [128, 1], F32)
        g1m = sb("g1m", [128, 1], F32)
        ones_c = sb("ones_c", [128, 1], F32)
        ones_cb = sb("ones_cb", [128, 1], BF16)
        ones_r = sb("ones_r", [1, 128], F32)
        meanrow = sb("meanrow", [1, NB], F32)
        varrow = sb("varrow", [1, NB], F32)
        rstdrow = sb("rstdrow", [1, NB], F32)

        PS = [psal("ps%d" % i, [128, 512]) for i in range(6)]
        PBT = [psal("pbt%d" % i, [128, 128], BF16) for i in range(2)]

        xb = sb("xb", [128, HC * NB], F32)
        xlb = sb("xlb", [128, HC * NB], BF16)
        xsq = sb("xsq", [128, 2 * NB], F32)
        ups = sb("ups", [128, 2 * NB], BF16)
        gsil = sb("gsil", [128, 2 * NB], BF16)
        wub = sb("wub", [128, 3 * HC * 128], BF16)
        wgb = sb("wgb", [128, 2 * HC * 128], BF16)

        _slots_init = [DGP.slot() for _ in range(15)]

        def init_small(e, sl=_slots_init):
            it = iter(sl)
            _dma(e, next(it), h128b[:, :], din["h128"])
            _dma(e, next(it), identb[:, :], din["ident"])
            _dma(e, next(it), mzb[:, :], din["mz"])
            for wsb_, nm in ((wqb, "wqT"), (wkb, "wkT"), (wvb, "wvT")):
                _dma(e, next(it),
                     wsb_[:, :].rearrange("p (c d) -> p c d", d=128),
                     din[nm].rearrange("(c p) d -> p c d", p=128))
            _dma(e, next(it), wob[:, :], din["woT"])
            for dst, nm in (
                (trimaskf, "trimask"), (suugb, "su_ug"), (scub, "wsu"),
                (scgb, "wsg"), (sdpb, "wsdp"), (sudpb, "su_dp"),
                (pmaskb, "pmask"), (gate_sb, "gate1"),
            ):
                _dma(e, next(it), dst[:, :], din[nm])

        E("gp", init_small)
        ld_small = DGP.barrier()

        if lvl <= -3:
            def fin0(e, w=tuple(DGP.barrier())):
                _waitall(e, w)
            E("gp", fin0)
            return _emit(nc, prog)

        def init_vec(e):
            _waitall(e, ld_small)
            e.memset(ones_c[:, :], 1.0)
            e.memset(ones_cb[:, :], 1.0)
            e.memset(ones_r[:, :], 1.0)
            e.tensor_scalar_mul(scub[:, :], scub[:, :], RS2)
            e.tensor_scalar_mul(scgb[:, :], scgb[:, :], RS2)
            e.tensor_scalar_mul(sdpb[:, :], sdpb[:, :], 0.5 * RS8)
            e.drain()
            e.tensor_mul(scub[:, :], scub[:, :], sudpb[:, :])
            e.drain().then_inc(V.sem, 1)

        E("v", init_vec)
        v_init = V.reserve(1)

        if lvl <= -2:
            def fin1(e, w=v_init):
                e.wait_ge(V.sem, w)
            E("v", fin1)
            def fin0(e, w=tuple(DGP.barrier())):
                _waitall(e, w)
            E("gp", fin0)
            return _emit(nc, prog)

        def init_g_s(e, w=v_init):
            e.wait_ge(V.sem, w)
            e.activation(out=grow[:, :], in_=gate_sb[:, :], func=AF.Exp,
                         scale=-1.0).then_inc(S.sem, 1)

        E("s", init_g_s)
        s_g = S.reserve(1)

        def init_g_v(e, w=s_g):
            e.wait_ge(S.sem, w)
            e.tensor_scalar_add(grow[:, :], grow[:, :], 1.0)
            e.drain()
            e.reciprocal(grow[:, :], grow[:, :]).then_inc(V.sem, 1)

        E("v", init_g_v)
        v_g = V.reserve(1)

        def init_g_t(e, w=v_g):
            e.wait_ge(V.sem, w)
            e.matmul(PS[4][:, 0:1], ones_r[:, :], grow[:, :],
                     start=True, stop=True).then_inc(T.sem, 1)

        E("t", init_g_t)
        t_g = T.reserve(1)

        def init_g_v2(e, w=t_g):
            e.wait_ge(T.sem, w)
            e.tensor_copy(g128[:, :], PS[4][:, 0:1])
            e.drain()
            e.tensor_scalar(g1m[:, :], g128[:, :], -1.0, 1.0,
                            ALU.mult, ALU.add).then_inc(V.sem, 1)

        E("v", init_g_v2)
        v_gdone = V.reserve(1)

        if lvl <= -1:
            def fin1(e, w=v_gdone):
                e.wait_ge(V.sem, w)
            E("v", fin1)
            def fin0(e, w=tuple(DGP.barrier())):
                _waitall(e, w)
            E("gp", fin0)
            return _emit(nc, prog)

        trk = {
            "up_mm": [], "gate_mm": [],
            "up_ev": [], "gate_ev": [], "y_done": [],
            "dn_mm_half": [], "dn_mm": [], "dn_ev": [],
            "wu_dma": [], "wg_dma": [],
            "ln_norm": {},
            "ln_base": {},
            "last_xlb_reader_T": 0,
            "spill": [None] * NBLK,
            "xload": [None] * NBLK,
            "x1_load": [None] * NBLK,
            "outstore": [None] * NBLK,
            "ds_ev_last": [None] * NBLK,
            "ffn2_dn": [None] * NBLK,
        }

        def emit_ln(bmean, brstd, bsum, bsq, su_fuse, xlb_ready_key,
                    wait_T_before_write):
            wT = wait_T_before_write

            def vcast(e, wT=wT):
                if wT:
                    e.wait_ge(T.sem, wT)
                e.drain()
                for c in range(HC):
                    e.tensor_copy(xlb[:, c * NB:(c + 1) * NB],
                                  xb[:, c * NB:(c + 1) * NB])
                e.drain().then_inc(V.sem, 1)

            E("v", vcast)
            v_cast = V.reserve(1)

            def tsum(e, w=v_cast):
                e.wait_ge(V.sem, w)
                for c in range(HC):
                    e.matmul(PS[bsum][0:1, 0:NB], ones_cb[:, :],
                             xlb[:, c * NB:(c + 1) * NB],
                             start=(c == 0), stop=(c == HC - 1))
                e.nop().then_inc(T.sem, 1)

            E("t", tsum)
            T.reserve(1)

            sq_v = []
            sqmm_t = []
            for c in range(HC):
                par = c % 2

                def vsq(e, c=c, par=par):
                    if c >= 2:
                        e.wait_ge(T.sem, sqmm_t[c - 2])
                    e.tensor_mul(ups[:, par * NB:(par + 1) * NB],
                                 xlb[:, c * NB:(c + 1) * NB],
                                 xlb[:, c * NB:(c + 1) * NB]).then_inc(V.sem, 1)

                E("v", vsq)
                sq_v.append(V.reserve(1))

                def tsqc(e, c=c, par=par, w=sq_v[-1]):
                    e.wait_ge(V.sem, w)
                    e.matmul(PS[bsq][0:1, 0:NB], ones_cb[:, :],
                             ups[:, par * NB:(par + 1) * NB],
                             start=(c == 0), stop=(c == HC - 1),
                             ).then_inc(T.sem, 1)

                E("t", tsqc)
                sqmm_t.append(T.reserve(1))

            t_sq = sqmm_t[-1]

            def vrows(e, w=t_sq):
                e.wait_ge(T.sem, w)
                e.tensor_scalar_mul(meanrow[0:1, :], PS[bsum][0:1, 0:NB], 1.0 / H)
                e.drain()
                e.tensor_mul(varrow[0:1, :], meanrow[0:1, :], meanrow[0:1, :])
                e.drain()
                e.scalar_tensor_tensor(varrow[0:1, :], PS[bsq][0:1, 0:NB], 1.0 / H,
                                       varrow[0:1, :], ALU.mult, ALU.subtract)
                e.drain()
                e.tensor_scalar_add(varrow[0:1, :], varrow[0:1, :],
                                    EPS_LN).then_inc(V.sem, 1)

            E("v", vrows)
            v_rows = V.reserve(1)

            def srows(e, w=v_rows):
                e.wait_ge(V.sem, w)
                e.activation(out=rstdrow[0:1, :], in_=varrow[0:1, :], func=AF.Ln)
                e.drain()
                e.activation(out=rstdrow[0:1, :], in_=rstdrow[0:1, :], func=AF.Exp,
                             scale=-0.5).then_inc(S.sem, 1)

            E("s", srows)
            s_rows = S.reserve(1)

            def tbcast(e, w=s_rows):
                e.wait_ge(S.sem, w)
                e.matmul(PS[bmean][:, 0:NB], ones_r[:, :], meanrow[0:1, :],
                         start=True, stop=True)
                e.matmul(PS[brstd][:, 0:NB], ones_r[:, :], rstdrow[0:1, :],
                         start=True, stop=True).then_inc(T.sem, 1)

            E("t", tbcast)
            t_bcast = T.reserve(1)

            def vnorm(e, w=t_bcast, su=su_fuse):
                e.wait_ge(T.sem, w)
                for c in range(HC):
                    par = c % 2
                    e.tensor_sub(xsq[:, par * NB:(par + 1) * NB],
                                 xb[:, c * NB:(c + 1) * NB], PS[bmean][:, 0:NB])
                    e.drain()
                    if su:
                        e.scalar_tensor_tensor(
                            xlb[:, c * NB:(c + 1) * NB],
                            xsq[:, par * NB:(par + 1) * NB],
                            suugb[:, c:c + 1], PS[brstd][:, 0:NB],
                            ALU.mult, ALU.mult).then_inc(V.sem, 1)
                    else:
                        e.tensor_mul(xlb[:, c * NB:(c + 1) * NB],
                                     xsq[:, par * NB:(par + 1) * NB],
                                     PS[brstd][:, 0:NB]).then_inc(V.sem, 1)
                e.drain()

            E("v", vnorm)
            v_norm = V.reserve(HC)
            trk["ln_norm"][xlb_ready_key] = v_norm
            trk["ln_base"][xlb_ready_key] = v_norm - HC
            return v_norm

        def emit_ffn(v_xlb_ready, xlb_base=None):
            for i in range(IC):
                gi = len(trk["up_mm"])
                par = i % 2
                _su = DSY.slot()

                def wdma(e, m=i, buf=i % 3, gi=gi, sl=_su):
                    if gi == 0:
                        e.wait_ge(CC.sem, cc_u)
                    if gi >= 3:
                        e.wait_ge(T.sem, trk["up_mm"][gi - 3])
                    _dma(e, sl,
                         wub[:, buf * 2048:(buf + 1) * 2048]
                         .rearrange("p (c i) -> p c i", i=128),
                         ago_u.ap()[m])

                E("sy", wdma)
                trk["wu_dma"].append((_su[0], _su[2]))

                def upmm(e, i=i, par=par, buf=i % 3, gi=gi, v_xlb=v_xlb_ready,
                         xlb_base=xlb_base, wd=trk["wu_dma"][-1]):
                    if i == 0 and xlb_base is None:
                        e.wait_ge(V.sem, v_xlb)
                    e.wait_ge(wd[0], wd[1])
                    if gi >= 2:
                        e.wait_ge(S.sem, trk["up_ev"][gi - 2])
                    for c in range(HC):
                        if i == 0 and xlb_base is not None:
                            e.wait_ge(V.sem, xlb_base + c + 1)
                        inst = e.matmul(PS[0 + par][:, 0:NB],
                                        wub[:, buf * 2048 + c * 128:
                                             buf * 2048 + (c + 1) * 128],
                                        xlb[:, c * NB:(c + 1) * NB],
                                        start=(c == 0), stop=(c == HC - 1))
                    inst.then_inc(T.sem, 1)

                E("t", upmm)
                trk["up_mm"].append(T.reserve(1))

                def upev(e, i=i, par=par, gi=gi, w=trk["up_mm"][-1]):
                    e.wait_ge(T.sem, w)
                    if gi == 0:
                        _waitall(e, tuple(g_dmas)[-8:])
                    e.activation(out=yb[:, i * NB:(i + 1) * NB],
                                 in_=PS[0 + par][:, 0:NB], func=AF.Identity,
                                 scale=scub[:, i:i + 1]).then_inc(S.sem, 1)

                E("s", upev)
                trk["up_ev"].append(S.reserve(1))

            for i in range(IC):
                gi = len(trk["gate_mm"])
                par = i % 2
                _sg = DSY.slot()

                def wdma_g(e, m=i, buf=i % 2, gi=gi, sl=_sg):
                    if gi == 0:
                        e.wait_ge(CC.sem, cc_g)
                    if gi >= 2:
                        e.wait_ge(T.sem, trk["gate_mm"][gi - 2])
                    _dma(e, sl,
                         wgb[:, buf * 2048:(buf + 1) * 2048]
                         .rearrange("p (c i) -> p c i", i=128),
                         ago_g.ap()[m])

                E("sy", wdma_g)
                trk["wg_dma"].append((_sg[0], _sg[2]))

                def gmm(e, par=par, buf=i % 2, gi=gi, wd=trk["wg_dma"][-1]):
                    if gi == 0:
                        e.wait_ge(S.sem, s1ev_last_v)
                    e.wait_ge(wd[0], wd[1])
                    if gi >= 2:
                        e.wait_ge(S.sem, trk["gate_ev"][gi - 2])
                    for c in range(HC):
                        inst = e.matmul(PS[2 + par][:, 0:NB],
                                        wgb[:, buf * 2048 + c * 128:
                                             buf * 2048 + (c + 1) * 128],
                                        xlb[:, c * NB:(c + 1) * NB],
                                        start=(c == 0), stop=(c == HC - 1))
                    inst.then_inc(T.sem, 1)

                E("t", gmm)
                trk["gate_mm"].append(T.reserve(1))

                def gev(e, i=i, par=par, gi=gi, w=trk["gate_mm"][-1]):
                    e.wait_ge(T.sem, w)
                    if gi >= 2:
                        e.wait_ge(V.sem, trk["y_done"][gi - 2])
                    e.activation(out=gsil[:, par * NB:(par + 1) * NB],
                                 in_=PS[2 + par][:, 0:NB], func=AF.Silu,
                                 scale=scgb[:, i:i + 1]).then_inc(S.sem, 1)

                E("s", gev)
                trk["gate_ev"].append(S.reserve(1))

                def yprod(e, i=i, par=par, gi=gi, w=trk["gate_ev"][-1],
                          wu=trk["up_ev"][-(IC - i)]):
                    e.wait_ge(S.sem, w)
                    e.tensor_mul(yb[:, i * NB:(i + 1) * NB],
                                 gsil[:, par * NB:(par + 1) * NB],
                                 yb[:, i * NB:(i + 1) * NB]).then_inc(V.sem, 1)

                E("v", yprod)
                trk["y_done"].append(V.reserve(1))

            trk["last_xlb_reader_T"] = trk["gate_mm"][-1]
            y_all = trk["y_done"][-1]

            dn_last = None
            for h in range(HC):
                gh = len(trk["dn_mm"])
                par = h % 2
                halves = []
                for half in range(2):
                    _sd = DSY.slot()

                    def wd_dma(e, h=h, half=half, gh=gh, sl=_sd):
                        if gh == 0 and half == 0:
                            e.wait_ge(CC.sem, cc_d)
                        if gh >= 1:
                            if half == 0:
                                e.wait_ge(T.sem, trk["dn_mm_half"][gh - 1])
                            else:
                                e.wait_ge(T.sem, trk["dn_mm"][gh - 1])
                        _dma(e, sl,
                             wdb[:, half * 32 * 128:(half + 1) * 32 * 128]
                             .rearrange("p (ic hh) -> p ic hh", hh=128),
                             ago_d.ap()[h, :, half * 32:(half + 1) * 32, :])

                    E("sy", wd_dma)
                    halves.append((_sd[0], _sd[2]))

                def dnmm(e, h=h, par=par, gh=gh, halves=tuple(halves),
                         y_all=y_all):
                    if h == 0:
                        e.wait_ge(V.sem, y_all)
                    if gh >= 2:
                        e.wait_ge(V.sem, trk["dn_ev"][gh - 2])
                    e.wait_ge(halves[0][0], halves[0][1])
                    for ic in range(32):
                        inst = e.matmul(PS[4 + par][:, 0:NB],
                                        wdb[:, ic * 128:(ic + 1) * 128],
                                        yb[:, ic * NB:(ic + 1) * NB],
                                        start=(ic == 0), stop=False)
                    inst.then_inc(T.sem, 1)
                    e.wait_ge(halves[1][0], halves[1][1])
                    for ic in range(32, 64):
                        inst = e.matmul(PS[4 + par][:, 0:NB],
                                        wdb[:, ic * 128:(ic + 1) * 128],
                                        yb[:, ic * NB:(ic + 1) * NB],
                                        start=False, stop=(ic == 63))
                    inst.then_inc(T.sem, 1)

                E("t", dnmm)
                trk["dn_mm_half"].append(T.reserve(1))
                trk["dn_mm"].append(T.reserve(1))

                def dnev(e, h=h, par=par, w=trk["dn_mm"][-1]):
                    e.wait_ge(T.sem, w)
                    e.scalar_tensor_tensor(xb[:, h * NB:(h + 1) * NB],
                                           PS[4 + par][:, 0:NB],
                                           sdpb[:, h:h + 1],
                                           xb[:, h * NB:(h + 1) * NB],
                                           ALU.mult, ALU.add).then_inc(V.sem, 1)

                E("v", dnev)
                trk["dn_ev"].append(V.reserve(1))
                dn_last = trk["dn_ev"][-1]
            return dn_last

        fold_stack = ExitStack()
        fdin = fold_stack.enter_context(nc.sbuf_tensor("fdin", [128, IC * HSL], BF16))
        fdsc = fold_stack.enter_context(nc.sbuf_tensor("fdsc", [128, 4 * HSL], BF16))
        fdout = fold_stack.enter_context(nc.sbuf_tensor("fdout", [128, HSL], BF16))
        fwin = fold_stack.enter_context(nc.sbuf_tensor("fwin", [128, HC * ISL], BF16))
        fwsc = fold_stack.enter_context(nc.sbuf_tensor("fwsc", [128, ISL], BF16))

        cc_u = cc_g = cc_d = 0
        u_dmas = g_dmas = ()
        do_ug = lvl >= 1 and lvl != 3

        def warm_v(e):
            e.memset(wrmb[:, :], 1.0)
            e.drain().then_inc(V.sem, 1)

        E("v", warm_v)
        v_warm = V.reserve(1)
        _slw = DGP.slot()

        def ag_warm(e, wv=v_warm, sl=_slw):
            e.wait_ge(V.sem, wv)
            _dma(e, sl, wrm_i.ap(), wrmb[:, :])
            e.wait_ge(sl[0], sl[2])
            e.collective_compute(
                "AllGather", ALU.bypass,
                replica_groups=[list(range(NCORES))],
                ins=[wrm_i.ap().opt()],
                outs=[wrm_o.ap().opt()],
            ).then_inc(CC.sem, 1)

        if WARM:
            E("gp", ag_warm)
            cc_warm = CC.reserve(1)
        else:
            cc_warm = 0

        if do_ug:
            ld_u_done = []
            for uh in range(2):
                _slu = DGP.slot()

                def ldslice_u(e, uh=uh, sl=_slu):
                    _dma(e, sl,
                         fwin[:, uh * 8 * ISL:(uh + 1) * 8 * ISL]
                         .rearrange("p (c i) -> p c i", i=ISL),
                         din["wu_slice"][uh * 1024:(uh + 1) * 1024, :]
                         .rearrange("(c p) i -> p c i", p=128))

                E("gp", ldslice_u)
                ld_u_done.append((_slu[0], _slu[2]))

        _slwd = DGP.slot()

        def ld_wd(e, sl=_slwd):
            _dma(e, sl,
                 fdin[:, :].rearrange("p (c h) -> p c h", h=HSL),
                 din["wd_slice"].rearrange("(c p) h -> p c h", p=128))

        E("gp", ld_wd)
        ld_wd_done = (_slwd[0], _slwd[2])

        _sx0 = DGP.slot()

        def xload0(e, sl=_sx0):
            _dma(e, sl,
                 xb[:, :].rearrange("p (c n) -> p c n", n=NB),
                 din["x"][:, 0:NB].rearrange("(c p) n -> p c n", p=128))

        E("gp", xload0)
        trk["xload"][0] = (_sx0[0], _sx0[2])

        if do_ug:
            ev_vals = []
            t_vals = []
            for g in range(2 * HC):
                c, half = g // 2, g % 2
                par = g % 2

                def sA(e, c=c, half=half, par=par, g=g, w=tuple(ld_u_done)):
                    if g == 0:
                        e.wait_ge(w[0][0], w[0][1])
                        _waitall(e, ld_small[:1])
                    if g == 16:
                        e.wait_ge(w[1][0], w[1][1])
                    if g >= 2:
                        e.wait_ge(V.sem, ev_vals[g - 2])
                    e.matmul(PS[par][:, 0:512], h128b[:, :],
                             fwin[:, c * ISL + half * 512: c * ISL + (half + 1) * 512],
                             start=True, stop=True).then_inc(T.sem, 1)

                E("t", sA)
                t_vals.append(T.reserve(1))

                def sAev(e, c=c, half=half, par=par, tv=t_vals[-1]):
                    e.wait_ge(T.sem, tv)
                    e.tensor_copy(fwin[:, c * ISL + half * 512:
                                       c * ISL + (half + 1) * 512],
                                  PS[par][:, 0:512]).then_inc(V.sem, 1)

                E("v", sAev)
                ev_vals.append(V.reserve(1))

            t_sA_u_last = t_vals[-1]

            def ubfly(e, w=ev_vals[-1]):
                e.drain()
                for bit in range(4):
                    step = 1 << bit
                    pairs = [j for j in range(HC) if not (j & step)]
                    for base in range(0, len(pairs), 2):
                        grp = pairs[base:base + 2]
                        for q, j in enumerate(grp):
                            sc = fwsc if q == 0 else fdsc
                            e.tensor_sub(sc[:, 0:ISL],
                                         fwin[:, j * ISL:(j + 1) * ISL],
                                         fwin[:, (j + step) * ISL:(j + step + 1) * ISL])
                        e.drain()
                        for q, j in enumerate(grp):
                            e.tensor_add(fwin[:, j * ISL:(j + 1) * ISL],
                                         fwin[:, j * ISL:(j + 1) * ISL],
                                         fwin[:, (j + step) * ISL:(j + step + 1) * ISL])
                        e.drain()
                        for q, j in enumerate(grp):
                            sc = fwsc if q == 0 else fdsc
                            e.tensor_copy(fwin[:, (j + step) * ISL:(j + step + 1) * ISL],
                                          sc[:, 0:ISL])
                        e.drain()
                e.drain().then_inc(V.sem, 1)

            E("v", ubfly)
            v_bfly = V.reserve(1)

            dma_vals = []
            for g in range(2 * HC):
                c, ihalf = g // 2, g % 2
                _s2 = DGP.slot()

                def dmf(e, c=c, ihalf=ihalf, g=g, sl=_s2, wb=v_bfly):
                    if g == 0:
                        e.wait_ge(V.sem, wb)
                    _dma(e, sl,
                         agi_u.ap()[ihalf * 4:(ihalf + 1) * 4, :, c, :]
                         .rearrange("li p i -> p li i"),
                         fwin[:, c * ISL + ihalf * 512: c * ISL + (ihalf + 1) * 512]
                         .rearrange("p (li i) -> p li i", i=128))

                E("gp", dmf)
                dma_vals.append((_s2[0], _s2[2]))
            u_dmas = dma_vals

            def ag_u_f(e, w=tuple(u_dmas)):
                if WARM:
                    e.wait_ge(CC.sem, cc_warm)
                _waitall(e, w)
                e.collective_compute(
                    "AllGather", ALU.bypass,
                    replica_groups=[list(range(NCORES))],
                    ins=[agi_u.ap().opt()],
                    outs=[ago_u.ap().opt()],
                ).then_inc(CC.sem, 1)

            E("gp", ag_u_f)
            cc_u = CC.reserve(1)

        s1_ev = []
        for c in range(IC):
            par = c % 2

            def s1mm(e, c=c, par=par, w=ld_wd_done,
                     wv=(s1_ev[c - 2] if c >= 2 else None)):
                if c == 0:
                    e.wait_ge(w[0], w[1])
                    _waitall(e, ld_small[:1])
                if wv is not None:
                    e.wait_ge(S.sem, wv)
                e.matmul(PS[2 + par][:, 0:HSL], h128b[:, :],
                         fdin[:, c * HSL:(c + 1) * HSL],
                         start=True, stop=True).then_inc(T.sem, 1)

            E("t", s1mm)
            t_done = T.reserve(1)

            def s1ev(e, c=c, par=par, tv=t_done):
                e.wait_ge(T.sem, tv)
                e.activation(out=fdin[:, c * HSL:(c + 1) * HSL],
                             in_=PS[2 + par][:, 0:HSL],
                             func=AF.Identity).then_inc(S.sem, 1)

            E("s", s1ev)
            s1_ev.append(S.reserve(1))

        s1ev_last_v = s1_ev[-1]

        def wd_bfly_part(bits):
            def part(e, bits=bits, w=s1ev_last_v):
                if bits[0] == 0:
                    e.wait_ge(S.sem, w)
                for bit in bits:
                    step = 1 << bit
                    pairs = [j for j in range(IC) if not (j & step)]
                    for base in range(0, len(pairs), 4):
                        grp = pairs[base:base + 4]
                        for q, j in enumerate(grp):
                            e.tensor_sub(fdsc[:, q * HSL:(q + 1) * HSL],
                                         fdin[:, j * HSL:(j + 1) * HSL],
                                         fdin[:, (j + step) * HSL:(j + step + 1) * HSL])
                        e.drain()
                        for q, j in enumerate(grp):
                            e.tensor_add(fdin[:, j * HSL:(j + 1) * HSL],
                                         fdin[:, j * HSL:(j + 1) * HSL],
                                         fdin[:, (j + step) * HSL:(j + step + 1) * HSL])
                        e.drain()
                        for q, j in enumerate(grp):
                            e.tensor_copy(fdin[:, (j + step) * HSL:(j + step + 1) * HSL],
                                          fdsc[:, q * HSL:(q + 1) * HSL])
                        e.drain()
                e.drain().then_inc(G.sem, 1)

            return part

        if lvl >= 2 and lvl != 3:
            ldg_vals = []
            for c in range(HC):
                _sg2 = DGP.slot()

                def ldslice_g(e, c=c, sl=_sg2,
                              pd=(tuple(u_dmas[2 * c:2 * c + 2])
                                  if u_dmas else ())):
                    _waitall(e, pd)
                    _dma(e, sl,
                         fwin[:, c * ISL:(c + 1) * ISL]
                         .rearrange("p (o i) -> p o i", o=1),
                         din["wg_slice"][c * 128:(c + 1) * 128, :]
                         .rearrange("(o p) i -> p o i", p=128))

                E("gp", ldslice_g)
                ldg_vals.append((_sg2[0], _sg2[2]))

            ev_vals_g = []
            t_vals_g = []
            for g in range(2 * HC):
                c, half = g // 2, g % 2
                par = g % 2

                def sAg(e, c=c, half=half, par=par, g=g, w=ldg_vals[c]):
                    if half == 0:
                        e.wait_ge(w[0], w[1])
                    if g >= 2:
                        e.wait_ge(V.sem, ev_vals_g[g - 2])
                    e.matmul(PS[par][:, 0:512], h128b[:, :],
                             fwin[:, c * ISL + half * 512: c * ISL + (half + 1) * 512],
                             start=True, stop=True).then_inc(T.sem, 1)

                E("t", sAg)
                t_vals_g.append(T.reserve(1))

                def sAevg(e, c=c, half=half, par=par, tv=t_vals_g[-1]):
                    e.wait_ge(T.sem, tv)
                    e.tensor_copy(fwin[:, c * ISL + half * 512:
                                       c * ISL + (half + 1) * 512],
                                  PS[par][:, 0:512]).then_inc(V.sem, 1)

                E("v", sAevg)
                ev_vals_g.append(V.reserve(1))

            t_sA_g_last = t_vals_g[-1]

            def ubfly_g(e, w=ev_vals_g[-1]):
                e.drain()
                for bit in range(4):
                    step = 1 << bit
                    pairs = [j for j in range(HC) if not (j & step)]
                    for base in range(0, len(pairs), 2):
                        grp = pairs[base:base + 2]
                        for q, j in enumerate(grp):
                            sc = fwsc if q == 0 else fdsc
                            e.tensor_sub(sc[:, 0:ISL],
                                         fwin[:, j * ISL:(j + 1) * ISL],
                                         fwin[:, (j + step) * ISL:(j + step + 1) * ISL])
                        e.drain()
                        for q, j in enumerate(grp):
                            e.tensor_add(fwin[:, j * ISL:(j + 1) * ISL],
                                         fwin[:, j * ISL:(j + 1) * ISL],
                                         fwin[:, (j + step) * ISL:(j + step + 1) * ISL])
                        e.drain()
                        for q, j in enumerate(grp):
                            sc = fwsc if q == 0 else fdsc
                            e.tensor_copy(fwin[:, (j + step) * ISL:(j + step + 1) * ISL],
                                          sc[:, 0:ISL])
                        e.drain()
                e.drain().then_inc(V.sem, 1)

            E("v", ubfly_g)
            v_bfly_g = V.reserve(1)

            g_dmas = []
            for g in range(2 * HC):
                c, ihalf = g // 2, g % 2
                _s2 = DGP.slot()

                def dmfg(e, c=c, ihalf=ihalf, g=g, sl=_s2, wb=v_bfly_g):
                    if g == 0:
                        e.wait_ge(V.sem, wb)
                    _dma(e, sl,
                         agi_g.ap()[ihalf * 4:(ihalf + 1) * 4, :, c, :]
                         .rearrange("li p i -> p li i"),
                         fwin[:, c * ISL + ihalf * 512: c * ISL + (ihalf + 1) * 512]
                         .rearrange("p (li i) -> p li i", i=128))

                E("gp", dmfg)
                g_dmas.append((_s2[0], _s2[2]))

            def ag_g_f(e, w=tuple(g_dmas)):
                _waitall(e, w)
                e.collective_compute(
                    "AllGather", ALU.bypass,
                    replica_groups=[list(range(NCORES))],
                    ins=[agi_g.ap().opt()],
                    outs=[ago_g.ap().opt()],
                ).then_inc(CC.sem, 1)

            E("gp", ag_g_f)
            cc_g = CC.reserve(1)
        else:
            t_sA_g_last = 0

        def xwait_t0(e, w=trk["xload"][0]):
            e.wait_ge(w[0], w[1])

        E("t", xwait_t0)

        def xwait_v0(e, w=trk["xload"][0], g=v_gdone):
            e.wait_ge(w[0], w[1])
            e.wait_ge(V.sem, g)

        E("v", xwait_v0)
        v_ln1_blk0 = emit_ln(4, 5, 4, 5, su_fuse=True,
                             xlb_ready_key=(0, "ln1"),
                             wait_T_before_write=0)

        E("v", wd_bfly_part((0, 1, 2)))
        G.reserve(1)
        E("v", wd_bfly_part((3, 4, 5)))
        g_s2_done = G.reserve(1)

        d_dma_vals = []
        for c in range(IC):
            _s3d = DGP.slot()
            d_dma_vals.append((_s3d[0], _s3d[2]))

            def s3d(e, c=c, wb=g_s2_done, sl=_s3d):
                if c == 0:
                    e.wait_ge(G.sem, wb)
                _dma(e, sl,
                     agi_d.ap()[:, :, c, :].rearrange("m p h -> p m h"),
                     fdin[:, c * HSL:(c + 1) * HSL]
                     .rearrange("p (m h) -> p m h", h=128))

            E("gp", s3d)

        g_s3_last = g_s2_done
        wd_out_dmas = tuple(d_dma_vals[-8:])

        all_fold_dma = tuple(DGP.barrier())

        if lvl == 3:
            def finf(e, w=all_fold_dma):
                _waitall(e, w)
            E("gp", finf)
            fold_stack.close()
            return _emit(nc, prog)

        def ag_d_f(e, w=all_fold_dma):
            _waitall(e, w)
            e.collective_compute(
                "AllGather", ALU.bypass,
                replica_groups=[list(range(NCORES))],
                ins=[agi_d.ap().opt()],
                outs=[ago_d.ap().opt()],
            ).then_inc(CC.sem, 1)

        E("gp", ag_d_f)
        cc_d = CC.reserve(1)

        fold_stack.close()

        pqf = sb("pqf", [128, NT], BF16)
        pkf = sb("pkf", [128, NT], BF16)
        vaugb = sb("vaugb", [128, NCH * ST], BF16)
        dS = sb("dS", [128, NCH * ST], F32)
        sfull = sb("sfull", [128, NCH * ST], BF16)
        rl = sb("rl", [128, 1], F32)
        rm = sb("rm", [128, 1], F32)
        grl = sb("grl", [128, 1], F32)
        grm = sb("grm", [128, 1], F32)
        blendb = sb("blendb", [128, 128], BF16)
        blendT = sb("blendT", [128, NB], BF16)
        wdb = sb("wdb", [128, IC * 128], BF16)
        yb = sb("yb", [128, IC * NB], BF16)

        def attn_safe(e, w=g_s3_last, d=wd_out_dmas, gd=tuple(g_dmas)):
            e.wait_ge(G.sem, w)
            _waitall(e, d)
            _waitall(e, gd)
            if t_sA_g_last:
                e.wait_ge(T.sem, t_sA_g_last)
            for j in range(NCH):
                e.memset(vaugb[:, j * ST + 128: j * ST + 129], 1.0)
            e.drain().then_inc(V.sem, 1)

        E("v", attn_safe)
        V.reserve(1)

        n_p1 = 0 if lvl < 5 else (1 if lvl == 5 else NBLK)
        s_elu_q = {}
        s_elu_k = {}
        for blk in range(n_p1):
            if blk == 0:
                v_ln1 = v_ln1_blk0
            else:
                _sx = DGP.slot()

                def xload(e, blk=blk, sl=_sx):
                    sp = trk["spill"][blk - 1]
                    e.wait_ge(sp[0], sp[1])
                    e.wait_ge(V.sem, trk["ln_norm"][(blk - 1, "ln2")])
                    _dma(e, sl,
                         xb[:, :].rearrange("p (c n) -> p c n", n=NB),
                         din["x"][:, blk * NB:(blk + 1) * NB]
                         .rearrange("(c p) n -> p c n", p=128))

                E("gp", xload)
                trk["xload"][blk] = (_sx[0], _sx[2])

                def xwait_t(e, blk=blk):
                    xl = trk["xload"][blk]
                    e.wait_ge(xl[0], xl[1])
                    e.wait_ge(V.sem, trk["ds_ev_last"][blk - 1])

                E("t", xwait_t)

                def xwait_v(e, blk=blk):
                    xl = trk["xload"][blk]
                    e.wait_ge(xl[0], xl[1])

                E("v", xwait_v)

                v_ln1 = emit_ln(2, 3, 4, 5, su_fuse=True,
                                xlb_ready_key=(blk, "ln1"),
                                wait_T_before_write=trk["last_xlb_reader_T"])

            dn_last = emit_ffn(v_ln1, xlb_base=trk["ln_base"][(blk, "ln1")])

            _ssp = DGP.slot()

            def spill(e, blk=blk, w=dn_last, sl=_ssp):
                e.wait_ge(V.sem, w)
                _dma(e, sl,
                     x1s[:, blk * NB:(blk + 1) * NB]
                     .rearrange("(c p) n -> p c n", p=128),
                     xb[:, :].rearrange("p (c n) -> p c n", n=NB))

            E("gp", spill)
            trk["spill"][blk] = (_ssp[0], _ssp[2])

            def xwait_t2(e, w=dn_last):
                e.wait_ge(V.sem, w)

            E("t", xwait_t2)
            v_ln2 = emit_ln(0, 1, 4, 5, su_fuse=False,
                            xlb_ready_key=(blk, "ln2"),
                            wait_T_before_write=trk["last_xlb_reader_T"])

            def qmm(e, w=v_ln2):
                for c in range(HC):
                    e.wait_ge(V.sem, w - HC + c + 1)
                    inst = e.matmul(PS[2][:, 0:NB], wqb[:, c * 128:(c + 1) * 128],
                             xlb[:, c * NB:(c + 1) * NB],
                             start=(c == 0), stop=(c == HC - 1))
                    if c == HC - 1:
                        inst.then_inc(T.sem, 1)

            E("t", qmm)
            t_q = T.reserve(1)

            def kmm(e):
                for c in range(HC):
                    inst = e.matmul(PS[3][:, 0:NB], wkb[:, c * 128:(c + 1) * 128],
                             xlb[:, c * NB:(c + 1) * NB],
                             start=(c == 0), stop=(c == HC - 1))
                    if c == HC - 1:
                        inst.then_inc(T.sem, 1)

            E("t", kmm)
            t_k = T.reserve(1)

            v_qmin = V.at() + 1
            v_qdone = V.at() + 2
            v_kmin = V.at() + 3
            v_kdone = V.at() + 4
            sq_val = S.at() + 1
            sk_val = S.at() + 2

            def elu_v(e, blk=blk, t_q=t_q, t_k=t_k, sq=sq_val, sk=sk_val):
                e.wait_ge(T.sem, t_q)
                e.tensor_scalar_min(xsq[:, 0:NB], PS[2][:, 0:NB],
                                    0.0).then_inc(V.sem, 1)
                e.wait_ge(S.sem, sq)
                e.tensor_scalar_max(xsq[:, NB:2 * NB], PS[2][:, 0:NB], 0.0)
                e.drain()
                e.tensor_add(pqf[:, blk * NB:(blk + 1) * NB],
                             xsq[:, 0:NB], xsq[:, NB:2 * NB]).then_inc(V.sem, 1)
                e.drain()
                e.wait_ge(T.sem, t_k)
                e.tensor_scalar_min(xsq[:, 0:NB], PS[3][:, 0:NB],
                                    0.0).then_inc(V.sem, 1)
                e.wait_ge(S.sem, sk)
                e.tensor_scalar_max(xsq[:, NB:2 * NB], PS[3][:, 0:NB], 0.0)
                e.drain()
                e.tensor_add(pkf[:, blk * NB:(blk + 1) * NB],
                             xsq[:, 0:NB], xsq[:, NB:2 * NB]).then_inc(V.sem, 1)

            def elu_s(e, vq=v_qmin, vk=v_kmin):
                e.wait_ge(V.sem, vq)
                e.activation(out=xsq[:, 0:NB], in_=xsq[:, 0:NB],
                             func=AF.Exp).then_inc(S.sem, 1)
                e.wait_ge(V.sem, vk)
                e.activation(out=xsq[:, 0:NB], in_=xsq[:, 0:NB],
                             func=AF.Exp).then_inc(S.sem, 1)

            E("v", elu_v)
            V.reserve(4)
            E("s", elu_s)
            s_elu_q[blk] = S.reserve(1)
            s_elu_k[blk] = S.reserve(1)
            assert s_elu_q[blk] == sq_val and V.at() == v_kdone
            v_pk_done = v_kdone

            v_vev = []
            t_vv = []
            for jl in range(CPB):
                j = blk * CPB + jl
                par = jl % 2

                def vmm(e, jl=jl, par=par, vn=v_ln2,
                        wv=(v_vev[jl - 2] if jl >= 2 else None)):
                    if jl == 0:
                        e.wait_ge(V.sem, vn)
                    if wv is not None:
                        e.wait_ge(V.sem, wv)
                    for c in range(HC):
                        inst = e.matmul(PS[0 + par][:, 0:128],
                                 xlb[:, c * NB + jl * CH: c * NB + (jl + 1) * CH],
                                 wvb[:, c * 128:(c + 1) * 128],
                                 start=(c == 0), stop=(c == HC - 1))
                        if c == HC - 1:
                            inst.then_inc(T.sem, 1)

                E("t", vmm)
                t_vv.append(T.reserve(1))

                def vev_f(e, j=j, par=par, w=t_vv[-1]):
                    e.wait_ge(T.sem, w)
                    e.tensor_copy(vaugb[:, j * ST:j * ST + 128],
                                  PS[0 + par][:, 0:128]).then_inc(V.sem, 1)

                E("v", vev_f)
                v_vev.append(V.reserve(1))

            v_pkT = []
            t_pkT = []
            t_ds = []
            ds_ev = []
            for jl in range(CPB):
                j = blk * CPB + jl
                par = jl % 2

                def pktr(e, j=j, jl=jl, par=par, vpk=v_pk_done,
                         wv=(v_pkT[jl - 2] if jl >= 2 else None)):
                    if jl == 0:
                        e.wait_ge(V.sem, vpk)
                    if wv is not None:
                        e.wait_ge(V.sem, wv)
                    e.transpose(PBT[par][:, 0:128],
                                pkf[:, j * CH:(j + 1) * CH],
                                identb[:, :]).then_inc(T.sem, 1)

                E("t", pktr)
                t_pkT.append(T.reserve(1))

                def pktev(e, par=par, w=t_pkT[-1]):
                    e.wait_ge(T.sem, w)
                    e.tensor_copy(blendT[:, par * 128:(par + 1) * 128],
                                  PBT[par][:, 0:128]).then_inc(V.sem, 1)

                E("v", pktev)
                v_pkT.append(V.reserve(1))

                def dsmm(e, j=j, par=par, wv=max(v_pkT[-1], v_vev[jl])):
                    e.wait_ge(V.sem, wv)
                    e.matmul(PS[4 + par][:, 0:129],
                             blendT[:, par * 128:(par + 1) * 128],
                             vaugb[:, j * ST:j * ST + 129],
                             start=True, stop=True).then_inc(T.sem, 1)

                E("t", dsmm)
                t_ds.append(T.reserve(1))

                def dsev(e, j=j, par=par, w=t_ds[-1]):
                    e.wait_ge(T.sem, w)
                    e.tensor_copy(dS[:, j * ST:j * ST + 129],
                                  PS[4 + par][:, 0:129]).then_inc(V.sem, 1)

                E("v", dsev)
                ds_ev.append(V.reserve(1))
            trk["ds_ev_last"][blk] = ds_ev[-1]

            def blkprefix(e, blk=blk):
                e.drain()
                for jl in range(CPB):
                    j = blk * CPB + jl
                    if j == 0:
                        continue
                    e.tensor_add(dS[:, j * ST:j * ST + 129],
                                 dS[:, j * ST:j * ST + 129],
                                 dS[:, (j - 1) * ST:(j - 1) * ST + 129])
                    e.drain()
                e.drain().then_inc(V.sem, 1)

            E("v", blkprefix)
            trk["ds_ev_last"][blk] = V.reserve(1)

        v_phase1_done = V.at()
        last_blk = n_p1 - 1

        if lvl >= 7:
            v_prefix = trk["ds_ev_last"][NBLK - 1]

            _ssi = DGP.slot()
            cc_s = CC.at() + 1

            def cc_state(e, w=v_prefix, sl=_ssi, cc_s=cc_s):
                e.wait_ge(V.sem, w)
                _dma(e, sl, cc_si.ap(),
                     dS[:, (NCH - 1) * ST:(NCH - 1) * ST + 129])
                e.wait_ge(sl[0], sl[2])
                e.collective_compute(
                    "AllGather", ALU.bypass,
                    replica_groups=[list(range(NCORES))],
                    ins=[cc_si.ap().opt()],
                    outs=[cc_so.ap().opt()],
                ).then_inc(CC.sem, 1)
                e.wait_ge(CC.sem, cc_s)

            E("gp", cc_state)
            CC.reserve(1)

            v_acc = []
            for k in range(NCORES):
                par = k % 4

                _sk = DGP.slot()

                def ccg_ld(e, k=k, par=par, sl=_sk,
                           wv=None):
                    if k >= 4:
                        e.wait_ge(V.sem, v_acc[k - 4])
                    _dma(e, sl, xsq[:, par * ST:par * ST + 129],
                         cc_so.ap()[k * 128:(k + 1) * 128, :])

                E("gp", ccg_ld)

                def ccg_acc(e, k=k, par=par, w=(_sk[0], _sk[2])):
                    e.wait_ge(w[0], w[1])
                    if k == 0:
                        e.tensor_scalar_mul(xsq[:, 768:897],
                                            xsq[:, par * ST:par * ST + 129],
                                            pmaskb[:, k:k + 1]).then_inc(V.sem, 1)
                    else:
                        e.drain()
                        e.scalar_tensor_tensor(xsq[:, 768:897],
                                               xsq[:, par * ST:par * ST + 129],
                                               pmaskb[:, k:k + 1],
                                               xsq[:, 768:897],
                                               ALU.mult, ALU.add).then_inc(V.sem, 1)

                E("v", ccg_acc)
                v_acc.append(V.reserve(1))

            def mk_sfull(e, w=v_acc[-1]):
                e.wait_ge(V.sem, w)
                e.tensor_copy(sfull[:, 0:129], xsq[:, 768:897])
                for j in range(1, NCH):
                    e.tensor_add(sfull[:, j * ST:j * ST + 129],
                                 dS[:, (j - 1) * ST:(j - 1) * ST + 129],
                                 xsq[:, 768:897])
                e.drain().then_inc(V.sem, 1)

            E("v", mk_sfull)
            v_sfull = V.reserve(1)

        if lvl >= 8:
            for blk in range(NBLK):
                _sx1 = DGP.slot()

                def x1load(e, blk=blk, w=v_phase1_done, sl=_sx1):
                    if blk == 0:
                        e.wait_ge(V.sem, w)
                        sp = trk["spill"][NBLK - 1]
                        e.wait_ge(sp[0], sp[1])
                    else:
                        os_ = trk["outstore"][blk - 1]
                        e.wait_ge(os_[0], os_[1])
                    _dma(e, sl,
                         xb[:, :].rearrange("p (c n) -> p c n", n=NB),
                         x1s[:, blk * NB:(blk + 1) * NB]
                         .rearrange("(c p) n -> p c n", p=128))

                E("gp", x1load)
                trk["x1_load"][blk] = (_sx1[0], _sx1[2])

                v_at_ev = []
                v_blend_ev = []
                for jl in range(CPB):
                    j = blk * CPB + jl
                    pn = jl % 2
                    bnum = 0 if pn == 0 else 2
                    bmem = 1 if pn == 0 else 3
                    bat = 4 + pn

                    def atmm(e, j=j, jl=jl, bat=bat,
                             w=(trk["ds_ev_last"][NBLK - 1] if blk == 0 else None),
                             wv=(v_at_ev[jl - 2] if jl >= 2 else None),
                             wdn=(trk["ffn2_dn"][blk - 1]
                                  if jl < 2 and blk > 0 else None)):
                        if jl == 0 and w is not None:
                            e.wait_ge(V.sem, w)
                        if wdn is not None:
                            e.wait_ge(V.sem, wdn)
                        if wv is not None:
                            e.wait_ge(V.sem, wv)
                        e.matmul(PS[bat][:, 0:128],
                                 pkf[:, j * CH:(j + 1) * CH],
                                 pqf[:, j * CH:(j + 1) * CH],
                                 start=True, stop=True).then_inc(T.sem, 1)

                    E("t", atmm)
                    t_at = T.reserve(1)

                    def atmask(e, bat=bat, w=t_at):
                        e.wait_ge(T.sem, w)
                        e.tensor_mul(blendb[:, :], PS[bat][:, 0:128],
                                     trimaskf[:, :]).then_inc(V.sem, 1)

                    E("v", atmask)
                    v_at_ev.append(V.reserve(1))

                    def nummm(e, j=j, jl=jl, bnum=bnum, bmem=bmem, w=v_at_ev[-1],
                              wv=(v_blend_ev[jl - 2] if jl >= 2 else None)):
                        e.wait_ge(V.sem, w)
                        if wv is not None:
                            e.wait_ge(V.sem, wv)
                        e.matmul(PS[bnum][:, 0:129],
                                 pqf[:, j * CH:(j + 1) * CH],
                                 sfull[:, j * ST:j * ST + 129],
                                 start=True, stop=False)
                        e.matmul(PS[bnum][:, 0:129], blendb[:, :],
                                 vaugb[:, j * ST:j * ST + 129],
                                 start=False, stop=True)
                        e.matmul(PS[bmem][:, 0:129],
                                 pqf[:, j * CH:(j + 1) * CH],
                                 mzb[:, :], start=True, stop=True).then_inc(T.sem, 1)

                    E("t", nummm)
                    t_num = T.reserve(1)

                    def blendv(e, bnum=bnum, bmem=bmem, w=t_num):
                        e.wait_ge(T.sem, w)
                        e.tensor_scalar_add(rl[:, :], PS[bnum][:, 128:129], EPS_AT)
                        e.tensor_scalar_add(rm[:, :], PS[bmem][:, 128:129], EPS_AT)
                        e.drain()
                        e.reciprocal(rl[:, :], rl[:, :])
                        e.drain()
                        e.reciprocal(rm[:, :], rm[:, :])
                        e.drain()
                        e.tensor_mul(grl[:, :], rl[:, :], g1m[:, :])
                        e.tensor_mul(grm[:, :], rm[:, :], g128[:, :])
                        e.drain()
                        e.tensor_scalar_mul(xsq[:, 640:768], PS[bnum][:, 0:128], grl[:, 0:1])
                        e.drain()
                        e.scalar_tensor_tensor(blendb[:, :], PS[bmem][:, 0:128],
                                               grm[:, 0:1], xsq[:, 640:768],
                                               ALU.mult, ALU.add).then_inc(V.sem, 1)

                    E("v", blendv)
                    v_blend = V.reserve(1)

                    def btr(e, jl=jl, w=v_blend):
                        e.wait_ge(V.sem, w)
                        e.transpose(PBT[jl % 2][:, 0:128], blendb[:, :],
                                    identb[:, :]).then_inc(T.sem, 1)

                    E("t", btr)
                    t_btr = T.reserve(1)

                    def btrev(e, jl=jl, w=t_btr):
                        e.wait_ge(T.sem, w)
                        e.tensor_copy(blendT[:, jl * CH:(jl + 1) * CH],
                                      PBT[jl % 2][:, 0:128]).then_inc(V.sem, 1)

                    E("v", btrev)
                    v_blend_ev.append(V.reserve(1))

                v_op_ev = []
                for h in range(HC):
                    par = h % 2

                    def opmm(e, h=h, par=par, wbl=v_blend_ev[-1],
                             x1ld=trk["x1_load"][blk],
                             wv=(v_op_ev[h - 2] if h >= 2 else None)):
                        if h == 0:
                            e.wait_ge(V.sem, wbl)
                            e.wait_ge(x1ld[0], x1ld[1])
                        if wv is not None:
                            e.wait_ge(V.sem, wv)
                        e.matmul(PS[par][:, 0:NB],
                                 wob[:, h * 128:(h + 1) * 128],
                                 blendT[:, 0:NB],
                                 start=True, stop=True).then_inc(T.sem, 1)

                    E("t", opmm)
                    t_op = T.reserve(1)

                    def opev(e, h=h, par=par, w=t_op):
                        e.wait_ge(T.sem, w)
                        e.tensor_add(xb[:, h * NB:(h + 1) * NB],
                                     PS[par][:, 0:NB],
                                     xb[:, h * NB:(h + 1) * NB]).then_inc(V.sem, 1)

                    E("v", opev)
                    v_op_ev.append(V.reserve(1))

                def xwait_t3(e, w=v_op_ev[-1]):
                    e.wait_ge(V.sem, w)

                E("t", xwait_t3)

                def xwait_v3(e, w=v_op_ev[-1]):
                    e.wait_ge(V.sem, w)

                E("v", xwait_v3)
                v_ln3 = emit_ln(2, 3, 4, 5, su_fuse=True,
                                xlb_ready_key=(blk, "ln3"),
                                wait_T_before_write=trk["last_xlb_reader_T"])
                dn3_last = emit_ffn(v_ln3, xlb_base=trk["ln_base"][(blk, "ln3")])
                trk["ffn2_dn"][blk] = dn3_last

                _so = DGP.slot()

                def outstore(e, blk=blk, w=dn3_last, sl=_so):
                    e.wait_ge(V.sem, w)
                    _dma(e, sl,
                         out[:, blk * NB:(blk + 1) * NB]
                         .rearrange("(c p) n -> p c n", p=128),
                         xb[:, :].rearrange("p (c n) -> p c n", n=NB))

                E("gp", outstore)
                trk["outstore"][blk] = (_so[0], _so[2])

        if "x1" in tap_aps:
            _st1 = DGP.slot()

            def tap_x1(e, w=trk["spill"][last_blk], sl=_st1):
                e.wait_ge(w[0], w[1])
                _dma(e, sl, tap_aps["x1"], x1s)

            E("gp", tap_x1)
        if "ds" in tap_aps:
            _st2 = DGP.slot()

            def tap_ds(e, w=v_prefix, sl=_st2):
                e.wait_ge(V.sem, w)
                _dma(e, sl, tap_aps["ds"], dS[:, :])

            E("gp", tap_ds)
        if "pq" in tap_aps:
            _st3 = DGP.slot()

            def tap_pq(e, w=v_phase1_done, sl=_st3):
                e.wait_ge(V.sem, w)
                _dma(e, sl, tap_aps["pq"], pqf[:, :])

            E("gp", tap_pq)
        if "sfull" in tap_aps:
            _st4 = DGP.slot()

            def tap_sf(e, w=v_sfull, sl=_st4):
                e.wait_ge(V.sem, w)
                _dma(e, sl, tap_aps["sfull"], sfull[:, :])

            E("gp", tap_sf)

        if "agu" in tap_aps:
            _st5 = DGP.slot()
            _st6 = DGP.slot()

            def tap_agu(e, w=cc_w, s5=_st5, s6=_st6):
                e.wait_ge(CC.sem, w)
                _dma(e, s5, tap_aps["agu"],
                     ago_u.ap()[7].rearrange("p c i -> p (c i)"))
                _dma(e, s6, tap_aps["agd"],
                     ago_d.ap()[15].rearrange("p ic hh -> p (ic hh)"))

            E("gp", tap_agu)

        def fin(e, w=tuple(DGP.barrier())):
            _waitall(e, w)

        E("gp", fin)

        return _emit(nc, prog)


def _emit(nc, prog):
    with nc.Block() as block:
        @block.sync
        def _(e):
            for k, fn in prog:
                if k == "sy":
                    fn(e)

        @block.gpsimd
        def _(e):
            for k, fn in prog:
                if k == "gp":
                    fn(e)

        @block.tensor
        def _(e):
            for k, fn in prog:
                if k == "t":
                    fn(e)

        @block.vector
        def _(e):
            for k, fn in prog:
                if k == "v":
                    fn(e)

        @block.scalar
        def _(e):
            for k, fn in prog:
                if k == "s":
                    fn(e)

    return nc


def prepare_inputs(inputs):
    f32 = np.float32
    x = np.asarray(inputs["L0_attn_hidden_states"], f32)
    x_t = np.ascontiguousarray(x.reshape(NT * NCORES, H).T)
    wuT = np.ascontiguousarray(np.asarray(inputs["L0_ffn_W_up"], f32).T)
    wgT = np.ascontiguousarray(np.asarray(inputs["L0_ffn_W_gate"], f32).T)
    wdT = np.ascontiguousarray(np.asarray(inputs["L0_ffn_W_down"], f32).T)
    wqT = np.ascontiguousarray(np.asarray(inputs["L0_attn_w_q"], f32).T)
    wkT = np.ascontiguousarray(np.asarray(inputs["L0_attn_w_k"], f32).T)
    wvT = np.ascontiguousarray(np.asarray(inputs["L0_attn_w_v"], f32).T)
    woT = np.ascontiguousarray(np.asarray(inputs["L0_attn_w_o"], f32).T)
    M = np.asarray(inputs["L0_attn_M"], f32)
    z = np.asarray(inputs["L0_attn_z"], f32)
    mz = np.ascontiguousarray(np.concatenate([M, z[:, None]], axis=1))
    gate1 = np.ascontiguousarray(np.asarray(inputs["L0_attn_gate"], f32).reshape(1, 1))
    su_ug = np.ascontiguousarray(
        np.asarray(inputs["L0_ffn_SU_ug"], f32).reshape(HC, 128).T)
    su_dp = np.ascontiguousarray(
        np.asarray(inputs["L0_ffn_SU_dp"], f32).reshape(IC, 128).T)
    wsug = np.asarray(inputs["L0_ffn_Wscale_ug"], f32)
    wsu = np.ascontiguousarray(wsug[:INTER].reshape(IC, 128).T)
    wsg = np.ascontiguousarray(wsug[INTER:].reshape(IC, 128).T)
    wsdp = np.ascontiguousarray(
        np.asarray(inputs["L0_ffn_Wscale_dp"], f32).reshape(HC, 128).T)
    h128 = _hadamard(128)
    ident = np.ascontiguousarray(np.eye(128, dtype=f32))
    trimask = np.ascontiguousarray(np.triu(np.ones((128, 128), f32)))

    shared = dict(wqT=wqT, wkT=wkT, wvT=wvT, woT=woT, mz=mz, gate1=gate1,
                  su_ug=su_ug, su_dp=su_dp, wsu=wsu, wsg=wsg, wsdp=wsdp,
                  h128=h128, ident=ident, trimask=trimask)
    in_maps = []
    for c in range(NCORES):
        pm = np.zeros((128, NCORES), f32)
        for k in range(NCORES):
            if k // 4 == c // 4 and k < c:
                pm[:, k] = 1.0
        m = dict(shared)
        m["x"] = np.ascontiguousarray(x_t[:, c * NT:(c + 1) * NT])
        m["wu_slice"] = np.ascontiguousarray(wuT[:, c * ISL:(c + 1) * ISL])
        m["wg_slice"] = np.ascontiguousarray(wgT[:, c * ISL:(c + 1) * ISL])
        m["wd_slice"] = np.ascontiguousarray(wdT[:, c * HSL:(c + 1) * HSL])
        m["pmask"] = pm
        in_maps.append(m)
    return in_maps


def run(inputs, taps=()):
    in_maps = prepare_inputs(inputs)
    nc = build_program(taps=taps)
    return run_bass_kernel_spmd(nc, in_maps, core_ids=list(range(NCORES)))


def kernel(**inputs):
    res = run(inputs)
    outs = [r["out"] for r in res.results]
    full = np.concatenate(outs, axis=1)
    return np.ascontiguousarray(full.T).reshape(2, NCORES * NT // 2, H)


# version 14
# speedup vs baseline: 1.0188x; 1.0083x over previous
import math
import numpy as np
from contextlib import ExitStack

import concourse.bass as bass
import concourse.mybir as mybir
from concourse.bass_utils import run_bass_kernel_spmd

F32 = mybir.dt.float32
BF16 = mybir.dt.bfloat16
AF = mybir.ActivationFunctionType
ALU = mybir.AluOpType

NCORES = 8
WARM = True
H = 2048
HC = 16
INTER = 8192
IC = 64
NT = 2048
NB = 512
NBLK = NT // NB
CH = 128
NCH = NT // CH
CPB = NB // CH
ISL = INTER // NCORES
HSL = H // NCORES
RS2 = 1.0 / math.sqrt(2048.0)
RS8 = 1.0 / math.sqrt(8192.0)
EPS_LN = 1e-5
EPS_AT = 1e-6
ST = 132


def _hadamard(n):
    Hm = np.array([[1.0]], dtype=np.float32)
    while Hm.shape[0] < n:
        Hm = np.block([[Hm, Hm], [Hm, -Hm]])
    return np.ascontiguousarray(Hm.astype(np.float32))


_H16 = _hadamard(16)


class Ctr:

    def __init__(self, stack, nc, name):
        self.sem = stack.enter_context(nc.semaphore(name))
        self.val = 0

    def reserve(self, n=1):
        self.val += n
        return self.val

    def at(self):
        return self.val


class DQ:

    def __init__(self, stack, nc, name, lanes=8):
        self.sems = [stack.enter_context(nc.semaphore("%s%d" % (name, i)))
                     for i in range(lanes)]
        self.vals = [0] * lanes
        self.i = 0

    def slot(self):
        L = self.i % len(self.sems)
        self.i += 1
        pre = self.vals[L]
        self.vals[L] += 16
        return (self.sems[L], pre, self.vals[L])

    def barrier(self):
        return [(s, v) for s, v in zip(self.sems, self.vals) if v > 0]


def _dma(e, q3, out, in_):
    sem, pre, _ = q3
    if pre:
        e.wait_ge(sem, pre)
    e.dma_start(out=out, in_=in_).then_inc(sem, 16)


def _waitall(e, pairs):
    for s, v in pairs:
        e.wait_ge(s, v)


_STAGES = {"ld": -3, "vec": -2, "g": -1, "init": 0, "foldu": 1, "foldg": 2, "foldd": 3, "fold": 4, "ffn1b0": 5, "phase1": 6, "phase2": 7, "full": 8}


def build_program(taps=(), stage="full"):
    nc = bass.Bass("TRN2", target_bir_lowering=False, debug=False)
    taps = set(taps)
    lvl = _STAGES[stage]

    din = {}
    for name, shape in [
        ("x", [H, NT]),
        ("wqT", [H, 128]), ("wkT", [H, 128]), ("wvT", [H, 128]),
        ("woT", [128, H]),
        ("mz", [128, 129]),
        ("gate1", [1, 1]),
        ("su_ug", [128, HC]),
        ("wsu", [128, IC]), ("wsg", [128, IC]),
        ("wsdp", [128, HC]),
        ("su_dp", [128, IC]),
        ("wu_slice", [H, ISL]), ("wg_slice", [H, ISL]),
        ("wd_slice", [INTER, HSL]),
        ("h128", [128, 128]),
        ("ident", [128, 128]),
        ("trimask", [128, 128]),
        ("pmask", [128, NCORES]),
    ]:
        din[name] = nc.dram_tensor(name, shape, F32, kind="ExternalInput").ap()

    out = nc.dram_tensor("out", [H, NT], F32, kind="ExternalOutput").ap()
    tap_aps = {}
    if "x1" in taps:
        tap_aps["x1"] = nc.dram_tensor("tap_x1", [H, NT], F32, kind="ExternalOutput").ap()
    if "ds" in taps:
        tap_aps["ds"] = nc.dram_tensor("tap_ds", [128, NCH * ST], F32, kind="ExternalOutput").ap()
    if "pq" in taps:
        tap_aps["pq"] = nc.dram_tensor("tap_pq", [128, NT], BF16, kind="ExternalOutput").ap()
    if "sfull" in taps:
        tap_aps["sfull"] = nc.dram_tensor("tap_sfull", [128, NCH * ST], BF16, kind="ExternalOutput").ap()
    if "agu" in taps:
        tap_aps["agu"] = nc.dram_tensor("tap_agu", [128, HC * 128], BF16, kind="ExternalOutput").ap()
        tap_aps["agd"] = nc.dram_tensor("tap_agd", [128, IC * 128], BF16, kind="ExternalOutput").ap()

    x1s = nc.dram_tensor("x1s", [H, NT], F32, kind="Internal").ap()
    agi_u = nc.dram_tensor("agi_u", [ISL // 128, 128, HC, 128], BF16, kind="Internal")
    ago_u = nc.dram_tensor("ago_u", [IC, 128, HC, 128], BF16, kind="Internal", addr_space="Shared")
    agi_g = nc.dram_tensor("agi_g", [ISL // 128, 128, HC, 128], BF16, kind="Internal")
    ago_g = nc.dram_tensor("ago_g", [IC, 128, HC, 128], BF16, kind="Internal", addr_space="Shared")
    agi_d = nc.dram_tensor("agi_d", [HSL // 128, 128, IC, 128], BF16, kind="Internal")
    ago_d = nc.dram_tensor("ago_d", [HC, 128, IC, 128], BF16, kind="Internal", addr_space="Shared")
    wrm_i = nc.dram_tensor("wrm_i", [1, 16], F32, kind="Internal")
    wrm_o = nc.dram_tensor("wrm_o", [NCORES, 16], F32, kind="Internal", addr_space="Shared")
    cc_si = nc.dram_tensor("cc_si", [128, 129], F32, kind="Internal")
    cc_so = nc.dram_tensor("cc_so", [NCORES * 128, 129], F32, kind="Internal", addr_space="Shared")

    st = ExitStack()
    sb = lambda n, s, d: st.enter_context(nc.sbuf_tensor(n, s, d))
    psal = lambda n, s, d=F32: st.enter_context(nc.psum_tensor(n, s, d))

    prog = []

    def E(key, fn):
        prog.append((key, fn))

    with st:
        T = Ctr(st, nc, "T")
        V = Ctr(st, nc, "V")
        S = Ctr(st, nc, "S")
        G = Ctr(st, nc, "G")
        DSY = DQ(st, nc, "DSY", 8)
        DGP = DQ(st, nc, "DGP", 8)
        CC = Ctr(st, nc, "CC")

        h128b = sb("h128b", [128, 128], BF16)
        identb = sb("identb", [128, 128], BF16)
        trimaskf = sb("trimaskf", [128, 128], F32)
        mzb = sb("mzb", [128, 129], BF16)
        wqb = sb("wqb", [128, HC * 128], BF16)
        wkb = sb("wkb", [128, HC * 128], BF16)
        wvb = sb("wvb", [128, HC * 128], BF16)
        wob = sb("wob", [128, HC * 128], BF16)
        suugb = sb("suugb", [128, HC], F32)
        scub = sb("scub", [128, IC], F32)
        scgb = sb("scgb", [128, IC], F32)
        sdpb = sb("sdpb", [128, HC], F32)
        sudpb = sb("sudpb", [128, IC], F32)
        wrmb = sb("wrmb", [1, 16], F32)
        pmaskb = sb("pmaskb", [128, NCORES], F32)
        gate_sb = sb("gate_sb", [1, 1], F32)
        grow = sb("grow", [1, 1], F32)
        g128 = sb("g128", [128, 1], F32)
        g1m = sb("g1m", [128, 1], F32)
        ones_c = sb("ones_c", [128, 1], F32)
        ones_cb = sb("ones_cb", [128, 1], BF16)
        ones_r = sb("ones_r", [1, 128], F32)
        meanrow = sb("meanrow", [1, NB], F32)
        varrow = sb("varrow", [1, NB], F32)
        rstdrow = sb("rstdrow", [1, NB], F32)

        PS = [psal("ps%d" % i, [128, 512]) for i in range(6)]
        PBT = [psal("pbt%d" % i, [128, 128], BF16) for i in range(2)]

        xb = sb("xb", [128, HC * NB], F32)
        xlb = sb("xlb", [128, HC * NB], BF16)
        xsq = sb("xsq", [128, 2 * NB], F32)
        ups = sb("ups", [128, 2 * NB], BF16)
        gsil = sb("gsil", [128, 2 * NB], BF16)
        wub = sb("wub", [128, 3 * HC * 128], BF16)
        wgb = sb("wgb", [128, 2 * HC * 128], BF16)

        _slots_init = [DGP.slot() for _ in range(15)]

        def init_small(e, sl=_slots_init):
            it = iter(sl)
            _dma(e, next(it), h128b[:, :], din["h128"])
            _dma(e, next(it), identb[:, :], din["ident"])
            _dma(e, next(it), mzb[:, :], din["mz"])
            for wsb_, nm in ((wqb, "wqT"), (wkb, "wkT"), (wvb, "wvT")):
                _dma(e, next(it),
                     wsb_[:, :].rearrange("p (c d) -> p c d", d=128),
                     din[nm].rearrange("(c p) d -> p c d", p=128))
            _dma(e, next(it), wob[:, :], din["woT"])
            for dst, nm in (
                (trimaskf, "trimask"), (suugb, "su_ug"), (scub, "wsu"),
                (scgb, "wsg"), (sdpb, "wsdp"), (sudpb, "su_dp"),
                (pmaskb, "pmask"), (gate_sb, "gate1"),
            ):
                _dma(e, next(it), dst[:, :], din[nm])

        E("gp", init_small)
        ld_small = DGP.barrier()

        if lvl <= -3:
            def fin0(e, w=tuple(DGP.barrier())):
                _waitall(e, w)
            E("gp", fin0)
            return _emit(nc, prog)

        def init_vec(e):
            _waitall(e, ld_small)
            e.memset(ones_c[:, :], 1.0)
            e.memset(ones_cb[:, :], 1.0)
            e.memset(ones_r[:, :], 1.0)
            e.tensor_scalar_mul(scub[:, :], scub[:, :], RS2)
            e.tensor_scalar_mul(scgb[:, :], scgb[:, :], RS2)
            e.tensor_scalar_mul(sdpb[:, :], sdpb[:, :], 0.5 * RS8)
            e.drain()
            e.tensor_mul(scub[:, :], scub[:, :], sudpb[:, :])
            e.drain().then_inc(V.sem, 1)

        E("v", init_vec)
        v_init = V.reserve(1)

        if lvl <= -2:
            def fin1(e, w=v_init):
                e.wait_ge(V.sem, w)
            E("v", fin1)
            def fin0(e, w=tuple(DGP.barrier())):
                _waitall(e, w)
            E("gp", fin0)
            return _emit(nc, prog)

        def init_g_s(e, w=v_init):
            e.wait_ge(V.sem, w)
            e.activation(out=grow[:, :], in_=gate_sb[:, :], func=AF.Exp,
                         scale=-1.0).then_inc(S.sem, 1)

        E("s", init_g_s)
        s_g = S.reserve(1)

        def init_g_v(e, w=s_g):
            e.wait_ge(S.sem, w)
            e.tensor_scalar_add(grow[:, :], grow[:, :], 1.0)
            e.drain()
            e.reciprocal(grow[:, :], grow[:, :]).then_inc(V.sem, 1)

        E("v", init_g_v)
        v_g = V.reserve(1)

        def init_g_t(e, w=v_g):
            e.wait_ge(V.sem, w)
            e.matmul(PS[4][:, 0:1], ones_r[:, :], grow[:, :],
                     start=True, stop=True).then_inc(T.sem, 1)

        E("t", init_g_t)
        t_g = T.reserve(1)

        def init_g_v2(e, w=t_g):
            e.wait_ge(T.sem, w)
            e.tensor_copy(g128[:, :], PS[4][:, 0:1])
            e.drain()
            e.tensor_scalar(g1m[:, :], g128[:, :], -1.0, 1.0,
                            ALU.mult, ALU.add).then_inc(V.sem, 1)

        E("v", init_g_v2)
        v_gdone = V.reserve(1)

        if lvl <= -1:
            def fin1(e, w=v_gdone):
                e.wait_ge(V.sem, w)
            E("v", fin1)
            def fin0(e, w=tuple(DGP.barrier())):
                _waitall(e, w)
            E("gp", fin0)
            return _emit(nc, prog)

        trk = {
            "up_mm": [], "gate_mm": [],
            "up_ev": [], "gate_ev": [], "y_done": [],
            "dn_mm_half": [], "dn_mm": [], "dn_ev": [],
            "wu_dma": [], "wg_dma": [],
            "ln_norm": {},
            "ln_base": {},
            "last_xlb_reader_T": 0,
            "spill": [None] * NBLK,
            "xload": [None] * NBLK,
            "x1_load": [None] * NBLK,
            "outstore": [None] * NBLK,
            "ds_ev_last": [None] * NBLK,
            "ffn2_dn": [None] * NBLK,
        }

        def emit_ln(bmean, brstd, bsum, bsq, su_fuse, xlb_ready_key,
                    wait_T_before_write):
            wT = wait_T_before_write

            def vcast(e, wT=wT):
                if wT:
                    e.wait_ge(T.sem, wT)
                e.drain()
                for c in range(HC):
                    e.tensor_copy(xlb[:, c * NB:(c + 1) * NB],
                                  xb[:, c * NB:(c + 1) * NB])
                e.drain().then_inc(V.sem, 1)

            E("v", vcast)
            v_cast = V.reserve(1)

            def tsum(e, w=v_cast):
                e.wait_ge(V.sem, w)
                for c in range(HC):
                    e.matmul(PS[bsum][0:1, 0:NB], ones_cb[:, :],
                             xlb[:, c * NB:(c + 1) * NB],
                             start=(c == 0), stop=(c == HC - 1))
                e.nop().then_inc(T.sem, 1)

            E("t", tsum)
            T.reserve(1)

            sq_v = []
            sqmm_t = []
            for c in range(HC):
                par = c % 2

                def vsq(e, c=c, par=par):
                    if c >= 2:
                        e.wait_ge(T.sem, sqmm_t[c - 2])
                    e.tensor_mul(ups[:, par * NB:(par + 1) * NB],
                                 xlb[:, c * NB:(c + 1) * NB],
                                 xlb[:, c * NB:(c + 1) * NB]).then_inc(V.sem, 1)

                E("v", vsq)
                sq_v.append(V.reserve(1))

                def tsqc(e, c=c, par=par, w=sq_v[-1]):
                    e.wait_ge(V.sem, w)
                    e.matmul(PS[bsq][0:1, 0:NB], ones_cb[:, :],
                             ups[:, par * NB:(par + 1) * NB],
                             start=(c == 0), stop=(c == HC - 1),
                             ).then_inc(T.sem, 1)

                E("t", tsqc)
                sqmm_t.append(T.reserve(1))

            t_sq = sqmm_t[-1]

            def vrows(e, w=t_sq):
                e.wait_ge(T.sem, w)
                e.tensor_scalar_mul(meanrow[0:1, :], PS[bsum][0:1, 0:NB], 1.0 / H)
                e.drain()
                e.tensor_mul(varrow[0:1, :], meanrow[0:1, :], meanrow[0:1, :])
                e.drain()
                e.scalar_tensor_tensor(varrow[0:1, :], PS[bsq][0:1, 0:NB], 1.0 / H,
                                       varrow[0:1, :], ALU.mult, ALU.subtract)
                e.drain()
                e.tensor_scalar_add(varrow[0:1, :], varrow[0:1, :],
                                    EPS_LN).then_inc(V.sem, 1)

            E("v", vrows)
            v_rows = V.reserve(1)

            def srows(e, w=v_rows):
                e.wait_ge(V.sem, w)
                e.activation(out=rstdrow[0:1, :], in_=varrow[0:1, :], func=AF.Ln)
                e.drain()
                e.activation(out=rstdrow[0:1, :], in_=rstdrow[0:1, :], func=AF.Exp,
                             scale=-0.5).then_inc(S.sem, 1)

            E("s", srows)
            s_rows = S.reserve(1)

            def tbcast(e, w=s_rows):
                e.wait_ge(S.sem, w)
                e.matmul(PS[bmean][:, 0:NB], ones_r[:, :], meanrow[0:1, :],
                         start=True, stop=True)
                e.matmul(PS[brstd][:, 0:NB], ones_r[:, :], rstdrow[0:1, :],
                         start=True, stop=True).then_inc(T.sem, 1)

            E("t", tbcast)
            t_bcast = T.reserve(1)

            def vnorm(e, w=t_bcast, su=su_fuse):
                e.wait_ge(T.sem, w)
                for c in range(HC):
                    par = c % 2
                    e.tensor_sub(xsq[:, par * NB:(par + 1) * NB],
                                 xb[:, c * NB:(c + 1) * NB], PS[bmean][:, 0:NB])
                    e.drain()
                    if su:
                        e.scalar_tensor_tensor(
                            xlb[:, c * NB:(c + 1) * NB],
                            xsq[:, par * NB:(par + 1) * NB],
                            suugb[:, c:c + 1], PS[brstd][:, 0:NB],
                            ALU.mult, ALU.mult).then_inc(V.sem, 1)
                    else:
                        e.tensor_mul(xlb[:, c * NB:(c + 1) * NB],
                                     xsq[:, par * NB:(par + 1) * NB],
                                     PS[brstd][:, 0:NB]).then_inc(V.sem, 1)
                e.drain()

            E("v", vnorm)
            v_norm = V.reserve(HC)
            trk["ln_norm"][xlb_ready_key] = v_norm
            trk["ln_base"][xlb_ready_key] = v_norm - HC
            return v_norm

        def emit_ffn(v_xlb_ready, xlb_base=None):
            for i in range(IC):
                gi = len(trk["up_mm"])
                par = i % 2
                _su = DSY.slot()

                def wdma(e, m=i, buf=i % 3, gi=gi, sl=_su):
                    if gi == 0:
                        e.wait_ge(CC.sem, cc_u)
                    if gi >= 3:
                        e.wait_ge(T.sem, trk["up_mm"][gi - 3])
                    _dma(e, sl,
                         wub[:, buf * 2048:(buf + 1) * 2048]
                         .rearrange("p (c i) -> p c i", i=128),
                         ago_u.ap()[m])

                E("sy", wdma)
                trk["wu_dma"].append((_su[0], _su[2]))

                def upmm(e, i=i, par=par, buf=i % 3, gi=gi, v_xlb=v_xlb_ready,
                         xlb_base=xlb_base, wd=trk["wu_dma"][-1]):
                    if i == 0 and xlb_base is None:
                        e.wait_ge(V.sem, v_xlb)
                    if gi == 0 and v_g_evac_last:
                        e.wait_ge(V.sem, v_g_evac_last)
                    e.wait_ge(wd[0], wd[1])
                    if gi >= 2:
                        e.wait_ge(S.sem, trk["up_ev"][gi - 2])
                    for c in range(HC):
                        if i == 0 and xlb_base is not None:
                            e.wait_ge(V.sem, xlb_base + c + 1)
                        inst = e.matmul(PS[0 + par][:, 0:NB],
                                        wub[:, buf * 2048 + c * 128:
                                             buf * 2048 + (c + 1) * 128],
                                        xlb[:, c * NB:(c + 1) * NB],
                                        start=(c == 0), stop=(c == HC - 1))
                    inst.then_inc(T.sem, 1)

                E("t", upmm)
                trk["up_mm"].append(T.reserve(1))

                def upev(e, i=i, par=par, gi=gi, w=trk["up_mm"][-1]):
                    e.wait_ge(T.sem, w)
                    if gi == 0:
                        _waitall(e, tuple(g_dmas)[-8:])
                    e.activation(out=yb[:, i * NB:(i + 1) * NB],
                                 in_=PS[0 + par][:, 0:NB], func=AF.Identity,
                                 scale=scub[:, i:i + 1]).then_inc(S.sem, 1)

                E("s", upev)
                trk["up_ev"].append(S.reserve(1))

            for i in range(IC):
                gi = len(trk["gate_mm"])
                par = i % 2
                _sg = DSY.slot()

                def wdma_g(e, m=i, buf=i % 2, gi=gi, sl=_sg):
                    if gi == 0:
                        e.wait_ge(CC.sem, cc_g)
                    if gi >= 2:
                        e.wait_ge(T.sem, trk["gate_mm"][gi - 2])
                    _dma(e, sl,
                         wgb[:, buf * 2048:(buf + 1) * 2048]
                         .rearrange("p (c i) -> p c i", i=128),
                         ago_g.ap()[m])

                E("sy", wdma_g)
                trk["wg_dma"].append((_sg[0], _sg[2]))

                def gmm(e, par=par, buf=i % 2, gi=gi, wd=trk["wg_dma"][-1]):
                    if gi == 0:
                        e.wait_ge(S.sem, s1ev_last_v)
                    e.wait_ge(wd[0], wd[1])
                    if gi >= 2:
                        e.wait_ge(S.sem, trk["gate_ev"][gi - 2])
                    for c in range(HC):
                        inst = e.matmul(PS[2 + par][:, 0:NB],
                                        wgb[:, buf * 2048 + c * 128:
                                             buf * 2048 + (c + 1) * 128],
                                        xlb[:, c * NB:(c + 1) * NB],
                                        start=(c == 0), stop=(c == HC - 1))
                    inst.then_inc(T.sem, 1)

                E("t", gmm)
                trk["gate_mm"].append(T.reserve(1))

                def gev(e, i=i, par=par, gi=gi, w=trk["gate_mm"][-1]):
                    e.wait_ge(T.sem, w)
                    if gi >= 2:
                        e.wait_ge(V.sem, trk["y_done"][gi - 2])
                    e.activation(out=gsil[:, par * NB:(par + 1) * NB],
                                 in_=PS[2 + par][:, 0:NB], func=AF.Silu,
                                 scale=scgb[:, i:i + 1]).then_inc(S.sem, 1)

                E("s", gev)
                trk["gate_ev"].append(S.reserve(1))

                def yprod(e, i=i, par=par, gi=gi, w=trk["gate_ev"][-1],
                          wu=trk["up_ev"][-(IC - i)]):
                    e.wait_ge(S.sem, w)
                    e.tensor_mul(yb[:, i * NB:(i + 1) * NB],
                                 gsil[:, par * NB:(par + 1) * NB],
                                 yb[:, i * NB:(i + 1) * NB]).then_inc(V.sem, 1)

                E("v", yprod)
                trk["y_done"].append(V.reserve(1))

            trk["last_xlb_reader_T"] = trk["gate_mm"][-1]
            y_all = trk["y_done"][-1]

            dn_last = None
            for h in range(HC):
                gh = len(trk["dn_mm"])
                par = h % 2
                halves = []
                for half in range(2):
                    _sd = DSY.slot()

                    def wd_dma(e, h=h, half=half, gh=gh, sl=_sd):
                        if gh == 0 and half == 0:
                            e.wait_ge(CC.sem, cc_d)
                        if gh >= 1:
                            if half == 0:
                                e.wait_ge(T.sem, trk["dn_mm_half"][gh - 1])
                            else:
                                e.wait_ge(T.sem, trk["dn_mm"][gh - 1])
                        _dma(e, sl,
                             wdb[:, half * 32 * 128:(half + 1) * 32 * 128]
                             .rearrange("p (ic hh) -> p ic hh", hh=128),
                             ago_d.ap()[h, :, half * 32:(half + 1) * 32, :])

                    E("sy", wd_dma)
                    halves.append((_sd[0], _sd[2]))

                def dnmm(e, h=h, par=par, gh=gh, halves=tuple(halves),
                         y_all=y_all):
                    if h == 0:
                        e.wait_ge(V.sem, y_all)
                    if gh >= 2:
                        e.wait_ge(V.sem, trk["dn_ev"][gh - 2])
                    e.wait_ge(halves[0][0], halves[0][1])
                    for ic in range(32):
                        inst = e.matmul(PS[4 + par][:, 0:NB],
                                        wdb[:, ic * 128:(ic + 1) * 128],
                                        yb[:, ic * NB:(ic + 1) * NB],
                                        start=(ic == 0), stop=False)
                    inst.then_inc(T.sem, 1)
                    e.wait_ge(halves[1][0], halves[1][1])
                    for ic in range(32, 64):
                        inst = e.matmul(PS[4 + par][:, 0:NB],
                                        wdb[:, ic * 128:(ic + 1) * 128],
                                        yb[:, ic * NB:(ic + 1) * NB],
                                        start=False, stop=(ic == 63))
                    inst.then_inc(T.sem, 1)

                E("t", dnmm)
                trk["dn_mm_half"].append(T.reserve(1))
                trk["dn_mm"].append(T.reserve(1))

                def dnev(e, h=h, par=par, w=trk["dn_mm"][-1]):
                    e.wait_ge(T.sem, w)
                    e.scalar_tensor_tensor(xb[:, h * NB:(h + 1) * NB],
                                           PS[4 + par][:, 0:NB],
                                           sdpb[:, h:h + 1],
                                           xb[:, h * NB:(h + 1) * NB],
                                           ALU.mult, ALU.add).then_inc(V.sem, 1)

                E("v", dnev)
                trk["dn_ev"].append(V.reserve(1))
                dn_last = trk["dn_ev"][-1]
            return dn_last

        fold_stack = ExitStack()
        fdin = fold_stack.enter_context(nc.sbuf_tensor("fdin", [128, IC * HSL], BF16))
        fdsc = fold_stack.enter_context(nc.sbuf_tensor("fdsc", [128, 4 * HSL], BF16))
        fdout = fold_stack.enter_context(nc.sbuf_tensor("fdout", [128, HSL], BF16))
        fwin = fold_stack.enter_context(nc.sbuf_tensor("fwin", [128, HC * ISL], BF16))
        fwsc = fold_stack.enter_context(nc.sbuf_tensor("fwsc", [128, ISL], BF16))

        cc_u = cc_g = cc_d = 0
        u_dmas = g_dmas = ()
        do_ug = lvl >= 1 and lvl != 3

        def warm_v(e):
            e.memset(wrmb[:, :], 1.0)
            e.drain().then_inc(V.sem, 1)

        E("v", warm_v)
        v_warm = V.reserve(1)
        _slw = DGP.slot()

        def ag_warm(e, wv=v_warm, sl=_slw):
            e.wait_ge(V.sem, wv)
            _dma(e, sl, wrm_i.ap(), wrmb[:, :])
            e.wait_ge(sl[0], sl[2])
            e.collective_compute(
                "AllGather", ALU.bypass,
                replica_groups=[list(range(NCORES))],
                ins=[wrm_i.ap().opt()],
                outs=[wrm_o.ap().opt()],
            ).then_inc(CC.sem, 1)

        if WARM:
            E("gp", ag_warm)
            cc_warm = CC.reserve(1)
        else:
            cc_warm = 0

        if do_ug:
            ld_u_done = []
            for uh in range(2):
                _slu = DGP.slot()

                def ldslice_u(e, uh=uh, sl=_slu):
                    _dma(e, sl,
                         fwin[:, uh * 8 * ISL:(uh + 1) * 8 * ISL]
                         .rearrange("p (c i) -> p c i", i=ISL),
                         din["wu_slice"][uh * 1024:(uh + 1) * 1024, :]
                         .rearrange("(c p) i -> p c i", p=128))

                E("gp", ldslice_u)
                ld_u_done.append((_slu[0], _slu[2]))

        _slwd = DGP.slot()

        def ld_wd(e, sl=_slwd):
            _dma(e, sl,
                 fdin[:, :].rearrange("p (c h) -> p c h", h=HSL),
                 din["wd_slice"].rearrange("(c p) h -> p c h", p=128))

        E("gp", ld_wd)
        ld_wd_done = (_slwd[0], _slwd[2])

        _sx0 = DGP.slot()

        def xload0(e, sl=_sx0):
            _dma(e, sl,
                 xb[:, :].rearrange("p (c n) -> p c n", n=NB),
                 din["x"][:, 0:NB].rearrange("(c p) n -> p c n", p=128))

        E("gp", xload0)
        trk["xload"][0] = (_sx0[0], _sx0[2])

        if do_ug:
            ev_vals = []
            t_vals = []
            for g in range(2 * HC):
                c, half = g // 2, g % 2
                par = g % 2

                def sA(e, c=c, half=half, par=par, g=g, w=tuple(ld_u_done)):
                    if g == 0:
                        e.wait_ge(w[0][0], w[0][1])
                        _waitall(e, ld_small[:1])
                    if g == 16:
                        e.wait_ge(w[1][0], w[1][1])
                    if g >= 2:
                        e.wait_ge(V.sem, ev_vals[g - 2])
                    e.matmul(PS[par][:, 0:512], h128b[:, :],
                             fwin[:, c * ISL + half * 512: c * ISL + (half + 1) * 512],
                             start=True, stop=True).then_inc(T.sem, 1)

                E("t", sA)
                t_vals.append(T.reserve(1))

                def sAev(e, c=c, half=half, par=par, tv=t_vals[-1]):
                    e.wait_ge(T.sem, tv)
                    e.tensor_copy(fwin[:, c * ISL + half * 512:
                                       c * ISL + (half + 1) * 512],
                                  PS[par][:, 0:512]).then_inc(V.sem, 1)

                E("v", sAev)
                ev_vals.append(V.reserve(1))

            t_sA_u_last = t_vals[-1]

            def ubfly(e, w=ev_vals[-1]):
                e.drain()
                for bit in range(4):
                    step = 1 << bit
                    pairs = [j for j in range(HC) if not (j & step)]
                    for base in range(0, len(pairs), 2):
                        grp = pairs[base:base + 2]
                        for q, j in enumerate(grp):
                            sc = fwsc if q == 0 else fdsc
                            e.tensor_sub(sc[:, 0:ISL],
                                         fwin[:, j * ISL:(j + 1) * ISL],
                                         fwin[:, (j + step) * ISL:(j + step + 1) * ISL])
                        e.drain()
                        for q, j in enumerate(grp):
                            e.tensor_add(fwin[:, j * ISL:(j + 1) * ISL],
                                         fwin[:, j * ISL:(j + 1) * ISL],
                                         fwin[:, (j + step) * ISL:(j + step + 1) * ISL])
                        e.drain()
                        for q, j in enumerate(grp):
                            sc = fwsc if q == 0 else fdsc
                            e.tensor_copy(fwin[:, (j + step) * ISL:(j + step + 1) * ISL],
                                          sc[:, 0:ISL])
                        e.drain()
                e.drain().then_inc(V.sem, 1)

            E("v", ubfly)
            v_bfly = V.reserve(1)

            dma_vals = []
            for g in range(2 * HC):
                c, ihalf = g // 2, g % 2
                _s2 = DGP.slot()

                def dmf(e, c=c, ihalf=ihalf, g=g, sl=_s2, wb=v_bfly):
                    if g == 0:
                        e.wait_ge(V.sem, wb)
                    _dma(e, sl,
                         agi_u.ap()[ihalf * 4:(ihalf + 1) * 4, :, c, :]
                         .rearrange("li p i -> p li i"),
                         fwin[:, c * ISL + ihalf * 512: c * ISL + (ihalf + 1) * 512]
                         .rearrange("p (li i) -> p li i", i=128))

                E("gp", dmf)
                dma_vals.append((_s2[0], _s2[2]))
            u_dmas = dma_vals

            def ag_u_f(e, w=tuple(u_dmas)):
                if WARM:
                    e.wait_ge(CC.sem, cc_warm)
                _waitall(e, w)
                e.collective_compute(
                    "AllGather", ALU.bypass,
                    replica_groups=[list(range(NCORES))],
                    ins=[agi_u.ap().opt()],
                    outs=[ago_u.ap().opt()],
                ).then_inc(CC.sem, 1)

            E("gp", ag_u_f)
            cc_u = CC.reserve(1)

        def xwait_t0(e, w=trk["xload"][0]):
            e.wait_ge(w[0], w[1])

        E("t", xwait_t0)

        def xwait_v0(e, w=trk["xload"][0], g=v_gdone):
            e.wait_ge(w[0], w[1])
            e.wait_ge(V.sem, g)

        E("v", xwait_v0)
        v_ln1_blk0 = emit_ln(4, 5, 4, 5, su_fuse=True,
                             xlb_ready_key=(0, "ln1"),
                             wait_T_before_write=0)

        s1_ev = []
        for c in range(IC):
            par = c % 2

            def s1mm(e, c=c, par=par, w=ld_wd_done,
                     wv=(s1_ev[c - 2] if c >= 2 else None)):
                if c == 0:
                    e.wait_ge(w[0], w[1])
                    _waitall(e, ld_small[:1])
                if wv is not None:
                    e.wait_ge(S.sem, wv)
                e.matmul(PS[2 + par][:, 0:HSL], h128b[:, :],
                         fdin[:, c * HSL:(c + 1) * HSL],
                         start=True, stop=True).then_inc(T.sem, 1)

            E("t", s1mm)
            t_done = T.reserve(1)

            def s1ev(e, c=c, par=par, tv=t_done):
                e.wait_ge(T.sem, tv)
                e.activation(out=fdin[:, c * HSL:(c + 1) * HSL],
                             in_=PS[2 + par][:, 0:HSL],
                             func=AF.Identity).then_inc(S.sem, 1)

            E("s", s1ev)
            s1_ev.append(S.reserve(1))

        s1ev_last_v = s1_ev[-1]

        def wd_bfly_part(bits):
            def part(e, bits=bits, w=s1ev_last_v):
                if bits[0] == 0:
                    e.wait_ge(S.sem, w)
                for bit in bits:
                    step = 1 << bit
                    pairs = [j for j in range(IC) if not (j & step)]
                    for base in range(0, len(pairs), 4):
                        grp = pairs[base:base + 4]
                        for q, j in enumerate(grp):
                            e.tensor_sub(fdsc[:, q * HSL:(q + 1) * HSL],
                                         fdin[:, j * HSL:(j + 1) * HSL],
                                         fdin[:, (j + step) * HSL:(j + step + 1) * HSL])
                        e.drain()
                        for q, j in enumerate(grp):
                            e.tensor_add(fdin[:, j * HSL:(j + 1) * HSL],
                                         fdin[:, j * HSL:(j + 1) * HSL],
                                         fdin[:, (j + step) * HSL:(j + step + 1) * HSL])
                        e.drain()
                        for q, j in enumerate(grp):
                            e.tensor_copy(fdin[:, (j + step) * HSL:(j + step + 1) * HSL],
                                          fdsc[:, q * HSL:(q + 1) * HSL])
                        e.drain()
                e.drain().then_inc(G.sem, 1)

            return part

        if lvl >= 2 and lvl != 3:
            ldg_vals = []
            for c in range(HC):
                _sg2 = DGP.slot()

                def ldslice_g(e, c=c, sl=_sg2,
                              pd=(tuple(u_dmas[2 * c:2 * c + 2])
                                  if u_dmas else ())):
                    _waitall(e, pd)
                    _dma(e, sl,
                         fwin[:, c * ISL:(c + 1) * ISL]
                         .rearrange("p (o i) -> p o i", o=1),
                         din["wg_slice"][c * 128:(c + 1) * 128, :]
                         .rearrange("(o p) i -> p o i", p=128))

                E("gp", ldslice_g)
                ldg_vals.append((_sg2[0], _sg2[2]))

            ev_vals_g = []
            t_vals_g = []
            for g in range(2 * HC):
                c, half = g // 2, g % 2
                par = g % 2

                def sAg(e, c=c, half=half, par=par, g=g, w=ldg_vals[c]):
                    if half == 0:
                        e.wait_ge(w[0], w[1])
                    if g >= 2:
                        e.wait_ge(V.sem, ev_vals_g[g - 2])
                    e.matmul(PS[par][:, 0:512], h128b[:, :],
                             fwin[:, c * ISL + half * 512: c * ISL + (half + 1) * 512],
                             start=True, stop=True).then_inc(T.sem, 1)

                E("t", sAg)
                t_vals_g.append(T.reserve(1))

                def sAevg(e, c=c, half=half, par=par, tv=t_vals_g[-1]):
                    e.wait_ge(T.sem, tv)
                    e.tensor_copy(fwin[:, c * ISL + half * 512:
                                       c * ISL + (half + 1) * 512],
                                  PS[par][:, 0:512]).then_inc(V.sem, 1)

                E("v", sAevg)
                ev_vals_g.append(V.reserve(1))

            t_sA_g_last = t_vals_g[-1]

            def ubfly_g(e, w=ev_vals_g[-1]):
                e.drain()
                for bit in range(4):
                    step = 1 << bit
                    pairs = [j for j in range(HC) if not (j & step)]
                    for base in range(0, len(pairs), 2):
                        grp = pairs[base:base + 2]
                        for q, j in enumerate(grp):
                            sc = fwsc if q == 0 else fdsc
                            e.tensor_sub(sc[:, 0:ISL],
                                         fwin[:, j * ISL:(j + 1) * ISL],
                                         fwin[:, (j + step) * ISL:(j + step + 1) * ISL])
                        e.drain()
                        for q, j in enumerate(grp):
                            e.tensor_add(fwin[:, j * ISL:(j + 1) * ISL],
                                         fwin[:, j * ISL:(j + 1) * ISL],
                                         fwin[:, (j + step) * ISL:(j + step + 1) * ISL])
                        e.drain()
                        for q, j in enumerate(grp):
                            sc = fwsc if q == 0 else fdsc
                            e.tensor_copy(fwin[:, (j + step) * ISL:(j + step + 1) * ISL],
                                          sc[:, 0:ISL])
                        e.drain()
                e.drain().then_inc(V.sem, 1)

            E("v", ubfly_g)
            v_bfly_g = V.reserve(1)

            g_dmas = []
            for g in range(2 * HC):
                c, ihalf = g // 2, g % 2
                _s2 = DGP.slot()

                def dmfg(e, c=c, ihalf=ihalf, g=g, sl=_s2, wb=v_bfly_g):
                    if g == 0:
                        e.wait_ge(V.sem, wb)
                    _dma(e, sl,
                         agi_g.ap()[ihalf * 4:(ihalf + 1) * 4, :, c, :]
                         .rearrange("li p i -> p li i"),
                         fwin[:, c * ISL + ihalf * 512: c * ISL + (ihalf + 1) * 512]
                         .rearrange("p (li i) -> p li i", i=128))

                E("gp", dmfg)
                g_dmas.append((_s2[0], _s2[2]))

            def ag_g_f(e, w=tuple(g_dmas)):
                _waitall(e, w)
                e.collective_compute(
                    "AllGather", ALU.bypass,
                    replica_groups=[list(range(NCORES))],
                    ins=[agi_g.ap().opt()],
                    outs=[ago_g.ap().opt()],
                ).then_inc(CC.sem, 1)

            E("gp", ag_g_f)
            cc_g = CC.reserve(1)
            v_g_evac_last = ev_vals_g[-1]
        else:
            t_sA_g_last = 0
            v_g_evac_last = 0

        E("v", wd_bfly_part((0, 1, 2)))
        G.reserve(1)
        E("v", wd_bfly_part((3, 4, 5)))
        g_s2_done = G.reserve(1)

        d_dma_vals = []
        for c in range(IC):
            _s3d = DGP.slot()
            d_dma_vals.append((_s3d[0], _s3d[2]))

            def s3d(e, c=c, wb=g_s2_done, sl=_s3d):
                if c == 0:
                    e.wait_ge(G.sem, wb)
                _dma(e, sl,
                     agi_d.ap()[:, :, c, :].rearrange("m p h -> p m h"),
                     fdin[:, c * HSL:(c + 1) * HSL]
                     .rearrange("p (m h) -> p m h", h=128))

            E("gp", s3d)

        g_s3_last = g_s2_done
        wd_out_dmas = tuple(d_dma_vals[-8:])

        all_fold_dma = tuple(DGP.barrier())

        if lvl == 3:
            def finf(e, w=all_fold_dma):
                _waitall(e, w)
            E("gp", finf)
            fold_stack.close()
            return _emit(nc, prog)

        def ag_d_f(e, w=all_fold_dma):
            _waitall(e, w)
            e.collective_compute(
                "AllGather", ALU.bypass,
                replica_groups=[list(range(NCORES))],
                ins=[agi_d.ap().opt()],
                outs=[ago_d.ap().opt()],
            ).then_inc(CC.sem, 1)

        E("gp", ag_d_f)
        cc_d = CC.reserve(1)

        fold_stack.close()

        pqf = sb("pqf", [128, NT], BF16)
        pkf = sb("pkf", [128, NT], BF16)
        vaugb = sb("vaugb", [128, NCH * ST], BF16)
        dS = sb("dS", [128, NCH * ST], F32)
        sfull = sb("sfull", [128, NCH * ST], BF16)
        rl = sb("rl", [128, 1], F32)
        rm = sb("rm", [128, 1], F32)
        grl = sb("grl", [128, 1], F32)
        grm = sb("grm", [128, 1], F32)
        blendb = sb("blendb", [128, 128], BF16)
        blendT = sb("blendT", [128, NB], BF16)
        wdb = sb("wdb", [128, IC * 128], BF16)
        yb = sb("yb", [128, IC * NB], BF16)

        def attn_safe(e, w=g_s3_last, d=wd_out_dmas, gd=tuple(g_dmas)):
            e.wait_ge(G.sem, w)
            _waitall(e, d)
            _waitall(e, gd)
            if t_sA_g_last:
                e.wait_ge(T.sem, t_sA_g_last)
            for j in range(NCH):
                e.memset(vaugb[:, j * ST + 128: j * ST + 129], 1.0)
            e.drain().then_inc(V.sem, 1)

        E("v", attn_safe)
        V.reserve(1)

        n_p1 = 0 if lvl < 5 else (1 if lvl == 5 else NBLK)
        s_elu_q = {}
        s_elu_k = {}
        for blk in range(n_p1):
            if blk == 0:
                v_ln1 = v_ln1_blk0
            else:
                _sx = DGP.slot()

                def xload(e, blk=blk, sl=_sx):
                    sp = trk["spill"][blk - 1]
                    e.wait_ge(sp[0], sp[1])
                    e.wait_ge(V.sem, trk["ln_norm"][(blk - 1, "ln2")])
                    _dma(e, sl,
                         xb[:, :].rearrange("p (c n) -> p c n", n=NB),
                         din["x"][:, blk * NB:(blk + 1) * NB]
                         .rearrange("(c p) n -> p c n", p=128))

                E("gp", xload)
                trk["xload"][blk] = (_sx[0], _sx[2])

                def xwait_t(e, blk=blk):
                    xl = trk["xload"][blk]
                    e.wait_ge(xl[0], xl[1])
                    e.wait_ge(V.sem, trk["ds_ev_last"][blk - 1])

                E("t", xwait_t)

                def xwait_v(e, blk=blk):
                    xl = trk["xload"][blk]
                    e.wait_ge(xl[0], xl[1])

                E("v", xwait_v)

                v_ln1 = emit_ln(2, 3, 4, 5, su_fuse=True,
                                xlb_ready_key=(blk, "ln1"),
                                wait_T_before_write=trk["last_xlb_reader_T"])

            dn_last = emit_ffn(v_ln1, xlb_base=trk["ln_base"][(blk, "ln1")])

            _ssp = DGP.slot()

            def spill(e, blk=blk, w=dn_last, sl=_ssp):
                e.wait_ge(V.sem, w)
                _dma(e, sl,
                     x1s[:, blk * NB:(blk + 1) * NB]
                     .rearrange("(c p) n -> p c n", p=128),
                     xb[:, :].rearrange("p (c n) -> p c n", n=NB))

            E("gp", spill)
            trk["spill"][blk] = (_ssp[0], _ssp[2])

            def xwait_t2(e, w=dn_last):
                e.wait_ge(V.sem, w)

            E("t", xwait_t2)
            v_ln2 = emit_ln(0, 1, 4, 5, su_fuse=False,
                            xlb_ready_key=(blk, "ln2"),
                            wait_T_before_write=trk["last_xlb_reader_T"])

            def qmm(e, w=v_ln2):
                for c in range(HC):
                    e.wait_ge(V.sem, w - HC + c + 1)
                    inst = e.matmul(PS[2][:, 0:NB], wqb[:, c * 128:(c + 1) * 128],
                             xlb[:, c * NB:(c + 1) * NB],
                             start=(c == 0), stop=(c == HC - 1))
                    if c == HC - 1:
                        inst.then_inc(T.sem, 1)

            E("t", qmm)
            t_q = T.reserve(1)

            def kmm(e):
                for c in range(HC):
                    inst = e.matmul(PS[3][:, 0:NB], wkb[:, c * 128:(c + 1) * 128],
                             xlb[:, c * NB:(c + 1) * NB],
                             start=(c == 0), stop=(c == HC - 1))
                    if c == HC - 1:
                        inst.then_inc(T.sem, 1)

            E("t", kmm)
            t_k = T.reserve(1)

            v_qmin = V.at() + 1
            v_qdone = V.at() + 2
            v_kmin = V.at() + 3
            v_kdone = V.at() + 4
            sq_val = S.at() + 1
            sk_val = S.at() + 2

            def elu_v(e, blk=blk, t_q=t_q, t_k=t_k, sq=sq_val, sk=sk_val):
                e.wait_ge(T.sem, t_q)
                e.tensor_scalar_min(xsq[:, 0:NB], PS[2][:, 0:NB],
                                    0.0).then_inc(V.sem, 1)
                e.wait_ge(S.sem, sq)
                e.tensor_scalar_max(xsq[:, NB:2 * NB], PS[2][:, 0:NB], 0.0)
                e.drain()
                e.tensor_add(pqf[:, blk * NB:(blk + 1) * NB],
                             xsq[:, 0:NB], xsq[:, NB:2 * NB]).then_inc(V.sem, 1)
                e.drain()
                e.wait_ge(T.sem, t_k)
                e.tensor_scalar_min(xsq[:, 0:NB], PS[3][:, 0:NB],
                                    0.0).then_inc(V.sem, 1)
                e.wait_ge(S.sem, sk)
                e.tensor_scalar_max(xsq[:, NB:2 * NB], PS[3][:, 0:NB], 0.0)
                e.drain()
                e.tensor_add(pkf[:, blk * NB:(blk + 1) * NB],
                             xsq[:, 0:NB], xsq[:, NB:2 * NB]).then_inc(V.sem, 1)

            def elu_s(e, vq=v_qmin, vk=v_kmin):
                e.wait_ge(V.sem, vq)
                e.activation(out=xsq[:, 0:NB], in_=xsq[:, 0:NB],
                             func=AF.Exp).then_inc(S.sem, 1)
                e.wait_ge(V.sem, vk)
                e.activation(out=xsq[:, 0:NB], in_=xsq[:, 0:NB],
                             func=AF.Exp).then_inc(S.sem, 1)

            E("v", elu_v)
            V.reserve(4)
            E("s", elu_s)
            s_elu_q[blk] = S.reserve(1)
            s_elu_k[blk] = S.reserve(1)
            assert s_elu_q[blk] == sq_val and V.at() == v_kdone
            v_pk_done = v_kdone

            v_vev = []
            t_vv = []
            for jl in range(CPB):
                j = blk * CPB + jl
                par = jl % 2

                def vmm(e, jl=jl, par=par, vn=v_ln2,
                        wv=(v_vev[jl - 2] if jl >= 2 else None)):
                    if jl == 0:
                        e.wait_ge(V.sem, vn)
                    if wv is not None:
                        e.wait_ge(V.sem, wv)
                    for c in range(HC):
                        inst = e.matmul(PS[0 + par][:, 0:128],
                                 xlb[:, c * NB + jl * CH: c * NB + (jl + 1) * CH],
                                 wvb[:, c * 128:(c + 1) * 128],
                                 start=(c == 0), stop=(c == HC - 1))
                        if c == HC - 1:
                            inst.then_inc(T.sem, 1)

                E("t", vmm)
                t_vv.append(T.reserve(1))

                def vev_f(e, j=j, par=par, w=t_vv[-1]):
                    e.wait_ge(T.sem, w)
                    e.tensor_copy(vaugb[:, j * ST:j * ST + 128],
                                  PS[0 + par][:, 0:128]).then_inc(V.sem, 1)

                E("v", vev_f)
                v_vev.append(V.reserve(1))

            v_pkT = []
            t_pkT = []
            t_ds = []
            ds_ev = []
            for jl in range(CPB):
                j = blk * CPB + jl
                par = jl % 2

                def pktr(e, j=j, jl=jl, par=par, vpk=v_pk_done,
                         wv=(v_pkT[jl - 2] if jl >= 2 else None)):
                    if jl == 0:
                        e.wait_ge(V.sem, vpk)
                    if wv is not None:
                        e.wait_ge(V.sem, wv)
                    e.transpose(PBT[par][:, 0:128],
                                pkf[:, j * CH:(j + 1) * CH],
                                identb[:, :]).then_inc(T.sem, 1)

                E("t", pktr)
                t_pkT.append(T.reserve(1))

                def pktev(e, par=par, w=t_pkT[-1]):
                    e.wait_ge(T.sem, w)
                    e.tensor_copy(blendT[:, par * 128:(par + 1) * 128],
                                  PBT[par][:, 0:128]).then_inc(V.sem, 1)

                E("v", pktev)
                v_pkT.append(V.reserve(1))

                def dsmm(e, j=j, par=par, wv=max(v_pkT[-1], v_vev[jl])):
                    e.wait_ge(V.sem, wv)
                    e.matmul(PS[4 + par][:, 0:129],
                             blendT[:, par * 128:(par + 1) * 128],
                             vaugb[:, j * ST:j * ST + 129],
                             start=True, stop=True).then_inc(T.sem, 1)

                E("t", dsmm)
                t_ds.append(T.reserve(1))

                def dsev(e, j=j, par=par, w=t_ds[-1]):
                    e.wait_ge(T.sem, w)
                    e.tensor_copy(dS[:, j * ST:j * ST + 129],
                                  PS[4 + par][:, 0:129]).then_inc(V.sem, 1)

                E("v", dsev)
                ds_ev.append(V.reserve(1))
            trk["ds_ev_last"][blk] = ds_ev[-1]

            def blkprefix(e, blk=blk):
                e.drain()
                for jl in range(CPB):
                    j = blk * CPB + jl
                    if j == 0:
                        continue
                    e.tensor_add(dS[:, j * ST:j * ST + 129],
                                 dS[:, j * ST:j * ST + 129],
                                 dS[:, (j - 1) * ST:(j - 1) * ST + 129])
                    e.drain()
                e.drain().then_inc(V.sem, 1)

            E("v", blkprefix)
            trk["ds_ev_last"][blk] = V.reserve(1)

        v_phase1_done = V.at()
        last_blk = n_p1 - 1

        if lvl >= 7:
            v_prefix = trk["ds_ev_last"][NBLK - 1]

            _ssi = DGP.slot()
            cc_s = CC.at() + 1

            def cc_state(e, w=v_prefix, sl=_ssi, cc_s=cc_s):
                e.wait_ge(V.sem, w)
                _dma(e, sl, cc_si.ap(),
                     dS[:, (NCH - 1) * ST:(NCH - 1) * ST + 129])
                e.wait_ge(sl[0], sl[2])
                e.collective_compute(
                    "AllGather", ALU.bypass,
                    replica_groups=[list(range(NCORES))],
                    ins=[cc_si.ap().opt()],
                    outs=[cc_so.ap().opt()],
                ).then_inc(CC.sem, 1)
                e.wait_ge(CC.sem, cc_s)

            E("gp", cc_state)
            CC.reserve(1)

            v_acc = []
            for k in range(NCORES):
                par = k % 4

                _sk = DGP.slot()

                def ccg_ld(e, k=k, par=par, sl=_sk,
                           wv=None):
                    if k >= 4:
                        e.wait_ge(V.sem, v_acc[k - 4])
                    _dma(e, sl, xsq[:, par * ST:par * ST + 129],
                         cc_so.ap()[k * 128:(k + 1) * 128, :])

                E("gp", ccg_ld)

                def ccg_acc(e, k=k, par=par, w=(_sk[0], _sk[2])):
                    e.wait_ge(w[0], w[1])
                    if k == 0:
                        e.tensor_scalar_mul(xsq[:, 768:897],
                                            xsq[:, par * ST:par * ST + 129],
                                            pmaskb[:, k:k + 1]).then_inc(V.sem, 1)
                    else:
                        e.drain()
                        e.scalar_tensor_tensor(xsq[:, 768:897],
                                               xsq[:, par * ST:par * ST + 129],
                                               pmaskb[:, k:k + 1],
                                               xsq[:, 768:897],
                                               ALU.mult, ALU.add).then_inc(V.sem, 1)

                E("v", ccg_acc)
                v_acc.append(V.reserve(1))

            def mk_sfull(e, w=v_acc[-1]):
                e.wait_ge(V.sem, w)
                e.tensor_copy(sfull[:, 0:129], xsq[:, 768:897])
                for j in range(1, NCH):
                    e.tensor_add(sfull[:, j * ST:j * ST + 129],
                                 dS[:, (j - 1) * ST:(j - 1) * ST + 129],
                                 xsq[:, 768:897])
                e.drain().then_inc(V.sem, 1)

            E("v", mk_sfull)
            v_sfull = V.reserve(1)

        if lvl >= 8:
            for blk in range(NBLK):
                _sx1 = DGP.slot()

                def x1load(e, blk=blk, w=v_phase1_done, sl=_sx1):
                    if blk == 0:
                        e.wait_ge(V.sem, w)
                        sp = trk["spill"][NBLK - 1]
                        e.wait_ge(sp[0], sp[1])
                    else:
                        os_ = trk["outstore"][blk - 1]
                        e.wait_ge(os_[0], os_[1])
                    _dma(e, sl,
                         xb[:, :].rearrange("p (c n) -> p c n", n=NB),
                         x1s[:, blk * NB:(blk + 1) * NB]
                         .rearrange("(c p) n -> p c n", p=128))

                E("gp", x1load)
                trk["x1_load"][blk] = (_sx1[0], _sx1[2])

                v_at_ev = []
                v_blend_ev = []
                for jl in range(CPB):
                    j = blk * CPB + jl
                    pn = jl % 2
                    bnum = 0 if pn == 0 else 2
                    bmem = 1 if pn == 0 else 3
                    bat = 4 + pn

                    def atmm(e, j=j, jl=jl, bat=bat,
                             w=(trk["ds_ev_last"][NBLK - 1] if blk == 0 else None),
                             wv=(v_at_ev[jl - 2] if jl >= 2 else None),
                             wdn=(trk["ffn2_dn"][blk - 1]
                                  if jl < 2 and blk > 0 else None)):
                        if jl == 0 and w is not None:
                            e.wait_ge(V.sem, w)
                        if wdn is not None:
                            e.wait_ge(V.sem, wdn)
                        if wv is not None:
                            e.wait_ge(V.sem, wv)
                        e.matmul(PS[bat][:, 0:128],
                                 pkf[:, j * CH:(j + 1) * CH],
                                 pqf[:, j * CH:(j + 1) * CH],
                                 start=True, stop=True).then_inc(T.sem, 1)

                    E("t", atmm)
                    t_at = T.reserve(1)

                    def atmask(e, bat=bat, w=t_at):
                        e.wait_ge(T.sem, w)
                        e.tensor_mul(blendb[:, :], PS[bat][:, 0:128],
                                     trimaskf[:, :]).then_inc(V.sem, 1)

                    E("v", atmask)
                    v_at_ev.append(V.reserve(1))

                    def nummm(e, j=j, jl=jl, bnum=bnum, bmem=bmem, w=v_at_ev[-1],
                              wv=(v_blend_ev[jl - 2] if jl >= 2 else None)):
                        e.wait_ge(V.sem, w)
                        if wv is not None:
                            e.wait_ge(V.sem, wv)
                        e.matmul(PS[bnum][:, 0:129],
                                 pqf[:, j * CH:(j + 1) * CH],
                                 sfull[:, j * ST:j * ST + 129],
                                 start=True, stop=False)
                        e.matmul(PS[bnum][:, 0:129], blendb[:, :],
                                 vaugb[:, j * ST:j * ST + 129],
                                 start=False, stop=True)
                        e.matmul(PS[bmem][:, 0:129],
                                 pqf[:, j * CH:(j + 1) * CH],
                                 mzb[:, :], start=True, stop=True).then_inc(T.sem, 1)

                    E("t", nummm)
                    t_num = T.reserve(1)

                    def blendv(e, bnum=bnum, bmem=bmem, w=t_num):
                        e.wait_ge(T.sem, w)
                        e.tensor_scalar_add(rl[:, :], PS[bnum][:, 128:129], EPS_AT)
                        e.tensor_scalar_add(rm[:, :], PS[bmem][:, 128:129], EPS_AT)
                        e.drain()
                        e.reciprocal(rl[:, :], rl[:, :])
                        e.drain()
                        e.reciprocal(rm[:, :], rm[:, :])
                        e.drain()
                        e.tensor_mul(grl[:, :], rl[:, :], g1m[:, :])
                        e.tensor_mul(grm[:, :], rm[:, :], g128[:, :])
                        e.drain()
                        e.tensor_scalar_mul(xsq[:, 640:768], PS[bnum][:, 0:128], grl[:, 0:1])
                        e.drain()
                        e.scalar_tensor_tensor(blendb[:, :], PS[bmem][:, 0:128],
                                               grm[:, 0:1], xsq[:, 640:768],
                                               ALU.mult, ALU.add).then_inc(V.sem, 1)

                    E("v", blendv)
                    v_blend = V.reserve(1)

                    def btr(e, jl=jl, w=v_blend):
                        e.wait_ge(V.sem, w)
                        e.transpose(PBT[jl % 2][:, 0:128], blendb[:, :],
                                    identb[:, :]).then_inc(T.sem, 1)

                    E("t", btr)
                    t_btr = T.reserve(1)

                    def btrev(e, jl=jl, w=t_btr):
                        e.wait_ge(T.sem, w)
                        e.tensor_copy(blendT[:, jl * CH:(jl + 1) * CH],
                                      PBT[jl % 2][:, 0:128]).then_inc(V.sem, 1)

                    E("v", btrev)
                    v_blend_ev.append(V.reserve(1))

                v_op_ev = []
                for h in range(HC):
                    par = h % 2

                    def opmm(e, h=h, par=par, wbl=v_blend_ev[-1],
                             x1ld=trk["x1_load"][blk],
                             wv=(v_op_ev[h - 2] if h >= 2 else None)):
                        if h == 0:
                            e.wait_ge(V.sem, wbl)
                            e.wait_ge(x1ld[0], x1ld[1])
                        if wv is not None:
                            e.wait_ge(V.sem, wv)
                        e.matmul(PS[par][:, 0:NB],
                                 wob[:, h * 128:(h + 1) * 128],
                                 blendT[:, 0:NB],
                                 start=True, stop=True).then_inc(T.sem, 1)

                    E("t", opmm)
                    t_op = T.reserve(1)

                    def opev(e, h=h, par=par, w=t_op):
                        e.wait_ge(T.sem, w)
                        e.tensor_add(xb[:, h * NB:(h + 1) * NB],
                                     PS[par][:, 0:NB],
                                     xb[:, h * NB:(h + 1) * NB]).then_inc(V.sem, 1)

                    E("v", opev)
                    v_op_ev.append(V.reserve(1))

                def xwait_t3(e, w=v_op_ev[-1]):
                    e.wait_ge(V.sem, w)

                E("t", xwait_t3)

                def xwait_v3(e, w=v_op_ev[-1]):
                    e.wait_ge(V.sem, w)

                E("v", xwait_v3)
                v_ln3 = emit_ln(2, 3, 4, 5, su_fuse=True,
                                xlb_ready_key=(blk, "ln3"),
                                wait_T_before_write=trk["last_xlb_reader_T"])
                dn3_last = emit_ffn(v_ln3, xlb_base=trk["ln_base"][(blk, "ln3")])
                trk["ffn2_dn"][blk] = dn3_last

                _so = DGP.slot()

                def outstore(e, blk=blk, w=dn3_last, sl=_so):
                    e.wait_ge(V.sem, w)
                    _dma(e, sl,
                         out[:, blk * NB:(blk + 1) * NB]
                         .rearrange("(c p) n -> p c n", p=128),
                         xb[:, :].rearrange("p (c n) -> p c n", n=NB))

                E("gp", outstore)
                trk["outstore"][blk] = (_so[0], _so[2])

        if "x1" in tap_aps:
            _st1 = DGP.slot()

            def tap_x1(e, w=trk["spill"][last_blk], sl=_st1):
                e.wait_ge(w[0], w[1])
                _dma(e, sl, tap_aps["x1"], x1s)

            E("gp", tap_x1)
        if "ds" in tap_aps:
            _st2 = DGP.slot()

            def tap_ds(e, w=v_prefix, sl=_st2):
                e.wait_ge(V.sem, w)
                _dma(e, sl, tap_aps["ds"], dS[:, :])

            E("gp", tap_ds)
        if "pq" in tap_aps:
            _st3 = DGP.slot()

            def tap_pq(e, w=v_phase1_done, sl=_st3):
                e.wait_ge(V.sem, w)
                _dma(e, sl, tap_aps["pq"], pqf[:, :])

            E("gp", tap_pq)
        if "sfull" in tap_aps:
            _st4 = DGP.slot()

            def tap_sf(e, w=v_sfull, sl=_st4):
                e.wait_ge(V.sem, w)
                _dma(e, sl, tap_aps["sfull"], sfull[:, :])

            E("gp", tap_sf)

        if "agu" in tap_aps:
            _st5 = DGP.slot()
            _st6 = DGP.slot()

            def tap_agu(e, w=cc_w, s5=_st5, s6=_st6):
                e.wait_ge(CC.sem, w)
                _dma(e, s5, tap_aps["agu"],
                     ago_u.ap()[7].rearrange("p c i -> p (c i)"))
                _dma(e, s6, tap_aps["agd"],
                     ago_d.ap()[15].rearrange("p ic hh -> p (ic hh)"))

            E("gp", tap_agu)

        def fin(e, w=tuple(DGP.barrier())):
            _waitall(e, w)

        E("gp", fin)

        return _emit(nc, prog)


def _emit(nc, prog):
    with nc.Block() as block:
        @block.sync
        def _(e):
            for k, fn in prog:
                if k == "sy":
                    fn(e)

        @block.gpsimd
        def _(e):
            for k, fn in prog:
                if k == "gp":
                    fn(e)

        @block.tensor
        def _(e):
            for k, fn in prog:
                if k == "t":
                    fn(e)

        @block.vector
        def _(e):
            for k, fn in prog:
                if k == "v":
                    fn(e)

        @block.scalar
        def _(e):
            for k, fn in prog:
                if k == "s":
                    fn(e)

    return nc


def prepare_inputs(inputs):
    f32 = np.float32
    x = np.asarray(inputs["L0_attn_hidden_states"], f32)
    x_t = np.ascontiguousarray(x.reshape(NT * NCORES, H).T)
    wuT = np.ascontiguousarray(np.asarray(inputs["L0_ffn_W_up"], f32).T)
    wgT = np.ascontiguousarray(np.asarray(inputs["L0_ffn_W_gate"], f32).T)
    wdT = np.ascontiguousarray(np.asarray(inputs["L0_ffn_W_down"], f32).T)
    wqT = np.ascontiguousarray(np.asarray(inputs["L0_attn_w_q"], f32).T)
    wkT = np.ascontiguousarray(np.asarray(inputs["L0_attn_w_k"], f32).T)
    wvT = np.ascontiguousarray(np.asarray(inputs["L0_attn_w_v"], f32).T)
    woT = np.ascontiguousarray(np.asarray(inputs["L0_attn_w_o"], f32).T)
    M = np.asarray(inputs["L0_attn_M"], f32)
    z = np.asarray(inputs["L0_attn_z"], f32)
    mz = np.ascontiguousarray(np.concatenate([M, z[:, None]], axis=1))
    gate1 = np.ascontiguousarray(np.asarray(inputs["L0_attn_gate"], f32).reshape(1, 1))
    su_ug = np.ascontiguousarray(
        np.asarray(inputs["L0_ffn_SU_ug"], f32).reshape(HC, 128).T)
    su_dp = np.ascontiguousarray(
        np.asarray(inputs["L0_ffn_SU_dp"], f32).reshape(IC, 128).T)
    wsug = np.asarray(inputs["L0_ffn_Wscale_ug"], f32)
    wsu = np.ascontiguousarray(wsug[:INTER].reshape(IC, 128).T)
    wsg = np.ascontiguousarray(wsug[INTER:].reshape(IC, 128).T)
    wsdp = np.ascontiguousarray(
        np.asarray(inputs["L0_ffn_Wscale_dp"], f32).reshape(HC, 128).T)
    h128 = _hadamard(128)
    ident = np.ascontiguousarray(np.eye(128, dtype=f32))
    trimask = np.ascontiguousarray(np.triu(np.ones((128, 128), f32)))

    shared = dict(wqT=wqT, wkT=wkT, wvT=wvT, woT=woT, mz=mz, gate1=gate1,
                  su_ug=su_ug, su_dp=su_dp, wsu=wsu, wsg=wsg, wsdp=wsdp,
                  h128=h128, ident=ident, trimask=trimask)
    in_maps = []
    for c in range(NCORES):
        pm = np.zeros((128, NCORES), f32)
        for k in range(NCORES):
            if k // 4 == c // 4 and k < c:
                pm[:, k] = 1.0
        m = dict(shared)
        m["x"] = np.ascontiguousarray(x_t[:, c * NT:(c + 1) * NT])
        m["wu_slice"] = np.ascontiguousarray(wuT[:, c * ISL:(c + 1) * ISL])
        m["wg_slice"] = np.ascontiguousarray(wgT[:, c * ISL:(c + 1) * ISL])
        m["wd_slice"] = np.ascontiguousarray(wdT[:, c * HSL:(c + 1) * HSL])
        m["pmask"] = pm
        in_maps.append(m)
    return in_maps


def run(inputs, taps=()):
    in_maps = prepare_inputs(inputs)
    nc = build_program(taps=taps)
    return run_bass_kernel_spmd(nc, in_maps, core_ids=list(range(NCORES)))


def kernel(**inputs):
    res = run(inputs)
    outs = [r["out"] for r in res.results]
    full = np.concatenate(outs, axis=1)
    return np.ascontiguousarray(full.T).reshape(2, NCORES * NT // 2, H)
